# version 06
# baseline (speedup 1.0000x reference)
import contextlib

import numpy as np
import ml_dtypes

import concourse.bass as bass
import concourse.mybir as mybir
from concourse.bass_utils import run_bass_kernel_spmd

F32 = mybir.dt.float32
BF16 = mybir.dt.bfloat16
I32 = mybir.dt.int32
AF = mybir.ActivationFunctionType
ALU = mybir.AluOpType
AX = mybir.AxisListType
BF = ml_dtypes.bfloat16

N_CORES = 8
D = 1024
B = 4
T_FULL = 4096
TOK = 2048
H = 16
DH = 64
CD = D // 128
NTB = TOK // 512
C0_L0 = 8.0
C1_L1 = 4.0
NEG_BIG = -30000.0
EPS = 1e-5

ENGS = ("tensor", "scalar", "vector", "gpsimd", "sync")


def ap_win(base2d, nwin, stride, width):
    ap = base2d.ap
    return bass.AP(base2d.tensor, base2d.offset,
                   [list(ap[0]), [stride, nwin], [1, width]])


class Prog:

    def __init__(self):
        self.tasks = {e: [] for e in ENGS}
        self.cnt = {c: 0 for c in ("pe", "act", "dve", "poolc", "spd", "poold", "scd")}
        self.n_added = 0
        self.barrier = {}
        self.scope = None

    def ev(self, ch):
        return (ch, self.cnt[ch])

    def add(self, eng, fn, waits=(), inc=None, n_dma=0):
        waits = [w for w in waits if w is not None] + list(self.barrier.items())
        ev = None
        if inc is not None:
            if n_dma:
                self.cnt[inc] += 16 * n_dma
            else:
                self.cnt[inc] += 1
            ev = (inc, self.cnt[inc])
        snap = dict(self.cnt)
        if inc is not None:
            snap[inc] = self.cnt[inc] - (16 * n_dma if n_dma else 1)
        self.tasks[eng].append((fn, tuple(waits), inc, n_dma, self.n_added, snap,
                                self.scope))
        self.n_added += 1
        return ev

    def realize(self, kb):
        waited = {}
        for ename in ENGS:
            tlist = self.tasks[ename]
            if not tlist:
                continue
            dec = getattr(kb.blk, ename)

            import os
            trunc = int(os.environ.get("KTRUNC", "1000000000"))
            serial = bool(os.environ.get("KSERIAL"))

            def body(eng, tlist=tlist, ename=ename, trunc=trunc, serial=serial):
                cur_scope = None
                cur_sid = None
                for fn, waits, inc, n_dma, idx, snap, scope in tlist:
                    if idx >= trunc:
                        continue
                    if scope != cur_scope:
                        if cur_scope is not None:
                            kb.nc.leave_named_scope(cur_scope, cur_sid, False)
                        if scope is not None:
                            cur_sid, _ = kb.nc.enter_named_scope(scope, False)
                        cur_scope = scope
                    if serial:
                        waits = tuple((ch, v) for ch, v in snap.items() if v > 0)
                    for (ch, val) in waits:
                        key = (ename, ch)
                        if waited.get(key, -1) < val:
                            eng.wait_ge(kb.sem[ch], val)
                            waited[key] = val
                    out = fn(eng)
                    if inc is not None:
                        if n_dma:
                            assert isinstance(out, (list, tuple)) and len(out) == n_dma
                            for ins in out:
                                ins.then_inc(kb.sem[inc], 16)
                        else:
                            assert out is not None
                            out.then_inc(kb.sem[inc], 1)
                if cur_scope is not None:
                    kb.nc.leave_named_scope(cur_scope, cur_sid, False)

            dec(body)


class KB:
    def __init__(self, nc, fused=False, arena_el=105472):
        self.nc = nc
        self.stack = contextlib.ExitStack()
        self.sem = {}
        for ch in ("pe", "act", "dve", "poolc", "spd", "poold", "scd",
                   "dve_i", "act_i", "pool_i"):
            self.sem[ch] = self.stack.enter_context(nc.semaphore(ch))
        self.icnt = {"dve_i": 0, "act_i": 0, "pool_i": 0}
        self.cache = {}
        self.blk = None
        self.fused = fused
        if fused:
            self.arena = self.stack.enter_context(
                nc.sbuf_tensor("ARENA", [128, arena_el], BF16))
            self.abump = 0
            self.parena = self.stack.enter_context(
                nc.psum_tensor("PSARENA", [128, 4096], F32))
            self.pbump = 0

    def block_boundary(self, prog):
        self.abump = 0
        self.pbump = 0
        prog.barrier = dict(prog.cnt)

    def edge(self, ch, eng, ins):
        ins.then_inc(self.sem[ch], 1)
        self.icnt[ch] += 1
        eng.wait_ge(self.sem[ch], self.icnt[ch])

    def sb(self, name, shape, dtype):
        if not self.fused or name.startswith("c_"):
            return self.stack.enter_context(self.nc.sbuf_tensor(name, shape, dtype))
        import math as _m
        n_el = int(np.prod(shape[1:])) * mybir.dt.size(dtype) // 2
        off = (self.abump + 31) & ~31
        self.abump = off + n_el
        assert self.abump <= self.arena.shape[1], f"arena OOM at {name}: {self.abump}"
        ap = self.arena[:, off:off + n_el]
        if dtype != BF16:
            ap = ap.bitcast(dtype)
        if len(shape) == 2:
            ap = ap[0:shape[0], :] if shape[0] != 128 else ap
        elif len(shape) == 3:
            ap = ap.rearrange("p (a b) -> p a b", a=shape[1])
            if shape[0] != 128:
                ap = ap[0:shape[0]]
        else:
            raise ValueError(shape)
        return _ArenaTensor(ap)

    def ps(self, name, shape, dtype=F32):
        if not self.fused:
            return self.stack.enter_context(self.nc.psum_tensor(name, shape, dtype))
        assert len(shape) == 2
        el = mybir.dt.size(dtype)
        nbanks = (shape[1] * el + 2047) // 2048
        off = self.pbump * 512
        self.pbump += nbanks
        assert self.pbump <= 8, f"psum OOM at {name}"
        ap = self.parena.ap()[:, off:off + nbanks * 512]
        if dtype != F32:
            ap = ap.bitcast(dtype)
        return _ArenaTensor(ap[0:shape[0], 0:shape[1]])

    def realize(self, prog):
        self.blk = self.stack.enter_context(self.nc.Block())
        prog.realize(self)
        self.stack.close()


class _ArenaTensor:

    def __init__(self, ap):
        self._ap = ap
        self.tensor = ap.tensor
        self.shape = ap.shape

    def __getitem__(self, key):
        return self._ap[key]

    def ap(self):
        return self._ap

    def rearrange(self, *a, **kw):
        return self._ap.rearrange(*a, **kw)

    def bitcast(self, dt):
        return self._ap.bitcast(dt)


def host_constants():
    mega = np.full((128, 896), NEG_BIG, dtype=np.float32)
    kk = np.arange(128)[:, None]
    uu = np.arange(896)[None, :]
    mega[kk <= uu - 384] = 0.0
    return dict(EYE_BF=np.eye(128, dtype=BF), EYE_F=np.eye(128, dtype=np.float32),
                MEGAMASK=mega)


def emit_consts(p, kb):
    ones_r = kb.sb("c_ones_r", [1, 128], BF16)
    onesf_r = kb.sb("c_onesf_r", [1, 128], F32)
    ones_c = kb.sb("c_ones_c", [128, 1], BF16)
    zero_c = kb.sb("c_zero_c", [128, 1], F32)
    eps_c = kb.sb("c_eps_c", [1, 1], F32)
    cl0_c = kb.sb("c_cl0", [128, 1], F32)
    cl1_c = kb.sb("c_cl1", [128, 1], F32)

    def f(v):
        v.memset(ones_r[:], 1.0)
        v.memset(onesf_r[:], 1.0)
        v.memset(ones_c[:], 1.0)
        v.memset(zero_c[:], 0.0)
        v.memset(eps_c[:], EPS)
        v.memset(cl0_c[:], -C0_L0)
        return v.memset(cl1_c[:], -C1_L1)

    ev = p.add("vector", f, inc="dve")
    return dict(ones_r=ones_r, onesf_r=onesf_r, ones_c=ones_c, zero_c=zero_c,
                eps_c=eps_c, cl0_c=cl0_c, cl1_c=cl1_c, ev=ev)


def emit_ln_tail(p, kb, cst, st, *, pref, Yr, Yb, Ysq, b_m, b_s, ps_sum, ps_sq,
                 outT, t0, ev_y, Xbf_out=None):
    if pref not in kb.cache:
        kb.cache[pref] = (kb.sb(f"{pref}_mean", [1, 512], F32),
                          kb.sb(f"{pref}_rstd", [1, 512], F32))
    mean_r, rstd_r = kb.cache[pref]

    def f_cast(g):
        return g.tensor_copy(Yb[:], Yr[:])
    ev_cast = p.add("gpsimd", f_cast, waits=[ev_y, st.get("Yb")], inc="poolc")

    def f_sq(sc):
        return sc.activation(Ysq[:], Yr[:], AF.Square)
    ev_sq = p.add("scalar", f_sq, waits=[ev_y, st.get("Ysq")], inc="act")

    def f_stats(t):
        for cd in range(CD):
            t.matmul(ps_sum, cst["ones_c"][:], Yb[:, cd, :],
                     start=(cd == 0), stop=(cd == CD - 1))
        for cd in range(CD):
            last = t.matmul(ps_sq, cst["ones_c"][:], Ysq[:, cd, :],
                            start=(cd == 0), stop=(cd == CD - 1))
        return last
    ev_stats = p.add("tensor", f_stats,
                     waits=[ev_cast, ev_sq, st.get("ps_row")], inc="pe")

    def f_rows(v):
        ins = v.tensor_scalar(mean_r[:], ps_sum, 1.0 / D, None, ALU.mult)
        v.tensor_scalar(rstd_r[:], ps_sq, 1.0 / D, None, ALU.mult)
        kb.edge("dve_i", v, ins)
        ins = v.tensor_tensor(ps_sum, mean_r[:], mean_r[:], ALU.mult)
        kb.edge("dve_i", v, ins)
        return v.tensor_tensor(rstd_r[:], rstd_r[:], ps_sum, ALU.subtract)
    ev_rows = p.add("vector", f_rows, waits=[ev_stats], inc="dve")
    st["ps_row"] = ev_rows

    def f_sqrt(sc):
        return sc.activation(rstd_r[:], rstd_r[:], AF.Sqrt, bias=cst["eps_c"][0:1, 0:1],
                             scale=1.0)
    ev_sqrt = p.add("scalar", f_sqrt, waits=[ev_rows], inc="act")

    def f_recip(v):
        return v.reciprocal(rstd_r[:], rstd_r[:])
    ev_recip = p.add("vector", f_recip, waits=[ev_sqrt], inc="dve")

    def f_bcast(t):
        t.matmul(b_m[:], cst["onesf_r"][:], mean_r[:], start=True, stop=True)
        return t.matmul(b_s[:], cst["onesf_r"][:], rstd_r[:], start=True, stop=True)
    ev_bcast = p.add("tensor", f_bcast,
                     waits=[ev_recip, st.get("b_m"), st.get("b_s")], inc="pe")

    def f_apply(v):
        mb = b_m[:].unsqueeze(1).broadcast_to((128, CD, 512))
        rb = b_s[:].unsqueeze(1).broadcast_to((128, CD, 512))
        ins = v.tensor_tensor(Yr[:], Yr[:], mb, ALU.subtract)
        kb.edge("dve_i", v, ins)
        out = v.tensor_tensor(Yr[:], Yr[:], rb, ALU.mult)
        if Xbf_out is not None:
            kb.edge("dve_i", v, out)
            out = v.tensor_copy(Xbf_out, Yr[:])
        return out
    ev_apply = p.add("vector", f_apply, waits=[ev_bcast], inc="dve")
    st["b_m"] = ev_apply
    st["b_s"] = ev_apply
    st["Yb"] = ev_stats
    st["Ysq"] = ev_stats

    if outT is None:
        return ev_apply, ev_apply

    def f_store(s):
        return [s.dma_start(out=outT[:, t0:t0 + 512].rearrange("(c p) n -> p c n", p=128),
                            in_=Yr[:])]
    ev_store = p.add("sync", f_store, waits=[ev_apply], inc="spd", n_dma=1)
    return ev_apply, ev_store


def emit_l0_attn(p, kb, cst, io):
    xT, x1T = io["xT"], io["x1T"]

    Wq = kb.sb("l0_Wq", [128, CD, 1024], BF16)
    Wd = kb.sb("l0_Wd", [128, CD, 256], BF16)
    Wu = kb.sb("l0_Wu", [128, 2, 2048], BF16)
    Wo = kb.sb("l0_Wo", [128, CD, 1024], BF16)
    bq_r = kb.sb("l0_bqr", [1, 1024], BF16)
    bu_r = kb.sb("l0_bur", [1, 2048], BF16)
    bd_c = kb.sb("l0_bdc", [128, 2], F32)
    bo_c = kb.sb("l0_boc", [128, CD], F32)
    eye_t = kb.sb("l0_eye", [128, 128], BF16)

    Xf = kb.sb("l0_Xf", [128, CD, 512], F32)
    Xb = kb.sb("l0_Xb", [128, CD, 512], BF16)
    KVL = kb.sb("l0_kvl", [128, 2, 512], BF16)
    Qt = [kb.sb(f"l0_qt{i}", [128, 1024], BF16) for i in range(2)]
    K2 = [kb.sb(f"l0_k2{i}", [128, 2048], BF16) for i in range(2)]
    V2 = [kb.sb(f"l0_v2{i}", [128, 2048], BF16) for i in range(2)]
    PR = kb.sb("l0_pr", [128, 8, 1024], BF16)
    PR2 = kb.sb("l0_pr2", [128, 8, 1024], BF16)
    Fold2 = kb.sb("l0_fold2", [128, 128], F32)
    Fold = kb.sb("l0_fold", [128, 128], F32)
    Sc = kb.sb("l0_s", [128, 256], F32)
    Am = kb.sb("l0_a", [128, 256], BF16)
    Lh = kb.sb("l0_l", [128, 16], F32)
    Rh = kb.sb("l0_r", [128, 16], F32)
    Ot = kb.sb("l0_ot", [128, 1024], F32)
    Ot2 = kb.sb("l0_ot2", [128, 1024], F32)
    Ob = [kb.sb(f"l0_ob{i}", [128, 1024], BF16) for i in range(2)]
    OT = kb.sb("l0_oT", [128, CD, 512], BF16)
    Yr = kb.sb("l0_y", [128, CD, 512], F32)
    Yb = kb.sb("l0_yb", [128, CD, 512], BF16)
    Ysq = kb.sb("l0_ysq", [128, CD, 512], BF16)

    bk = [kb.ps(f"l0_b{i}", [128, 512], F32) for i in range(4)]
    tp0 = kb.ps("l0_tp0", [128, 128], BF16)
    tp1 = kb.ps("l0_tp1", [128, 128], BF16)
    ps_ra = kb.ps("l0_rowa", [1, 512], F32)
    ps_rb = kb.ps("l0_rowb", [1, 512], F32)

    def f_wl(g):
        return [g.dma_start(out=Wq[:], in_=io["l0_wq"].rearrange("(c p) n -> p c n", p=128)),
                g.dma_start(out=Wd[:], in_=io["l0_wd"].rearrange("(c p) n -> p c n", p=128)),
                g.dma_start(out=Wu[:], in_=io["l0_wu"].rearrange("(c p) n -> p c n", p=128)),
                g.dma_start(out=Wo[:], in_=io["l0_wo"].rearrange("(c p) n -> p c n", p=128)),
                g.dma_start(out=bq_r[:], in_=io["l0_bq"][None, :]),
                g.dma_start(out=bu_r[:], in_=io["l0_bu"][None, :])]
    ev_wbf = p.add("gpsimd", f_wl, inc="poold", n_dma=6)

    def f_wl2(s):
        out = [s.dma_start(out=bd_c[:, c:c + 1], in_=io["l0_bd"][c * 128:(c + 1) * 128, None])
               for c in range(2)]
        out += [s.dma_start(out=bo_c[:, c:c + 1], in_=io["l0_bo"][c * 128:(c + 1) * 128, None])
                for c in range(CD)]
        out.append(s.dma_start(out=eye_t[:], in_=io["EYE_BF"]))
        return out
    ev_wsp = p.add("sync", f_wl2, inc="spd", n_dma=11)

    st = {}
    lu = {"Xf": None, "Xb": None, "OT": None, "Yr": None,
          "bk": [None] * 4, "tp": [None, None],
          "QK2": [None, None], "V2": [None, None], "Am": [],
          "Ob": [None, None]}
    pend_tp = []

    def emit_tp(item):
        ev_ob_, s0_, Ob_ = item
        ev_t = None
        for cd in range(CD):
            tp = (tp0, tp1)[cd % 2]

            def f_tp(t, cd=cd, tp=tp, ev_ob_=ev_ob_, Ob_=Ob_):
                return t.transpose(tp[:], Ob_[:, cd * 128:(cd + 1) * 128], eye_t[:])
            ev_t = p.add("tensor", f_tp,
                         waits=[ev_ob_, ev_wsp, lu["tp"][cd % 2]], inc="pe")

            def f_tpe(sc, cd=cd, tp=tp, s0_=s0_):
                return sc.activation(OT[:, cd, s0_:s0_ + 128], tp[:], AF.Identity,
                                     bias=cst["zero_c"][:, 0:1], scale=1.0)
            ev_te = p.add("scalar", f_tpe, waits=[ev_t, lu["OT"]], inc="act")
            lu["tp"][cd % 2] = ev_te
        lu["Ob"][0 if Ob_ is Ob[0] else 1] = ev_t
        return ev_te
    ev_store = None

    for tb in range(NTB):
        t0 = tb * 512

        def f_lx(s, t0=t0):
            return [s.dma_start(out=Xf[:], in_=xT[:, t0:t0 + 512]
                                .rearrange("(c p) n -> p c n", p=128))]
        ev_xf = p.add("sync", f_lx, waits=[lu["Xf"]], inc="spd", n_dma=1)

        def f_lxb(g, t0=t0):
            return [g.dma_start(out=Xb[:], in_=xT[:, t0:t0 + 512]
                                .rearrange("(c p) n -> p c n", p=128))]
        ev_xb = p.add("gpsimd", f_lxb, waits=[lu["Xb"]], inc="poold", n_dma=1)

        def f_kvl(t):
            last = None
            for cl in range(2):
                for cd in range(CD):
                    last = t.matmul(bk[cl][:], Wd[:, cd, cl * 128:(cl + 1) * 128],
                                    Xb[:, cd, :], start=(cd == 0), stop=(cd == CD - 1))
            return last
        ev_kvl_mm = p.add("tensor", f_kvl,
                          waits=[ev_wbf, ev_wsp, ev_xb, lu["bk"][0], lu["bk"][1]],
                          inc="pe")

        def f_kvl_ev(sc):
            sc.activation(KVL[:, 0, :], bk[0][:], AF.Identity, bias=bd_c[:, 0:1], scale=1.0)
            return sc.activation(KVL[:, 1, :], bk[1][:], AF.Identity,
                                 bias=bd_c[:, 1:2], scale=1.0)
        ev_kvl = p.add("scalar", f_kvl_ev, waits=[ev_kvl_mm], inc="act")
        lu["bk"][0] = lu["bk"][1] = ev_kvl

        ev_ot_tiles = []
        for tt4 in range(4):
            s0 = tt4 * 128
            tp_ = (tb * 4 + tt4) % 2
            Qt_, K2_, V2_ = Qt[tp_], K2[tp_], V2[tp_]

            def f_qk(t, s0=s0):
                for o in range(2):
                    for cd in range(CD):
                        t.matmul(bk[o][:], Xb[:, cd, s0:s0 + 128],
                                 Wq[:, cd, o * 512:(o + 1) * 512],
                                 start=(cd == 0), stop=False)
                    t.matmul(bk[o][:], cst["ones_r"][:], bq_r[0:1, o * 512:(o + 1) * 512],
                             start=False, stop=True)
                for o in range(2):
                    for cl in range(2):
                        t.matmul(bk[2 + o][:], KVL[:, cl, s0:s0 + 128],
                                 Wu[:, cl, o * 512:(o + 1) * 512],
                                 start=(cl == 0), stop=False)
                    last = t.matmul(bk[2 + o][:], cst["ones_r"][:],
                                    bu_r[0:1, o * 512:(o + 1) * 512], start=False, stop=True)
                return last
            ev_qk_mm = p.add("tensor", f_qk,
                             waits=[ev_kvl, cst["ev"]] + lu["bk"], inc="pe")

            def f_qk_ev(sc, Qt_=Qt_, K2_=K2_):
                sc.activation(Qt_[:, 0:512], bk[0][:], AF.Identity,
                              bias=cst["zero_c"][:, 0:1], scale=1.0)
                sc.activation(Qt_[:, 512:1024], bk[1][:], AF.Identity,
                              bias=cst["zero_c"][:, 0:1], scale=1.0)
                sc.activation(K2_[:, 0:512], bk[2][:], AF.Identity,
                              bias=cst["zero_c"][:, 0:1], scale=1.0)
                sc.activation(K2_[:, 1024:1536], bk[2][:], AF.Identity,
                              bias=cst["zero_c"][:, 0:1], scale=1.0)
                sc.activation(K2_[:, 512:1024], bk[3][:], AF.Identity,
                              bias=cst["zero_c"][:, 0:1], scale=1.0)
                return sc.activation(K2_[:, 1536:2048], bk[3][:], AF.Identity,
                                     bias=cst["zero_c"][:, 0:1], scale=1.0)
            ev_qk = p.add("scalar", f_qk_ev, waits=[ev_qk_mm] + lu["QK2"][tp_] if isinstance(lu["QK2"][tp_], list) else [ev_qk_mm, lu["QK2"][tp_]], inc="act")
            for i in range(4):
                lu["bk"][i] = ev_qk

            def f_vmm(t, s0=s0):
                for o in range(2):
                    for cl in range(2):
                        t.matmul(bk[o][:], KVL[:, cl, s0:s0 + 128],
                                 Wu[:, cl, (2 + o) * 512:(3 + o) * 512],
                                 start=(cl == 0), stop=False)
                    last = t.matmul(bk[o][:], cst["ones_r"][:],
                                    bu_r[0:1, (2 + o) * 512:(3 + o) * 512],
                                    start=False, stop=True)
                return last
            ev_v_mm = p.add("tensor", f_vmm, waits=[ev_qk], inc="pe")

            def f_v_ev(sc, V2_=V2_):
                sc.activation(V2_[:, 0:512], bk[0][:], AF.Identity,
                              bias=cst["zero_c"][:, 0:1], scale=1.0)
                sc.activation(V2_[:, 1024:1536], bk[0][:], AF.Identity,
                              bias=cst["zero_c"][:, 0:1], scale=1.0)
                sc.activation(V2_[:, 512:1024], bk[1][:], AF.Identity,
                              bias=cst["zero_c"][:, 0:1], scale=1.0)
                return sc.activation(V2_[:, 1536:2048], bk[1][:], AF.Identity,
                                     bias=cst["zero_c"][:, 0:1], scale=1.0)
            ev_v = p.add("scalar", f_v_ev, waits=[ev_v_mm] + (lu["V2"][tp_] if isinstance(lu["V2"][tp_], list) else [lu["V2"][tp_]]), inc="act")
            lu["bk"][0] = lu["bk"][1] = ev_v

            if tb == 0 and tt4 == 0 and "d_qt" in io:
                def f_dbg1(s):
                    return [s.dma_start(out=io["d_kvl"], in_=KVL[:]),
                            s.dma_start(out=io["d_qt"], in_=Qt[:]),
                            s.dma_start(out=io["d_k2"], in_=K2[:]),
                            s.dma_start(out=io["d_v2"], in_=V2[:])]
                p.add("sync", f_dbg1, waits=[ev_qk, ev_v, ev_kvl], inc="spd", n_dma=4)
            ev_s_both = []
            for half in range(2):
                p0 = half * 8
                eng, ch, ich = ("vector", "dve", "dve_i")
                PRb = PR if half == 0 else PR2

                def f_s(v, p0=p0, ich=ich, PRb=PRb, Qt_=Qt_, K2_=K2_):
                    qv = Qt_[:].unsqueeze(1).broadcast_to((128, 8, 1024))
                    kv = ap_win(K2_[:, p0 * 64:], 8, 64, 1024)
                    ins = v.tensor_tensor(PRb[:], qv, kv, ALU.mult)
                    prv = PRb[:].rearrange("p a (h d) -> p a h d", d=64)
                    w = 64
                    while w > 2:
                        kb.edge(ich, v, ins)
                        ins = v.tensor_tensor(prv[:, :, :, 0:w // 2], prv[:, :, :, 0:w // 2],
                                              prv[:, :, :, w // 2:w], ALU.add)
                        w //= 2
                    kb.edge(ich, v, ins)
                    sv = Sc[:, p0 * 16:(p0 + 8) * 16].rearrange("p (a h) -> p a h", h=16)
                    return v.tensor_tensor(sv, prv[:, :, :, 0:1].squeeze(3),
                                           prv[:, :, :, 1:2].squeeze(3), ALU.add)
                ev_s_both.append(p.add(eng, f_s, waits=[ev_qk, ev_v], inc=ch))
            ev_s = ev_s_both[0]
            lu["QK2"][tp_] = list(ev_s_both)

            if tb == 0 and tt4 == 0 and "d_sc" in io:
                def f_dbg2(s):
                    return [s.dma_start(out=io["d_sc"], in_=Sc[:])]
                p.add("sync", f_dbg2, waits=[ev_s], inc="spd", n_dma=1)

            def f_exp(sc):
                return sc.activation(Am[:], Sc[:], AF.Exp, bias=cst["cl0_c"][:, 0:1],
                                     scale=0.125)
            ev_am = p.add("scalar", f_exp, waits=ev_s_both + lu["Am"], inc="act")

            def f_l(v):
                ins = v.tensor_tensor(Fold[:], Am[:, 0:128], Am[:, 128:256], ALU.add)
                kb.edge("dve_i", v, ins)
                ins = v.tensor_tensor(Fold[:, 0:64], Fold[:, 0:64], Fold[:, 64:128], ALU.add)
                kb.edge("dve_i", v, ins)
                ins = v.tensor_tensor(Fold[:, 0:32], Fold[:, 0:32], Fold[:, 32:64], ALU.add)
                kb.edge("dve_i", v, ins)
                ins = v.tensor_tensor(Lh[:], Fold[:, 0:16], Fold[:, 16:32], ALU.add)
                kb.edge("dve_i", v, ins)
                return v.reciprocal(Rh[:], Lh[:])
            ev_r = p.add("vector", f_l, waits=[ev_am], inc="dve")

            if tb == 0 and tt4 == 0 and "d_am" in io:
                def f_dbg3(s):
                    return [s.dma_start(out=io["d_am"], in_=Am[:]),
                            s.dma_start(out=io["d_lh"], in_=Lh[:])]
                p.add("sync", f_dbg3, waits=[ev_am, ev_r], inc="spd", n_dma=2)
            ev_av_both = []
            for half in range(2):
                p0 = half * 8
                eng, ch, ich = ("vector", "dve", "dve_i")
                PRb = PR if half == 0 else PR2

                def f_av(v, p0=p0, half=half, ich=ich, PRb=PRb, V2_=V2_):
                    av = (Am[:, p0 * 16:(p0 + 8) * 16]
                          .rearrange("p (a h) -> p a h", h=16)
                          .unsqueeze(3).broadcast_to((128, 8, 16, 64)))
                    vv = ap_win(V2_[:, p0 * 64:], 8, 64, 1024).rearrange(
                        "p a (h d) -> p a h d", d=64)
                    pr4 = PRb[:].rearrange("p a (h d) -> p a h d", d=64)
                    ins = v.tensor_tensor(pr4, av, vv, ALU.mult)
                    kb.edge(ich, v, ins)
                    ins = v.tensor_tensor(PRb[:, 0:4, :], PRb[:, 0:4, :], PRb[:, 4:8, :],
                                          ALU.add)
                    kb.edge(ich, v, ins)
                    ins = v.tensor_tensor(PRb[:, 0:2, :], PRb[:, 0:2, :], PRb[:, 2:4, :],
                                          ALU.add)
                    kb.edge(ich, v, ins)
                    dst = Ot if half == 0 else Ot2
                    return v.tensor_tensor(dst[:], PRb[:, 0, :], PRb[:, 1, :], ALU.add)
                ev_av_both.append(p.add(eng, f_av, waits=[ev_r], inc=ch))
            ev_av = ev_av_both[0]
            lu["V2"][tp_] = list(ev_av_both)
            lu["Am"] = list(ev_av_both)

            Ob_ = Ob[tp_]

            def f_onorm(v, Ob_=Ob_):
                ins = v.tensor_tensor(Ot[:], Ot[:], Ot2[:], ALU.add)
                kb.edge("dve_i", v, ins)
                ot3 = Ot[:].rearrange("p (h d) -> p h d", d=64)
                ob3 = Ob_[:].rearrange("p (h d) -> p h d", d=64)
                rb = Rh[:].unsqueeze(2).broadcast_to((128, 16, 64))
                return v.tensor_tensor(ob3, ot3, rb, ALU.mult)
            ev_ob = p.add("vector", f_onorm,
                          waits=ev_av_both + [lu["Ob"][tp_]], inc="dve")
            pend_tp.append((ev_ob, s0, Ob_))
            if len(pend_tp) > 1:
                emit_tp(pend_tp.pop(0))

        while pend_tp:
            emit_tp(pend_tp.pop(0))
        ev_ot_tiles = [lu["tp"][0], lu["tp"][1]]
        ev_y = None
        for wave in range(2):
            def f_op(t, wave=wave):
                last = None
                for i in range(4):
                    co = wave * 4 + i
                    for cd in range(CD):
                        last = t.matmul(bk[i][:], Wo[:, cd, co * 128:(co + 1) * 128],
                                        OT[:, cd, :], start=(cd == 0), stop=(cd == CD - 1))
                return last
            ev_op = p.add("tensor", f_op,
                          waits=ev_ot_tiles + lu["bk"] + [ev_xf, ev_wsp], inc="pe")

            def f_op_ev(v, wave=wave):
                last = None
                for i in range(4):
                    co = wave * 4 + i
                    ins = v.tensor_scalar(Yr[:, co, :], bk[i][:], bo_c[:, co:co + 1], None,
                                          ALU.add)
                    kb.edge("dve_i", v, ins)
                    last = v.tensor_tensor(Yr[:, co, :], Yr[:, co, :], Xf[:, co, :],
                                           ALU.add)
                return last
            ev_y = p.add("vector", f_op_ev, waits=[ev_op, lu["Yr"]], inc="dve")
            for i in range(4):
                lu["bk"][i] = ev_y
        lu["Xf"] = ev_y
        lu["Xb"] = ev_s
        lu["OT"] = ev_y
        lu["Am"] = lu["Am"]

        if tb == 0 and "d_y" in io:
            def f_dbg5(s):
                return [s.dma_start(out=io["d_ot8"], in_=OT[:]),
                        s.dma_start(out=io["d_y"], in_=Yr[:])]
            p.add("sync", f_dbg5, waits=[ev_y], inc="spd", n_dma=2)
        ev_apply, ev_store = emit_ln_tail(
            p, kb, cst, st, pref="l0ln", Yr=Yr, Yb=Yb, Ysq=Ysq,
            b_m=bk[0], b_s=bk[1], ps_sum=ps_ra[0:1, :],
            ps_sq=ps_rb[0:1, :], outT=x1T, t0=t0, ev_y=ev_y)
        lu["Yr"] = ev_apply
        lu["bk"][0] = st["b_m"]
        lu["bk"][1] = st["b_s"]
    return ev_store


def emit_l0_ffn(p, kb, cst, io):
    x1T, x2T = io["x1T_in"], io["x2T"]
    NCH = 24
    SC127 = 1.0 / 127.0

    Wd = kb.sb("f0_Wd", [128, NCH, 1024], BF16)
    Wg = [kb.sb(f"f0_Wg{i}", [128, CD, 128], BF16) for i in range(2)]
    Wu = [kb.sb(f"f0_Wu{i}", [128, CD, 128], BF16) for i in range(2)]
    Xf = kb.sb("f0_Xf", [128, CD, 512], F32)
    Xb = kb.sb("f0_Xb", [128, CD, 512], BF16)
    Hh = kb.sb("f0_h", [128, NCH, 512], BF16)
    Sg = [kb.sb(f"f0_sg{i}", [128, 512], BF16) for i in range(2)]
    Yr = kb.sb("f0_y", [128, CD, 512], F32)
    Yb = kb.sb("f0_yb", [128, CD, 512], BF16)
    Ysq = kb.sb("f0_ysq", [128, CD, 512], BF16)

    bk = [kb.ps(f"f0_b{i}", [128, 512], F32) for i in range(4)]
    ps_ra = kb.ps("f0_rowa", [1, 512], F32)
    ps_rb = kb.ps("f0_rowb", [1, 512], F32)

    def f_wd(g):
        return [g.dma_start(out=Wd[:, c * 6:(c + 1) * 6, :],
                            in_=io["f0_wd"][:, :].rearrange("(c p) n -> p c n", p=128)
                            [:, c * 6:(c + 1) * 6, :])
                for c in range(4)]
    ev_wd = p.add("gpsimd", f_wd, inc="poold", n_dma=4)

    st = {}
    lu = {"Xf": None, "Xb": None, "Yr": None, "bk": [None] * 4,
          "Wg": [None, None], "Wu": [None, None], "Sg": [None, None], "Hh": None}
    ev_store = None
    ev_wbuf = {}

    for tb in range(NTB):
        t0 = tb * 512

        def f_lx(s, t0=t0):
            return [s.dma_start(out=Xf[:], in_=x1T[:, t0:t0 + 512]
                                .rearrange("(c p) n -> p c n", p=128))]
        ev_xf = p.add("sync", f_lx, waits=[lu["Xf"]], inc="spd", n_dma=1)

        def f_lxb(g, t0=t0):
            return [g.dma_start(out=Xb[:], in_=x1T[:, t0:t0 + 512]
                                .rearrange("(c p) n -> p c n", p=128))]
        ev_xb = p.add("gpsimd", f_lxb, waits=[lu["Xb"]], inc="poold", n_dma=1)

        for ch in range(NCH):
            par = ch % 2

            def f_wld(g, ch=ch, par=par):
                return [g.dma_start(out=Wg[par][:],
                                    in_=io["f0_wg"][:, ch * 128:(ch + 1) * 128]
                                    .rearrange("(c p) n -> p c n", p=128)),
                        g.dma_start(out=Wu[par][:],
                                    in_=io["f0_wu"][:, ch * 128:(ch + 1) * 128]
                                    .rearrange("(c p) n -> p c n", p=128))]
            ev_w = p.add("gpsimd", f_wld,
                         waits=[lu["Wg"][par], lu["Wu"][par]], inc="poold", n_dma=2)
            ev_wbuf[par] = ev_w

            b0, b1 = 2 * par, 2 * par + 1

            def f_mm(t, par=par, b0=b0, b1=b1):
                for cd in range(CD):
                    t.matmul(bk[b0][:], Wg[par][:, cd, :], Xb[:, cd, :],
                             start=(cd == 0), stop=(cd == CD - 1))
                last = None
                for cd in range(CD):
                    last = t.matmul(bk[b1][:], Wu[par][:, cd, :], Xb[:, cd, :],
                                    start=(cd == 0), stop=(cd == CD - 1))
                return last
            ev_mm = p.add("tensor", f_mm,
                          waits=[ev_w, ev_xb, lu["bk"][b0], lu["bk"][b1]], inc="pe")
            lu["Wg"][par] = ev_mm
            lu["Wu"][par] = ev_mm

            def f_silu(sc, par=par, b0=b0):
                return sc.activation(Sg[par][:], bk[b0][:], AF.Silu, scale=SC127)
            ev_silu = p.add("scalar", f_silu, waits=[ev_mm, lu["Sg"][par]], inc="act")
            lu["bk"][b0] = ev_silu

            def f_hmul(v, ch=ch, par=par, b1=b1):
                return v.scalar_tensor_tensor(Hh[:, ch, :], bk[b1][:], SC127, Sg[par][:],
                                              ALU.mult, ALU.mult)
            ev_h = p.add("vector", f_hmul, waits=[ev_silu, lu["Hh"]], inc="dve")
            lu["bk"][b1] = ev_h
            lu["Sg"][par] = ev_h
        lu["Xb"] = ev_mm if False else lu["Wg"][par]
        lu["Xb"] = ev_h
        ev_hdone = ev_h

        ev_y = None
        for wave in range(2):
            def f_dp(t, wave=wave):
                last = None
                for i in range(4):
                    co = wave * 4 + i
                    for ch in range(NCH):
                        last = t.matmul(bk[i][:], Wd[:, ch, co * 128:(co + 1) * 128],
                                        Hh[:, ch, :], start=(ch == 0),
                                        stop=(ch == NCH - 1))
                return last
            ev_dp = p.add("tensor", f_dp,
                          waits=[ev_hdone, ev_wd, ev_xf] + lu["bk"], inc="pe")

            def f_dp_ev(v, wave=wave):
                last = None
                for i in range(4):
                    co = wave * 4 + i
                    ins = v.tensor_scalar(Yr[:, co, :], bk[i][:], SC127, None, ALU.mult)
                    kb.edge("dve_i", v, ins)
                    last = v.tensor_tensor(Yr[:, co, :], Yr[:, co, :], Xf[:, co, :],
                                           ALU.add)
                return last
            ev_y = p.add("vector", f_dp_ev, waits=[ev_dp, lu["Yr"]], inc="dve")
            for i in range(4):
                lu["bk"][i] = ev_y
        lu["Xf"] = ev_y
        lu["Hh"] = ev_dp

        ev_apply, ev_store = emit_ln_tail(
            p, kb, cst, st, pref="f0ln", Yr=Yr, Yb=Yb, Ysq=Ysq,
            b_m=bk[0], b_s=bk[1], ps_sum=ps_ra[0:1, :],
            ps_sq=ps_rb[0:1, :], outT=x2T, t0=t0, ev_y=ev_y)
        if "x2bf_cc_in" in io:
            def f_bfst(g, t0=t0, ev_apply=ev_apply):
                return [g.dma_start(out=io["x2bf_cc_in"][:, t0:t0 + 512]
                                    .rearrange("(c p) n -> p c n", p=128), in_=Yr[:])]
            ev_store = p.add("gpsimd", f_bfst, waits=[ev_apply, ev_store],
                             inc="poold", n_dma=1)
            io.setdefault("x2bf_evs", []).append(ev_store)
        lu["Yr"] = ev_apply
        lu["bk"][0] = st["b_m"]
        lu["bk"][1] = st["b_s"]
    return ev_store


def emit_l1_attn(p, kb, cst, io):
    x2T = io.get("x2T_in")
    x2G = io.get("x2bf_cc_out")
    ev_g = io.get("ev_x2g")
    o_out = io["o_out"]
    NB = T_FULL // 512
    NKT = T_FULL // 128
    NCC = 4

    Wq = kb.sb("a1_Wq", [128, CD, 512], BF16)
    Wk = kb.sb("a1_Wk", [128, CD, 512], BF16)
    Wv = kb.sb("a1_Wv", [128, CD, 512], BF16)
    Xb = [kb.sb(f"a1_Xb{i}", [128, CD, 512], BF16) for i in range(2)]
    QT = kb.sb("a1_QT", [128, NCC, T_FULL], BF16)
    KT = kb.sb("a1_KT", [128, NCC, T_FULL], BF16)
    Vt = kb.sb("a1_V", [128, NKT, 8 * 65], BF16)
    Pt = [kb.sb(f"a1_P{i}", [128, 1024], BF16) for i in range(2)]
    Fsb = [kb.sb(f"a1_F{i}", [64, 1024], F32) for i in range(2)]
    frow = [kb.sb(f"a1_frow{i}", [1, 1024], F32) for i in range(2)]
    mask = kb.sb("a1_mask", [128, 896], F32)
    OT = kb.sb("a1_OT", [128, NCC, T_FULL], BF16)

    SP = [kb.ps(f"a1_s{i}", [128, 1024], F32) for i in range(2)]
    O = [kb.ps(f"a1_o{i}", [128, 512], F32) for i in range(4)]
    Sv = [SP[0][:, 0:512], SP[0][:, 512:1024], SP[1][:, 0:512], SP[1][:, 512:1024]]

    def f_wl(g):
        return [g.dma_start(out=Wq[:], in_=io["a1_wq"].rearrange("(c p) n -> p c n", p=128)),
                g.dma_start(out=Wk[:], in_=io["a1_wk"].rearrange("(c p) n -> p c n", p=128)),
                g.dma_start(out=Wv[:], in_=io["a1_wv"].rearrange("(c p) n -> p c n", p=128))]
    ev_w = p.add("gpsimd", f_wl, inc="poold", n_dma=3)

    def f_ml(s):
        return [s.dma_start(out=mask[:], in_=io["MEGAMASK"])]
    ev_mask = p.add("sync", f_ml, inc="spd", n_dma=1)

    def f_ones(v):
        return v.memset(Vt[:].rearrange("p k (h e) -> p k h e", e=65)[:, :, :, 64:65], 1.0)
    ev_vones = p.add("vector", f_ones, inc="dve")

    lu = {"Xb": [None, None], "S": [None] * 4, "O": [None] * 4, "QT": None,
          "KT": None, "Vt": ev_vones, "Pt": [None] * 2, "Fsb": None, "frow": None,
          "OT": None}
    for blk in range(NB):
        par = blk % 2
        t0 = blk * 512

        if x2G is None:
            def f_lx(g, t0=t0, par=par):
                return [g.dma_start(out=Xb[par][:], in_=x2T[:, t0:t0 + 512]
                                    .rearrange("(c p) n -> p c n", p=128))]
        else:
            half, tl = t0 // TOK, t0 % TOK
            def f_lx(g, half=half, tl=tl, par=par):
                return [g.dma_start(out=Xb[par][:],
                                    in_=x2G[half * D:(half + 1) * D, tl:tl + 512]
                                    .rearrange("(c p) n -> p c n", p=128))]
        ev_x = p.add("gpsimd", f_lx, waits=[lu["Xb"][par], ev_g], inc="poold", n_dma=1)

        def f_qmm(t, par=par):
            last = None
            for o in range(4):
                for cd in range(CD):
                    last = t.matmul(Sv[o], Wq[:, cd, o * 128:(o + 1) * 128],
                                    Xb[par][:, cd, :],
                                    start=(cd == 0), stop=(cd == CD - 1))
            return last
        ev_qmm = p.add("tensor", f_qmm, waits=[ev_w, ev_x] + lu["S"], inc="pe")

        def f_qev(sc, t0=t0):
            last = None
            for o in range(4):
                last = sc.activation(QT[:, o, t0:t0 + 512], Sv[o], AF.Identity,
                                     bias=cst["zero_c"][:, 0:1], scale=1.0)
            return last
        ev_qev = p.add("scalar", f_qev, waits=[ev_qmm, lu["QT"]], inc="act")
        for i in range(4):
            lu["S"][i] = ev_qev

        def f_kmm(t, par=par):
            last = None
            for o in range(4):
                for cd in range(CD):
                    last = t.matmul(Sv[o], Wk[:, cd, o * 128:(o + 1) * 128],
                                    Xb[par][:, cd, :],
                                    start=(cd == 0), stop=(cd == CD - 1))
            return last
        ev_kmm = p.add("tensor", f_kmm, waits=[ev_qev], inc="pe")

        def f_kev(sc, t0=t0):
            last = None
            for o in range(4):
                last = sc.activation(KT[:, o, t0:t0 + 512], Sv[o], AF.Identity,
                                     bias=cst["zero_c"][:, 0:1], scale=1.0)
            return last
        ev_kev = p.add("scalar", f_kev, waits=[ev_kmm, lu["KT"]], inc="act")
        for i in range(4):
            lu["S"][i] = ev_kev

        ev_vev = None
        for tt4 in range(4):
            kt = blk * 4 + tt4
            s0 = tt4 * 128

            def f_vmm(t, par=par, s0=s0, tt4=tt4):
                last = None
                for cd in range(CD):
                    last = t.matmul(O[tt4 % 4][:], Xb[par][:, cd, s0:s0 + 128],
                                    Wv[:, cd, :], start=(cd == 0), stop=(cd == CD - 1))
                return last
            ev_vmm = p.add("tensor", f_vmm,
                           waits=[ev_kev, lu["O"][tt4 % 4]], inc="pe")

            def f_vev(sc, kt=kt, tt4=tt4):
                vv = Vt[:].rearrange("p k (h e) -> p k h e", e=65)
                return sc.activation(vv[:, kt, :, 0:64], O[tt4 % 4][:]
                                     .rearrange("p (h d) -> p h d", d=64),
                                     AF.Identity, bias=cst["zero_c"][:, 0:1], scale=1.0)
            ev_vev = p.add("scalar", f_vev, waits=[ev_vmm, lu["Vt"]], inc="act")
            lu["O"][tt4 % 4] = ev_vev
        lu["Vt"] = ev_vev
        lu["Xb"][par] = ev_vev

    ev_nm = None
    for cc in range(NCC):
        for qb in range(NB):
            q0 = qb * 512
            kts = list(range(4 * qb, 4 * qb + 4)) + list(range(0, 4 * qb))
            ob = (qb % 2) * 2
            ev_exp_of = {}
            pending = None
            for ki, kt in enumerate(kts + [None]):
                if kt is not None:
                    par = ki % 2
                    diag = kt - 4 * qb

                    def f_smm(t, cc=cc, q0=q0, kt=kt, par=par):
                        t.matmul(SP[par][:, 0:512],
                                 KT[0:64, cc, kt * 128:(kt + 1) * 128],
                                 QT[0:64, cc, q0:q0 + 512], start=True, stop=True,
                                 tile_position=(0, 0))
                        return t.matmul(SP[par][:, 512:1024],
                                        KT[64:128, cc, kt * 128:(kt + 1) * 128],
                                        QT[64:128, cc, q0:q0 + 512], start=True,
                                        stop=True, tile_position=(64, 0))
                    ev_smm = p.add("tensor", f_smm,
                                   waits=[lu["S"][par * 2], lu["S"][par * 2 + 1],
                                          ev_kev, ev_mask], inc="pe")

                    ev_pre = ev_smm
                    if diag >= 0:
                        off = 384 - 128 * diag

                        def f_mask(v, par=par, off=off):
                            sp3 = SP[par].rearrange("p (h n) -> p h n", h=2)
                            mk = mask[:, off:off + 512].unsqueeze(1) \
                                .broadcast_to((128, 2, 512))
                            return v.tensor_tensor(sp3, sp3, mk, ALU.add)
                        ev_pre = p.add("vector", f_mask, waits=[ev_smm], inc="dve")

                    def f_exp(sc, par=par):
                        return sc.activation(Pt[par][:], SP[par][:], AF.Exp,
                                             bias=cst["cl1_c"][:, 0:1], scale=1.0 / 32.0)
                    ev_exp = p.add("scalar", f_exp,
                                   waits=[ev_pre, lu["Pt"][par]], inc="act")
                    ev_exp_of[kt] = ev_exp
                    lu["S"][par * 2] = ev_exp
                    lu["S"][par * 2 + 1] = ev_exp

                if pending is not None:
                    pk, ppar = pending
                    first, lastk = (pk == kts[0]), (pk == kts[-1])

                    def f_av(t, cc=cc, pk=pk, ppar=ppar, first=first, lastk=lastk, ob=ob):
                        vv = Vt[:, pk, cc * 130:cc * 130 + 130]
                        t.matmul(O[ob][0:65, :], vv[:, 0:65], Pt[ppar][:, 0:512],
                                 start=first, stop=lastk)
                        return t.matmul(O[ob + 1][0:65, :], vv[:, 65:130],
                                        Pt[ppar][:, 512:1024], start=first, stop=lastk)
                    w_extra = [lu["O"][ob], lu["O"][ob + 1]] if first else []
                    ev_av = p.add("tensor", f_av,
                                  waits=[ev_exp_of[pk]] + w_extra, inc="pe")
                    lu["Pt"][ppar] = ev_av
                pending = (kt, ki % 2) if kt is not None else None
            ev_avl = ev_av

            fpar = (cc * NB + qb) % 2

            def f_recip(v, ob=ob, fpar=fpar):
                ins = v.reciprocal(frow[fpar][0:1, 0:512], O[ob][64:65, :])
                kb.edge("dve_i", v, ins)
                return v.reciprocal(frow[fpar][0:1, 512:1024], O[ob + 1][64:65, :])
            ev_rc = p.add("vector", f_recip, waits=[ev_avl, lu["frow"]], inc="dve")

            def f_fb(t, fpar=fpar):
                t.matmul(SP[0][0:64, 0:512], cst["onesf_r"][0:1, 0:64],
                         frow[fpar][0:1, 0:512], start=True, stop=True)
                return t.matmul(SP[0][0:64, 512:1024], cst["onesf_r"][0:1, 0:64],
                                frow[fpar][0:1, 512:1024], start=True, stop=True)
            ev_fb = p.add("tensor", f_fb,
                          waits=[ev_rc, lu["S"][0], lu["S"][1]], inc="pe")

            def f_fcp(sc, fpar=fpar):
                return sc.activation(Fsb[fpar][:], SP[0][0:64, :], AF.Identity,
                                     bias=cst["zero_c"][0:64, 0:1], scale=1.0)
            ev_fcp = p.add("scalar", f_fcp, waits=[ev_fb, lu["Fsb"]], inc="act")
            lu["S"][0] = ev_fcp
            lu["S"][1] = ev_fcp
            lu["frow"] = ev_fcp

            def f_norm(v, cc=cc, q0=q0, ob=ob, fpar=fpar):
                ins = v.tensor_tensor(OT[0:64, cc, q0:q0 + 512], O[ob][0:64, :],
                                      Fsb[fpar][:, 0:512], ALU.mult)
                kb.edge("dve_i", v, ins)
                return v.tensor_tensor(OT[64:128, cc, q0:q0 + 512], O[ob + 1][0:64, :],
                                       Fsb[fpar][:, 512:1024], ALU.mult)
            ev_nm = p.add("vector", f_norm, waits=[ev_fcp], inc="dve")
            lu["O"][ob] = ev_nm
            lu["O"][ob + 1] = ev_nm
            lu["Fsb"] = ev_nm
            lu["OT"] = ev_nm

    def f_store(s):
        return [s.dma_start(out=o_out.rearrange("(c p) n -> p c n", p=128), in_=OT[:])]
    ev_st = p.add("sync", f_store, waits=[lu["OT"]], inc="spd", n_dma=1)
    return ev_st


def emit_l1_outproj(p, kb, cst, io):
    oT, x2T, x3T = io.get("oT_own"), io["x2T_own"], io["x3T"]
    oG = io.get("o_cc_out")
    ev_og = io.get("ev_og")

    Wo = kb.sb("a2_Wo", [128, CD, 1024], BF16)
    OTb = kb.sb("a2_OT", [128, CD, 512], BF16)
    if oG is not None:
        OT2 = kb.sb("a2_OT2", [128, CD, 512], BF16)
        Msel = kb.sb("a2_msel", [128, 2], F32)
    Xf = kb.sb("a2_Xf", [128, CD, 512], F32)
    Yr = kb.sb("a2_y", [128, CD, 512], F32)
    Yb = kb.sb("a2_yb", [128, CD, 512], BF16)
    Ysq = kb.sb("a2_ysq", [128, CD, 512], BF16)
    bk = [kb.ps(f"a2_b{i}", [128, 512], F32) for i in range(4)]
    ps_ra = kb.ps("a2_rowa", [1, 512], F32)
    ps_rb = kb.ps("a2_rowb", [1, 512], F32)

    def f_wl(g):
        out = [g.dma_start(out=Wo[:], in_=io["a1_wo"].rearrange("(c p) n -> p c n", p=128))]
        if oG is not None:
            out.append(g.dma_start(out=Msel[:], in_=io["msel"]))
        return out
    ev_w = p.add("gpsimd", f_wl, inc="poold", n_dma=2 if oG is not None else 1)

    st = {}
    lu = {"Xf": None, "OTb": None, "Yr": None, "bk": [None] * 4}
    ev_store = None
    for tb in range(NTB):
        t0 = tb * 512

        def f_lx(s, t0=t0):
            return [s.dma_start(out=Xf[:], in_=x2T[:, t0:t0 + 512]
                                .rearrange("(c p) n -> p c n", p=128))]
        ev_xf = p.add("sync", f_lx, waits=[lu["Xf"]], inc="spd", n_dma=1)

        if oG is None:
            def f_lo(g, t0=t0):
                return [g.dma_start(out=OTb[:], in_=oT[:, t0:t0 + 512]
                                    .rearrange("(c p) n -> p c n", p=128))]
            ev_o = p.add("gpsimd", f_lo, waits=[lu["OTb"]], inc="poold", n_dma=1)
        else:
            def f_lo(g, t0=t0):
                return [g.dma_start(out=OTb[:], in_=oG[:, t0:t0 + 512]
                                    .rearrange("(c p) n -> p c n", p=128)),
                        g.dma_start(out=OT2[:], in_=oG[:, TOK + t0:TOK + t0 + 512]
                                    .rearrange("(c p) n -> p c n", p=128))]
            ev_o2 = p.add("gpsimd", f_lo, waits=[lu["OTb"], ev_og], inc="poold", n_dma=2)

            def f_sel(v):
                ins = v.tensor_scalar(OTb[:], OTb[:], Msel[:, 0:1], None, ALU.mult)
                kb.edge("dve_i", v, ins)
                return v.scalar_tensor_tensor(OTb[:], OT2[:], Msel[:, 1:2], OTb[:],
                                              ALU.mult, ALU.add)
            ev_o = p.add("vector", f_sel, waits=[ev_o2, ev_w], inc="dve")

        ev_y = None
        for wave in range(2):
            def f_op(t, wave=wave):
                last = None
                for i in range(4):
                    co = wave * 4 + i
                    for cd in range(CD):
                        last = t.matmul(bk[i][:], Wo[:, cd, co * 128:(co + 1) * 128],
                                        OTb[:, cd, :], start=(cd == 0),
                                        stop=(cd == CD - 1))
                return last
            ev_op = p.add("tensor", f_op,
                          waits=[ev_w, ev_o, ev_xf] + lu["bk"], inc="pe")

            def f_op_ev(v, wave=wave):
                last = None
                for i in range(4):
                    co = wave * 4 + i
                    last = v.tensor_tensor(Yr[:, co, :], bk[i][:], Xf[:, co, :], ALU.add)
                return last
            ev_y = p.add("vector", f_op_ev, waits=[ev_op, lu["Yr"]], inc="dve")
            for i in range(4):
                lu["bk"][i] = ev_y
        lu["Xf"] = ev_y
        lu["OTb"] = ev_y

        ev_apply, ev_store = emit_ln_tail(
            p, kb, cst, st, pref="a2ln", Yr=Yr, Yb=Yb, Ysq=Ysq,
            b_m=bk[0], b_s=bk[1], ps_sum=ps_ra[0:1, :],
            ps_sq=ps_rb[0:1, :], outT=x3T, t0=t0, ev_y=ev_y)
        lu["Yr"] = ev_apply
        lu["bk"][0] = st["b_m"]
        lu["bk"][1] = st["b_s"]
    return ev_store


def emit_l1_ffn(p, kb, cst, io):
    x3T = io["x3T_in"]
    out_tok = io["out_tok"]
    g1h, g2h = io["g1_hbm"], io["g2_hbm"]
    NH = 32

    Hbuf = kb.sb("f1_h", [128, NH, 2048], BF16)
    arena = kb.sb("f1_arena", [128, 16384], BF16)
    Xb = arena[:, 0:16384].rearrange("p (c n) -> p c n", n=2048)
    Yr = arena[:, 0:8192].bitcast(F32).rearrange("p (c n) -> p c n", n=512)
    Yb = arena[:, 8192:12288].rearrange("p (c n) -> p c n", n=512)
    Ysq = arena[:, 12288:16384].rearrange("p (c n) -> p c n", n=512)
    arena2 = kb.sb("f1_arena2", [128, 4096], BF16)
    W1b = [arena2[:, i * 1024:(i + 1) * 1024].rearrange("p (c n) -> p c n", n=128)
           for i in range(2)]
    W2b = [arena2[:, 2048 + i * 1024:2048 + (i + 1) * 1024]
           .rearrange("p (c n) -> p c n", n=128) for i in range(2)]
    Hw = [kb.sb(f"f1_hw{i}", [128, NH, 128], BF16) for i in range(2)]
    Xf = kb.sb("f1_Xf", [128, CD, 512], F32)
    Sg = kb.sb("f1_sg", [128, 512], BF16)
    Gt = [kb.sb(f"f1_g{i}", [128, 512], BF16) for i in range(2)]
    Sc1 = kb.sb("f1_s1", [128, NH], F32)
    Sc2 = kb.sb("f1_s2", [128, NH], F32)
    Sh1 = kb.sb("f1_sh1", [128, NH], F32)
    Sh2 = kb.sb("f1_sh2", [128, NH], F32)
    Sc3 = kb.sb("f1_s3", [128, CD], F32)
    eyef = kb.sb("f1_eyef", [128, 128], F32)
    XTok = arena2[:, 0:2048].bitcast(F32)
    kb.cache["f1ln"] = (arena2[0:1, 2048:3072].bitcast(F32),
                        arena2[0:1, 3072:4096].bitcast(F32))

    bk = [kb.ps(f"f1_b{i}", [128, 512], F32) for i in range(4)]
    tpf = [kb.ps(f"f1_tp{i}", [128, 128], F32) for i in range(2)]
    ps_ra = kb.ps("f1_rowa", [1, 512], F32)
    ps_rb = kb.ps("f1_rowb", [1, 512], F32)

    def f_scl(s):
        out = []
        for (tile, name, n) in [(Sc1, "f1_s1v", NH), (Sc2, "f1_s2v", NH),
                                (Sh1, "f1_sh1v", NH), (Sh2, "f1_sh2v", NH),
                                (Sc3, "f1_s3v", CD)]:
            for ch in range(n):
                out.append(s.dma_start(out=tile[:, ch:ch + 1],
                                       in_=io[name][ch * 128:(ch + 1) * 128, None]))
        out.append(s.dma_start(out=eyef[:], in_=io["EYE_F"]))
        return out
    ev_scl = p.add("sync", f_scl, inc="spd", n_dma=4 * NH + CD + 1)

    def f_xb(g):
        return [g.dma_start(out=Xb, in_=x3T.rearrange("(c p) n -> p c n", p=128))]
    ev_xb = p.add("gpsimd", f_xb, inc="poold", n_dma=1)

    lu = {"W1b": [None, None], "W2b": [None, None], "Hw": [None, None],
          "bk": [None] * 4, "Sg": None, "Gt": [None, None], "Hbuf": None,
          "Xf": None, "Yr": None, "tpf": [None, None], "XTok": [None] * 8}
    st = {}

    ev_h = None
    ev_mm = None
    for ch in range(NH):
        par = ch % 2

        def f_wl(g, ch=ch, par=par):
            return [g.dma_start(out=W1b[par],
                                in_=io["f1_w1"][:, ch * 128:(ch + 1) * 128]
                                .rearrange("(c p) n -> p c n", p=128)),
                    g.dma_start(out=W2b[par],
                                in_=io["f1_w2"][:, ch * 128:(ch + 1) * 128]
                                .rearrange("(c p) n -> p c n", p=128))]
        ev_w = p.add("gpsimd", f_wl, waits=[lu["W1b"][par], lu["W2b"][par]],
                     inc="poold", n_dma=2)

        for tb in range(NTB):
            def f_mm(t, par=par, tb=tb):
                last = None
                for cd in range(CD):
                    t.matmul(bk[0][:], W1b[par][:, cd, :],
                             Xb[:, cd, tb * 512:(tb + 1) * 512],
                             start=(cd == 0), stop=(cd == CD - 1))
                for cd in range(CD):
                    last = t.matmul(bk[1][:], W2b[par][:, cd, :],
                                    Xb[:, cd, tb * 512:(tb + 1) * 512],
                                    start=(cd == 0), stop=(cd == CD - 1))
                return last
            ev_mm = p.add("tensor", f_mm,
                          waits=[ev_w, ev_xb, lu["bk"][0], lu["bk"][1]], inc="pe")

            def f_pre(v, ch=ch):
                return v.tensor_scalar(Sg[:], bk[0][:], Sc1[:, ch:ch + 1], None, ALU.mult)
            ev_pre = p.add("vector", f_pre, waits=[ev_mm, lu["Sg"], ev_scl], inc="dve")

            def f_silu(sc):
                return sc.activation(Sg[:], Sg[:], AF.Silu)
            ev_silu = p.add("scalar", f_silu, waits=[ev_pre], inc="act")
            lu["bk"][0] = ev_silu

            def f_hm(v, ch=ch, tb=tb):
                ins = v.tensor_scalar(Gt[0][:], bk[1][:], Sc2[:, ch:ch + 1], None, ALU.mult)
                kb.edge("dve_i", v, ins)
                return v.tensor_tensor(Hbuf[:, ch, tb * 512:(tb + 1) * 512], Gt[0][:],
                                       Sg[:], ALU.mult)
            ev_h = p.add("vector", f_hm,
                         waits=[ev_silu, lu["Gt"][0], lu["Hbuf"]], inc="dve")
            lu["bk"][1] = ev_h
            lu["Sg"] = ev_h
            lu["Gt"][0] = ev_h
        lu["W1b"][par] = ev_mm
        lu["W2b"][par] = ev_mm
    ev_hdone = ev_h

    def matmul_4096(src_ev, wname, scol, dsth):
        ev_s = None
        lu["Gt"] = [lu["Gt"][0], lu["Gt"][1]]
        for co in range(NH):
            par = co % 2

            def f_wl(g, co=co, par=par):
                return [g.dma_start(out=Hw[par][:],
                                    in_=io[wname][:, co * 128:(co + 1) * 128]
                                    .rearrange("(c p) n -> p c n", p=128))]
            ev_w = p.add("gpsimd", f_wl, waits=[lu["Hw"][par]], inc="poold", n_dma=1)

            for tb in range(NTB):
                bki = tb % 2

                def f_mm(t, par=par, tb=tb, bki=bki):
                    last = None
                    for ci in range(NH):
                        last = t.matmul(bk[bki][:], Hw[par][:, ci, :],
                                        Hbuf[:, ci, tb * 512:(tb + 1) * 512],
                                        start=(ci == 0), stop=(ci == NH - 1))
                    return last
                ev_mm2 = p.add("tensor", f_mm,
                               waits=[ev_w, src_ev, lu["bk"][bki]], inc="pe")

                def f_pre(v, co=co, bki=bki):
                    return v.tensor_scalar(Gt[bki][:], bk[bki][:], scol[:, co:co + 1],
                                           None, ALU.mult)
                ev_pre = p.add("vector", f_pre,
                               waits=[ev_mm2, lu["Gt"][bki], ev_scl], inc="dve")

                def f_gelu(sc, bki=bki):
                    return sc.activation(Gt[bki][:], Gt[bki][:], AF.Gelu)
                ev_g = p.add("scalar", f_gelu, waits=[ev_pre], inc="act")
                lu["bk"][bki] = ev_g

                def f_st(s, co=co, tb=tb, bki=bki):
                    return [s.dma_start(out=dsth[co * 128:(co + 1) * 128,
                                                 tb * 512:(tb + 1) * 512],
                                        in_=Gt[bki][:])]
                ev_s = p.add("sync", f_st, waits=[ev_g], inc="spd", n_dma=1)
                lu["Gt"][bki] = ev_s
            lu["Hw"][par] = ev_mm2
        return ev_s

    ev_g1 = matmul_4096(ev_hdone, "f1_h1", Sh1, g1h)

    def f_lg1(g):
        return [g.dma_start(out=Hbuf[:, c * 8:(c + 1) * 8, :],
                            in_=g1h.rearrange("(c p) n -> p c n", p=128)
                            [:, c * 8:(c + 1) * 8, :]) for c in range(4)]
    ev_lg1 = p.add("gpsimd", f_lg1, waits=[ev_g1], inc="poold", n_dma=4)

    ev_g2 = matmul_4096(ev_lg1, "f1_h2", Sh2, g2h)

    def f_lg2(g):
        return [g.dma_start(out=Hbuf[:, c * 8:(c + 1) * 8, :],
                            in_=g2h.rearrange("(c p) n -> p c n", p=128)
                            [:, c * 8:(c + 1) * 8, :]) for c in range(4)]
    ev_lg2 = p.add("gpsimd", f_lg2, waits=[ev_g2], inc="poold", n_dma=4)

    ev_fin = None
    lu["bk"] = [None] * 4
    lu["Yr"] = ev_hdone
    lu["Hw"] = [lu["Hw"][0], lu["Hw"][1]]
    for tb in range(NTB):
        t0 = tb * 512

        def f_lx(s, t0=t0):
            return [s.dma_start(out=Xf[:], in_=x3T[:, t0:t0 + 512]
                                .rearrange("(c p) n -> p c n", p=128))]
        ev_xf = p.add("sync", f_lx, waits=[lu["Xf"]], inc="spd", n_dma=1)

        ev_y = None
        for co in range(CD):
            par = co % 2

            def f_w3(g, co=co, par=par, tb=tb):
                return [g.dma_start(out=Hw[par][:],
                                    in_=io["f1_w3"][:, co * 128:(co + 1) * 128]
                                    .rearrange("(c p) n -> p c n", p=128))]
            ev_w3 = p.add("gpsimd", f_w3, waits=[lu["Hw"][par]], inc="poold", n_dma=1)

            def f_dp(t, par=par, tb=tb):
                last = None
                for ci in range(NH):
                    last = t.matmul(bk[par][:], Hw[par][:, ci, :],
                                    Hbuf[:, ci, tb * 512:(tb + 1) * 512],
                                    start=(ci == 0), stop=(ci == NH - 1))
                return last
            ev_dp = p.add("tensor", f_dp,
                          waits=[ev_w3, ev_lg2, ev_xf, lu["bk"][par]], inc="pe")
            lu["Hw"][par] = ev_dp

            def f_ev(v, co=co, par=par):
                ins = v.tensor_scalar(Yr[:, co, :], bk[par][:], Sc3[:, co:co + 1],
                                      None, ALU.mult)
                kb.edge("dve_i", v, ins)
                return v.tensor_tensor(Yr[:, co, :], Yr[:, co, :], Xf[:, co, :], ALU.add)
            ev_y = p.add("vector", f_ev, waits=[ev_dp, lu["Yr"], ev_scl], inc="dve")
            lu["bk"][par] = ev_y
        lu["Xf"] = ev_y
        lu["Yr"] = None

        ev_apply, _ = emit_ln_tail(
            p, kb, cst, st, pref="f1ln", Yr=Yr, Yb=Yb, Ysq=Ysq,
            b_m=bk[2], b_s=bk[3], ps_sum=ps_ra[0:1, :],
            ps_sq=ps_rb[0:1, :], outT=None, t0=t0, ev_y=ev_y)
        lu["Yr"] = ev_apply

        for tt in range(4):
            for cd in range(CD):
                tp = tpf[cd % 2]

                def f_tp(t, tt=tt, cd=cd, tp=tp):
                    return t.transpose(tp[:], Yr[:, cd, tt * 128:(tt + 1) * 128], eyef[:])
                ev_t = p.add("tensor", f_tp,
                             waits=[ev_apply, ev_scl, lu["tpf"][cd % 2]], inc="pe")

                def f_te(sc, cd=cd, tp=tp):
                    return sc.activation(XTok[:, cd * 128:(cd + 1) * 128], tp[:],
                                         AF.Identity, bias=cst["zero_c"][:, 0:1],
                                         scale=1.0)
                ev_te = p.add("scalar", f_te,
                              waits=[ev_t, lu["XTok"][cd]], inc="act")
                lu["tpf"][cd % 2] = ev_te

            def f_sto(s, t0=t0, tt=tt):
                return [s.dma_start(out=out_tok[t0 + tt * 128:t0 + (tt + 1) * 128, :],
                                    in_=XTok)]
            ev_fin = p.add("sync", f_sto,
                           waits=[lu["tpf"][0], lu["tpf"][1]], inc="spd", n_dma=1)
            for cd in range(CD):
                lu["XTok"][cd] = ev_fin
    return ev_fin


def build_l0attn_nc():
    nc = bass.Bass("TRN2", target_bir_lowering=False, detect_race_conditions=False)
    io = {
        "xT": nc.dram_tensor("xT", [D, TOK], F32, kind="ExternalInput").ap(),
        "x1T": nc.dram_tensor("x1T", [D, TOK], F32, kind="ExternalOutput").ap(),
        "l0_wq": nc.dram_tensor("l0_wq", [D, D], F32, kind="ExternalInput").ap(),
        "l0_bq": nc.dram_tensor("l0_bq", [D], F32, kind="ExternalInput").ap(),
        "l0_wd": nc.dram_tensor("l0_wd", [D, 256], F32, kind="ExternalInput").ap(),
        "l0_bd": nc.dram_tensor("l0_bd", [256], F32, kind="ExternalInput").ap(),
        "l0_wu": nc.dram_tensor("l0_wu", [256, 2048], F32, kind="ExternalInput").ap(),
        "l0_bu": nc.dram_tensor("l0_bu", [2048], F32, kind="ExternalInput").ap(),
        "l0_wo": nc.dram_tensor("l0_wo", [D, D], F32, kind="ExternalInput").ap(),
        "l0_bo": nc.dram_tensor("l0_bo", [D], F32, kind="ExternalInput").ap(),
        "EYE_BF": nc.dram_tensor("EYE_BF", [128, 128], BF16, kind="ExternalInput").ap(),
    }
    import os
    if os.environ.get("KDEBUG"):
        for nm, shape, dt in [("d_kvl", [128, 2, 512], BF16), ("d_qt", [128, 1024], BF16),
                              ("d_k2", [128, 2048], BF16), ("d_v2", [128, 2048], BF16),
                              ("d_sc", [128, 256], F32), ("d_am", [128, 256], BF16),
                              ("d_lh", [128, 16], F32), ("d_ob", [128, 1024], BF16),
                              ("d_ot8", [128, CD, 512], BF16), ("d_y", [128, CD, 512], F32)]:
            io[nm] = nc.dram_tensor(nm, shape, dt, kind="ExternalOutput").ap()
    kb = KB(nc)
    p = Prog()
    cst = emit_consts(p, kb)
    ev_store = emit_l0_attn(p, kb, cst, io)

    def f_fin(s):
        return None
    p.add("sync", lambda s: None, waits=[ev_store])
    kb.realize(p)
    return nc


def l0attn_host_inputs(inputs):
    wu = np.ascontiguousarray(inputs["L0_attn_Wup"])
    bu = np.ascontiguousarray(inputs["L0_attn_bup"])
    perm = np.empty(2048, dtype=np.int64)
    g = np.arange(16)
    d = np.arange(64)
    perm[(g[:, None] * 64 + d[None, :]).ravel()] = (g[:, None] * 128 + d[None, :]).ravel()
    perm[(1024 + g[:, None] * 64 + d[None, :]).ravel()] = \
        (g[:, None] * 128 + 64 + d[None, :]).ravel()
    cc = host_constants()
    return {
        "l0_wq": np.ascontiguousarray(inputs["L0_attn_Wq"]),
        "l0_bq": np.ascontiguousarray(inputs["L0_attn_bq"]),
        "l0_wd": np.ascontiguousarray(inputs["L0_attn_Wdown"]),
        "l0_bd": np.ascontiguousarray(inputs["L0_attn_bdown"]),
        "l0_wu": np.ascontiguousarray(wu[:, perm]),
        "l0_bu": np.ascontiguousarray(bu[perm]),
        "l0_wo": np.ascontiguousarray(inputs["L0_attn_Wout"]),
        "l0_bo": np.ascontiguousarray(inputs["L0_attn_bout"]),
        "EYE_BF": cc["EYE_BF"],
    }


def run_l0attn(inputs, x_cur):
    shared = l0attn_host_inputs(inputs)
    nc = build_l0attn_nc()
    in_maps = []
    for c in range(N_CORES):
        b, h = c // 2, c % 2
        xT = np.ascontiguousarray(x_cur[b, h * TOK:(h + 1) * TOK, :].T)
        m = dict(shared)
        m["xT"] = xT
        in_maps.append(m)
    res = run_bass_kernel_spmd(nc, in_maps, core_ids=list(range(N_CORES)))
    out = np.empty((B, T_FULL, D), dtype=np.float32)
    for c in range(N_CORES):
        b, h = c // 2, c % 2
        out[b, h * TOK:(h + 1) * TOK, :] = res.results[c]["x1T"].T
    return out


def build_l0ffn_nc():
    nc = bass.Bass("TRN2", target_bir_lowering=False, detect_race_conditions=False)
    io = {
        "x1T_in": nc.dram_tensor("x1T_in", [D, TOK], F32, kind="ExternalInput").ap(),
        "x2T": nc.dram_tensor("x2T", [D, TOK], F32, kind="ExternalOutput").ap(),
        "f0_wg": nc.dram_tensor("f0_wg", [D, 3072], I32, kind="ExternalInput").ap(),
        "f0_wu": nc.dram_tensor("f0_wu", [D, 3072], I32, kind="ExternalInput").ap(),
        "f0_wd": nc.dram_tensor("f0_wd", [3072, D], I32, kind="ExternalInput").ap(),
    }
    kb = KB(nc)
    p = Prog()
    cst = emit_consts(p, kb)
    ev_store = emit_l0_ffn(p, kb, cst, io)
    p.add("sync", lambda s: None, waits=[ev_store])
    kb.realize(p)
    return nc


def l0ffn_host_inputs(inputs):
    return {
        "f0_wg": np.ascontiguousarray(inputs["L0_ffn_w_gate"].T),
        "f0_wu": np.ascontiguousarray(inputs["L0_ffn_w_up"].T),
        "f0_wd": np.ascontiguousarray(inputs["L0_ffn_w_down"].T),
    }


def run_l0ffn(inputs, x1):
    shared = l0ffn_host_inputs(inputs)
    nc = build_l0ffn_nc()
    in_maps = []
    for c in range(N_CORES):
        b, h = c // 2, c % 2
        m = dict(shared)
        m["x1T_in"] = np.ascontiguousarray(x1[b, h * TOK:(h + 1) * TOK, :].T)
        in_maps.append(m)
    res = run_bass_kernel_spmd(nc, in_maps, core_ids=list(range(N_CORES)))
    out = np.empty((B, T_FULL, D), dtype=np.float32)
    for c in range(N_CORES):
        b, h = c // 2, c % 2
        out[b, h * TOK:(h + 1) * TOK, :] = res.results[c]["x2T"].T
    return out


def build_l1attn_nc():
    nc = bass.Bass("TRN2", target_bir_lowering=False, detect_race_conditions=False)
    io = {
        "x2T_in": nc.dram_tensor("x2T_in", [D, T_FULL], F32, kind="ExternalInput").ap(),
        "o_out": nc.dram_tensor("o_out", [512, T_FULL], BF16, kind="ExternalOutput").ap(),
        "a1_wq": nc.dram_tensor("a1_wq", [D, 512], F32, kind="ExternalInput").ap(),
        "a1_wk": nc.dram_tensor("a1_wk", [D, 512], F32, kind="ExternalInput").ap(),
        "a1_wv": nc.dram_tensor("a1_wv", [D, 512], F32, kind="ExternalInput").ap(),
        "MEGAMASK": nc.dram_tensor("MEGAMASK", [128, 896], F32, kind="ExternalInput").ap(),
    }
    kb = KB(nc)
    p = Prog()
    cst = emit_consts(p, kb)
    ev = emit_l1_attn(p, kb, cst, io)
    p.add("sync", lambda s: None, waits=[ev])
    kb.realize(p)
    return nc


def build_l1out_nc():
    nc = bass.Bass("TRN2", target_bir_lowering=False, detect_race_conditions=False)
    io = {
        "oT_own": nc.dram_tensor("oT_own", [D, TOK], BF16, kind="ExternalInput").ap(),
        "x2T_own": nc.dram_tensor("x2T_own", [D, TOK], F32, kind="ExternalInput").ap(),
        "x3T": nc.dram_tensor("x3T", [D, TOK], F32, kind="ExternalOutput").ap(),
        "a1_wo": nc.dram_tensor("a1_wo", [D, D], F32, kind="ExternalInput").ap(),
    }
    kb = KB(nc)
    p = Prog()
    cst = emit_consts(p, kb)
    ev = emit_l1_outproj(p, kb, cst, io)
    p.add("sync", lambda s: None, waits=[ev])
    kb.realize(p)
    return nc


def l1_head_weight_slices(inputs):
    wqkv = inputs["L1_attn_w_qkv"]
    out = {}
    for hc in range(2):
        cols_q, cols_k, cols_v = [], [], []
        for h in range(8 * hc, 8 * hc + 8):
            cols_q += list(range(h * 192, h * 192 + 64))
            cols_k += list(range(h * 192 + 64, h * 192 + 128))
            cols_v += list(range(h * 192 + 128, h * 192 + 192))
        out[hc] = (np.ascontiguousarray(wqkv[:, cols_q]),
                   np.ascontiguousarray(wqkv[:, cols_k]),
                   np.ascontiguousarray(wqkv[:, cols_v]))
    return out


def run_l1attn(inputs, x2):
    cc = host_constants()
    wslices = l1_head_weight_slices(inputs)
    nc = build_l1attn_nc()
    in_maps = []
    x2T = [np.ascontiguousarray(x2[b].T) for b in range(B)]
    for c in range(N_CORES):
        b, hc = c // 2, c % 2
        wq, wk, wv = wslices[hc]
        in_maps.append(dict(x2T_in=x2T[b], a1_wq=wq, a1_wk=wk, a1_wv=wv,
                            MEGAMASK=cc["MEGAMASK"]))
    res = run_bass_kernel_spmd(nc, in_maps, core_ids=list(range(N_CORES)))
    o_full = []
    for b in range(B):
        o_full.append(np.concatenate([res.results[2 * b]["o_out"],
                                      res.results[2 * b + 1]["o_out"]], axis=0))
    return o_full


def run_l1out(inputs, x2, o_full):
    nc = build_l1out_nc()
    wo = np.ascontiguousarray(inputs["L1_attn_w_o"])
    in_maps = []
    for c in range(N_CORES):
        b, h = c // 2, c % 2
        in_maps.append(dict(
            oT_own=np.ascontiguousarray(o_full[b][:, h * TOK:(h + 1) * TOK]),
            x2T_own=np.ascontiguousarray(x2[b, h * TOK:(h + 1) * TOK, :].T),
            a1_wo=wo))
    res = run_bass_kernel_spmd(nc, in_maps, core_ids=list(range(N_CORES)))
    out = np.empty((B, T_FULL, D), dtype=np.float32)
    for c in range(N_CORES):
        b, h = c // 2, c % 2
        out[b, h * TOK:(h + 1) * TOK, :] = res.results[c]["x3T"].T
    return out


def build_l1ffn_nc():
    nc = bass.Bass("TRN2", target_bir_lowering=False, detect_race_conditions=False)
    io = {
        "x3T_in": nc.dram_tensor("x3T_in", [D, TOK], F32, kind="ExternalInput").ap(),
        "out_tok": nc.dram_tensor("out_tok", [TOK, D], F32, kind="ExternalOutput").ap(),
        "g1_hbm": nc.dram_tensor("g1_hbm", [4096, TOK], BF16, kind="Internal").ap(),
        "g2_hbm": nc.dram_tensor("g2_hbm", [4096, TOK], BF16, kind="Internal").ap(),
        "f1_w1": nc.dram_tensor("f1_w1", [D, 4096], F32, kind="ExternalInput").ap(),
        "f1_w2": nc.dram_tensor("f1_w2", [D, 4096], F32, kind="ExternalInput").ap(),
        "f1_h1": nc.dram_tensor("f1_h1", [4096, 4096], F32, kind="ExternalInput").ap(),
        "f1_h2": nc.dram_tensor("f1_h2", [4096, 4096], F32, kind="ExternalInput").ap(),
        "f1_w3": nc.dram_tensor("f1_w3", [4096, D], F32, kind="ExternalInput").ap(),
        "f1_s1v": nc.dram_tensor("f1_s1v", [4096], F32, kind="ExternalInput").ap(),
        "f1_s2v": nc.dram_tensor("f1_s2v", [4096], F32, kind="ExternalInput").ap(),
        "f1_sh1v": nc.dram_tensor("f1_sh1v", [4096], F32, kind="ExternalInput").ap(),
        "f1_sh2v": nc.dram_tensor("f1_sh2v", [4096], F32, kind="ExternalInput").ap(),
        "f1_s3v": nc.dram_tensor("f1_s3v", [D], F32, kind="ExternalInput").ap(),
        "EYE_F": nc.dram_tensor("EYE_F", [128, 128], F32, kind="ExternalInput").ap(),
    }
    kb = KB(nc)
    p = Prog()
    cst = emit_consts(p, kb)
    ev = emit_l1_ffn(p, kb, cst, io)
    p.add("sync", lambda s: None, waits=[ev])
    kb.realize(p)
    return nc


def l1ffn_host_inputs(inputs):
    cc = host_constants()
    return {
        "f1_w1": np.ascontiguousarray(inputs["L1_ffn_w1_w"].T),
        "f1_w2": np.ascontiguousarray(inputs["L1_ffn_w2_w"].T),
        "f1_h1": np.ascontiguousarray(inputs["L1_ffn_h1_w"].T),
        "f1_h2": np.ascontiguousarray(inputs["L1_ffn_h2_w"].T),
        "f1_w3": np.ascontiguousarray(inputs["L1_ffn_w3_w"].T),
        "f1_s1v": np.ascontiguousarray(inputs["L1_ffn_w1_s"]),
        "f1_s2v": np.ascontiguousarray(inputs["L1_ffn_w2_s"]),
        "f1_sh1v": np.ascontiguousarray(inputs["L1_ffn_h1_s"]),
        "f1_sh2v": np.ascontiguousarray(inputs["L1_ffn_h2_s"]),
        "f1_s3v": np.ascontiguousarray(inputs["L1_ffn_w3_s"]),
        "EYE_F": cc["EYE_F"],
    }


def run_l1ffn(inputs, x3):
    shared = l1ffn_host_inputs(inputs)
    nc = build_l1ffn_nc()
    in_maps = []
    for c in range(N_CORES):
        b, h = c // 2, c % 2
        m = dict(shared)
        m["x3T_in"] = np.ascontiguousarray(x3[b, h * TOK:(h + 1) * TOK, :].T)
        in_maps.append(m)
    res = run_bass_kernel_spmd(nc, in_maps, core_ids=list(range(N_CORES)))
    out = np.empty((B, T_FULL, D), dtype=np.float32)
    for c in range(N_CORES):
        b, h = c // 2, c % 2
        out[b, h * TOK:(h + 1) * TOK, :] = res.results[c]["out_tok"]
    return out


PAIRS = [[0, 1], [2, 3], [4, 5], [6, 7]]


def build_fused_nc():
    nc = bass.Bass("TRN2", target_bir_lowering=False, detect_race_conditions=False)
    ext_f32 = lambda n, s: nc.dram_tensor(n, s, F32, kind="ExternalInput").ap()
    io = {
        "xT": ext_f32("xT", [D, TOK]),
        "l0_wq": ext_f32("l0_wq", [D, D]), "l0_bq": ext_f32("l0_bq", [D]),
        "l0_wd": ext_f32("l0_wd", [D, 256]), "l0_bd": ext_f32("l0_bd", [256]),
        "l0_wu": ext_f32("l0_wu", [256, 2048]), "l0_bu": ext_f32("l0_bu", [2048]),
        "l0_wo": ext_f32("l0_wo", [D, D]), "l0_bo": ext_f32("l0_bo", [D]),
        "EYE_BF": nc.dram_tensor("EYE_BF", [128, 128], BF16, kind="ExternalInput").ap(),
        "f0_wg": nc.dram_tensor("f0_wg", [D, 3072], I32, kind="ExternalInput").ap(),
        "f0_wu": nc.dram_tensor("f0_wu", [D, 3072], I32, kind="ExternalInput").ap(),
        "f0_wd": nc.dram_tensor("f0_wd", [3072, D], I32, kind="ExternalInput").ap(),
        "a1_wq": ext_f32("a1_wq", [D, 512]), "a1_wk": ext_f32("a1_wk", [D, 512]),
        "a1_wv": ext_f32("a1_wv", [D, 512]), "a1_wo": ext_f32("a1_wo", [D, D]),
        "MEGAMASK": ext_f32("MEGAMASK", [128, 896]),
        "msel": ext_f32("msel", [128, 2]),
        "f1_w1": ext_f32("f1_w1", [D, 4096]), "f1_w2": ext_f32("f1_w2", [D, 4096]),
        "f1_h1": ext_f32("f1_h1", [4096, 4096]), "f1_h2": ext_f32("f1_h2", [4096, 4096]),
        "f1_w3": ext_f32("f1_w3", [4096, D]),
        "f1_s1v": ext_f32("f1_s1v", [4096]), "f1_s2v": ext_f32("f1_s2v", [4096]),
        "f1_sh1v": ext_f32("f1_sh1v", [4096]), "f1_sh2v": ext_f32("f1_sh2v", [4096]),
        "f1_s3v": ext_f32("f1_s3v", [D]),
        "EYE_F": ext_f32("EYE_F", [128, 128]),
        "out_tok": nc.dram_tensor("out_tok", [TOK, D], F32, kind="ExternalOutput").ap(),
        "x1T": nc.dram_tensor("x1T", [D, TOK], F32, kind="Internal").ap(),
        "x3T": nc.dram_tensor("x3T", [D, TOK], F32, kind="Internal").ap(),
        "g1_hbm": nc.dram_tensor("g1_hbm", [4096, TOK], BF16, kind="Internal").ap(),
        "g2_hbm": nc.dram_tensor("g2_hbm", [4096, TOK], BF16, kind="Internal").ap(),
    }
    x2T = nc.dram_tensor("x2T", [D, TOK], F32, kind="Internal")
    io["x2T"] = x2T.ap()
    x2bf_in = nc.dram_tensor("x2bf_in", [D, TOK], BF16)
    x2bf_out = nc.dram_tensor("x2bf_out", [2 * D, TOK], BF16)
    o_cc_in = nc.dram_tensor("o_cc_in", [512, T_FULL], BF16)
    o_cc_out = nc.dram_tensor("o_cc_out", [D, T_FULL], BF16)
    io["x2bf_cc_in"] = x2bf_in.ap()
    io["x2bf_cc_out"] = x2bf_out.ap()
    io["o_out"] = o_cc_in.ap()
    io["o_cc_out"] = o_cc_out.ap()

    kb = KB(nc, fused=True)
    p = Prog()
    cst = emit_consts(p, kb)

    io["x1T_in"] = io["x1T"]
    io["x2T_own"] = io["x2T"]
    io["x3T_in"] = io["x3T"]

    p.scope = "B0_l0attn"
    emit_l0_attn(p, kb, cst, io)
    kb.block_boundary(p)
    p.scope = "B1_l0ffn"
    emit_l0_ffn(p, kb, cst, io)
    kb.block_boundary(p)
    p.scope = "B2_cc1"

    def f_cc1(g):
        return g.collective_compute("AllGather", ALU.bypass, replica_groups=PAIRS,
                                    ins=[x2bf_in.ap().opt()],
                                    outs=[x2bf_out.ap().opt()])
    io["ev_x2g"] = p.add("gpsimd", f_cc1, waits=io.get("x2bf_evs", []), inc="poolc")

    p.scope = "B3_l1attn"
    emit_l1_attn(p, kb, cst, io)
    ev_ost = None
    kb.block_boundary(p)
    p.scope = "B4_cc2"

    def f_cc2(g):
        return g.collective_compute("AllGather", ALU.bypass, replica_groups=PAIRS,
                                    ins=[o_cc_in.ap().opt()],
                                    outs=[o_cc_out.ap().opt()])
    io["ev_og"] = p.add("gpsimd", f_cc2, inc="poolc")

    p.scope = "B5_l1out"
    emit_l1_outproj(p, kb, cst, io)
    kb.block_boundary(p)
    p.scope = "B6_l1ffn"
    ev_fin = emit_l1_ffn(p, kb, cst, io)
    p.add("sync", lambda s: None, waits=[ev_fin])
    kb.realize(p)
    return nc


def fused_in_maps(inputs):
    shared = {}
    shared.update(l0attn_host_inputs(inputs))
    shared.update(l0ffn_host_inputs(inputs))
    shared.update(l1ffn_host_inputs(inputs))
    shared["a1_wo"] = np.ascontiguousarray(inputs["L1_attn_w_o"])
    cc = host_constants()
    shared["MEGAMASK"] = cc["MEGAMASK"]
    wslices = l1_head_weight_slices(inputs)
    x = np.asarray(inputs["L0_attn_x"], dtype=np.float32)
    in_maps = []
    for c in range(N_CORES):
        b, hc = c // 2, c % 2
        m = dict(shared)
        wq, wk, wv = wslices[hc]
        m["a1_wq"], m["a1_wk"], m["a1_wv"] = wq, wk, wv
        msel = np.zeros((128, 2), dtype=np.float32)
        msel[:, 0] = 1.0 - hc
        msel[:, 1] = hc
        m["msel"] = msel
        m["xT"] = np.ascontiguousarray(x[b, hc * TOK:(hc + 1) * TOK, :].T)
        in_maps.append(m)
    return in_maps


def kernel_fused(**inputs):
    nc = build_fused_nc()
    in_maps = fused_in_maps(inputs)
    res = run_bass_kernel_spmd(nc, in_maps, core_ids=list(range(N_CORES)))
    out = np.empty((B, T_FULL, D), dtype=np.float32)
    for c in range(N_CORES):
        b, hc = c // 2, c % 2
        out[b, hc * TOK:(hc + 1) * TOK, :] = res.results[c]["out_tok"]
    return out


def kernel_unfused(**inputs):
    x = np.asarray(inputs["L0_attn_x"], dtype=np.float32)
    x1 = run_l0attn(inputs, x)
    x2 = run_l0ffn(inputs, x1)
    o_full = run_l1attn(inputs, x2)
    x3 = run_l1out(inputs, x2, o_full)
    return run_l1ffn(inputs, x3)


def kernel(**inputs):
    return kernel_fused(**inputs)


# version 07
# speedup vs baseline: 1.0223x; 1.0223x over previous
import contextlib

import numpy as np
import ml_dtypes

import concourse.bass as bass
import concourse.mybir as mybir
from concourse.bass_utils import run_bass_kernel_spmd

F32 = mybir.dt.float32
BF16 = mybir.dt.bfloat16
I32 = mybir.dt.int32
AF = mybir.ActivationFunctionType
ALU = mybir.AluOpType
AX = mybir.AxisListType
BF = ml_dtypes.bfloat16

N_CORES = 8
D = 1024
B = 4
T_FULL = 4096
TOK = 2048
H = 16
DH = 64
CD = D // 128
NTB = TOK // 512
C0_L0 = 8.0
C1_L1 = 4.0
NEG_BIG = -30000.0
EPS = 1e-5

ENGS = ("tensor", "scalar", "vector", "gpsimd", "sync")


def ap_win(base2d, nwin, stride, width):
    ap = base2d.ap
    return bass.AP(base2d.tensor, base2d.offset,
                   [list(ap[0]), [stride, nwin], [1, width]])


class Prog:

    def __init__(self):
        self.tasks = {e: [] for e in ENGS}
        self.cnt = {c: 0 for c in ("pe", "act", "dve", "poolc", "spd", "poold", "scd")}
        self.n_added = 0
        self.barrier = {}
        self.scope = None

    def ev(self, ch):
        return (ch, self.cnt[ch])

    def add(self, eng, fn, waits=(), inc=None, n_dma=0):
        waits = [w for w in waits if w is not None] + list(self.barrier.items())
        ev = None
        if inc is not None:
            if n_dma:
                self.cnt[inc] += 16 * n_dma
            else:
                self.cnt[inc] += 1
            ev = (inc, self.cnt[inc])
        snap = dict(self.cnt)
        if inc is not None:
            snap[inc] = self.cnt[inc] - (16 * n_dma if n_dma else 1)
        self.tasks[eng].append((fn, tuple(waits), inc, n_dma, self.n_added, snap,
                                self.scope))
        self.n_added += 1
        return ev

    def realize(self, kb):
        waited = {}
        for ename in ENGS:
            tlist = self.tasks[ename]
            if not tlist:
                continue
            dec = getattr(kb.blk, ename)

            import os
            trunc = int(os.environ.get("KTRUNC", "1000000000"))
            serial = bool(os.environ.get("KSERIAL"))

            def body(eng, tlist=tlist, ename=ename, trunc=trunc, serial=serial):
                cur_scope = None
                cur_sid = None
                for fn, waits, inc, n_dma, idx, snap, scope in tlist:
                    if idx >= trunc:
                        continue
                    if scope != cur_scope:
                        if cur_scope is not None:
                            kb.nc.leave_named_scope(cur_scope, cur_sid, False)
                        if scope is not None:
                            cur_sid, _ = kb.nc.enter_named_scope(scope, False)
                        cur_scope = scope
                    if serial:
                        waits = tuple((ch, v) for ch, v in snap.items() if v > 0)
                    for (ch, val) in waits:
                        key = (ename, ch)
                        if waited.get(key, -1) < val:
                            eng.wait_ge(kb.sem[ch], val)
                            waited[key] = val
                    out = fn(eng)
                    if inc is not None:
                        if n_dma:
                            assert isinstance(out, (list, tuple)) and len(out) == n_dma
                            for ins in out:
                                ins.then_inc(kb.sem[inc], 16)
                        else:
                            assert out is not None
                            out.then_inc(kb.sem[inc], 1)
                if cur_scope is not None:
                    kb.nc.leave_named_scope(cur_scope, cur_sid, False)

            dec(body)


class KB:
    def __init__(self, nc, fused=False, arena_el=105472):
        self.nc = nc
        self.stack = contextlib.ExitStack()
        self.sem = {}
        for ch in ("pe", "act", "dve", "poolc", "spd", "poold", "scd",
                   "dve_i", "act_i", "pool_i"):
            self.sem[ch] = self.stack.enter_context(nc.semaphore(ch))
        self.icnt = {"dve_i": 0, "act_i": 0, "pool_i": 0}
        self.cache = {}
        self.blk = None
        self.fused = fused
        if fused:
            self.arena = self.stack.enter_context(
                nc.sbuf_tensor("ARENA", [128, arena_el], BF16))
            self.abump = 0
            self.parena = self.stack.enter_context(
                nc.psum_tensor("PSARENA", [128, 4096], F32))
            self.pbump = 0

    def block_boundary(self, prog):
        self.abump = 0
        self.pbump = 0
        prog.barrier = dict(prog.cnt)

    def edge(self, ch, eng, ins):
        ins.then_inc(self.sem[ch], 1)
        self.icnt[ch] += 1
        eng.wait_ge(self.sem[ch], self.icnt[ch])

    def sb(self, name, shape, dtype):
        if not self.fused or name.startswith("c_"):
            return self.stack.enter_context(self.nc.sbuf_tensor(name, shape, dtype))
        import math as _m
        n_el = int(np.prod(shape[1:])) * mybir.dt.size(dtype) // 2
        off = (self.abump + 31) & ~31
        self.abump = off + n_el
        assert self.abump <= self.arena.shape[1], f"arena OOM at {name}: {self.abump}"
        ap = self.arena[:, off:off + n_el]
        if dtype != BF16:
            ap = ap.bitcast(dtype)
        if len(shape) == 2:
            ap = ap[0:shape[0], :] if shape[0] != 128 else ap
        elif len(shape) == 3:
            ap = ap.rearrange("p (a b) -> p a b", a=shape[1])
            if shape[0] != 128:
                ap = ap[0:shape[0]]
        else:
            raise ValueError(shape)
        return _ArenaTensor(ap)

    def ps(self, name, shape, dtype=F32):
        if not self.fused:
            return self.stack.enter_context(self.nc.psum_tensor(name, shape, dtype))
        assert len(shape) == 2
        el = mybir.dt.size(dtype)
        nbanks = (shape[1] * el + 2047) // 2048
        off = self.pbump * 512
        self.pbump += nbanks
        assert self.pbump <= 8, f"psum OOM at {name}"
        ap = self.parena.ap()[:, off:off + nbanks * 512]
        if dtype != F32:
            ap = ap.bitcast(dtype)
        return _ArenaTensor(ap[0:shape[0], 0:shape[1]])

    def realize(self, prog):
        self.blk = self.stack.enter_context(self.nc.Block())
        prog.realize(self)
        self.stack.close()


class _ArenaTensor:

    def __init__(self, ap):
        self._ap = ap
        self.tensor = ap.tensor
        self.shape = ap.shape

    def __getitem__(self, key):
        return self._ap[key]

    def ap(self):
        return self._ap

    def rearrange(self, *a, **kw):
        return self._ap.rearrange(*a, **kw)

    def bitcast(self, dt):
        return self._ap.bitcast(dt)


def host_constants():
    mega = np.full((128, 896), NEG_BIG, dtype=np.float32)
    kk = np.arange(128)[:, None]
    uu = np.arange(896)[None, :]
    mega[kk <= uu - 384] = 0.0
    return dict(EYE_BF=np.eye(128, dtype=BF), EYE_F=np.eye(128, dtype=np.float32),
                MEGAMASK=mega)


def emit_consts(p, kb):
    ones_r = kb.sb("c_ones_r", [1, 128], BF16)
    onesf_r = kb.sb("c_onesf_r", [1, 128], F32)
    ones_c = kb.sb("c_ones_c", [128, 1], BF16)
    zero_c = kb.sb("c_zero_c", [128, 1], F32)
    eps_c = kb.sb("c_eps_c", [1, 1], F32)
    cl0_c = kb.sb("c_cl0", [128, 1], F32)
    cl1_c = kb.sb("c_cl1", [128, 1], F32)

    def f(v):
        v.memset(ones_r[:], 1.0)
        v.memset(onesf_r[:], 1.0)
        v.memset(ones_c[:], 1.0)
        v.memset(zero_c[:], 0.0)
        v.memset(eps_c[:], EPS)
        v.memset(cl0_c[:], -C0_L0)
        return v.memset(cl1_c[:], -C1_L1)

    ev = p.add("vector", f, inc="dve")
    return dict(ones_r=ones_r, onesf_r=onesf_r, ones_c=ones_c, zero_c=zero_c,
                eps_c=eps_c, cl0_c=cl0_c, cl1_c=cl1_c, ev=ev)


def emit_ln_tail(p, kb, cst, st, *, pref, Yr, Yb, Ysq, b_m, b_s, ps_sum, ps_sq,
                 outT, t0, ev_y, Xbf_out=None):
    if pref not in kb.cache:
        kb.cache[pref] = (kb.sb(f"{pref}_mean", [1, 512], F32),
                          kb.sb(f"{pref}_rstd", [1, 512], F32))
    mean_r, rstd_r = kb.cache[pref]

    def f_cast(g):
        return g.tensor_copy(Yb[:], Yr[:])
    ev_cast = p.add("gpsimd", f_cast, waits=[ev_y, st.get("Yb")], inc="poolc")

    def f_sq(sc):
        return sc.activation(Ysq[:], Yr[:], AF.Square)
    ev_sq = p.add("scalar", f_sq, waits=[ev_y, st.get("Ysq")], inc="act")

    def f_stats(t):
        for cd in range(CD):
            t.matmul(ps_sum, cst["ones_c"][:], Yb[:, cd, :],
                     start=(cd == 0), stop=(cd == CD - 1))
        for cd in range(CD):
            last = t.matmul(ps_sq, cst["ones_c"][:], Ysq[:, cd, :],
                            start=(cd == 0), stop=(cd == CD - 1))
        return last
    ev_stats = p.add("tensor", f_stats,
                     waits=[ev_cast, ev_sq, st.get("ps_row")], inc="pe")

    def f_rows(v):
        ins = v.tensor_scalar(mean_r[:], ps_sum, 1.0 / D, None, ALU.mult)
        v.tensor_scalar(rstd_r[:], ps_sq, 1.0 / D, None, ALU.mult)
        kb.edge("dve_i", v, ins)
        ins = v.tensor_tensor(ps_sum, mean_r[:], mean_r[:], ALU.mult)
        kb.edge("dve_i", v, ins)
        return v.tensor_tensor(rstd_r[:], rstd_r[:], ps_sum, ALU.subtract)
    ev_rows = p.add("vector", f_rows, waits=[ev_stats], inc="dve")
    st["ps_row"] = ev_rows

    def f_sqrt(sc):
        return sc.activation(rstd_r[:], rstd_r[:], AF.Sqrt, bias=cst["eps_c"][0:1, 0:1],
                             scale=1.0)
    ev_sqrt = p.add("scalar", f_sqrt, waits=[ev_rows], inc="act")

    def f_recip(v):
        return v.reciprocal(rstd_r[:], rstd_r[:])
    ev_recip = p.add("vector", f_recip, waits=[ev_sqrt], inc="dve")

    def f_bcast(t):
        t.matmul(b_m[:], cst["onesf_r"][:], mean_r[:], start=True, stop=True)
        return t.matmul(b_s[:], cst["onesf_r"][:], rstd_r[:], start=True, stop=True)
    ev_bcast = p.add("tensor", f_bcast,
                     waits=[ev_recip, st.get("b_m"), st.get("b_s")], inc="pe")

    def f_apply(v):
        mb = b_m[:].unsqueeze(1).broadcast_to((128, CD, 512))
        rb = b_s[:].unsqueeze(1).broadcast_to((128, CD, 512))
        ins = v.tensor_tensor(Yr[:], Yr[:], mb, ALU.subtract)
        kb.edge("dve_i", v, ins)
        out = v.tensor_tensor(Yr[:], Yr[:], rb, ALU.mult)
        if Xbf_out is not None:
            kb.edge("dve_i", v, out)
            out = v.tensor_copy(Xbf_out, Yr[:])
        return out
    ev_apply = p.add("vector", f_apply, waits=[ev_bcast], inc="dve")
    st["b_m"] = ev_apply
    st["b_s"] = ev_apply
    st["Yb"] = ev_stats
    st["Ysq"] = ev_stats

    if outT is None:
        return ev_apply, ev_apply

    def f_store(s):
        return [s.dma_start(out=outT[:, t0:t0 + 512].rearrange("(c p) n -> p c n", p=128),
                            in_=Yr[:])]
    ev_store = p.add("sync", f_store, waits=[ev_apply], inc="spd", n_dma=1)
    return ev_apply, ev_store


def emit_l0_attn(p, kb, cst, io):
    xT, x1T = io["xT"], io["x1T"]

    Wq = kb.sb("l0_Wq", [128, CD, 1024], BF16)
    Wd = kb.sb("l0_Wd", [128, CD, 256], BF16)
    Wu = kb.sb("l0_Wu", [128, 2, 2048], BF16)
    Wo = kb.sb("l0_Wo", [128, CD, 1024], BF16)
    bq_r = kb.sb("l0_bqr", [1, 1024], BF16)
    bu_r = kb.sb("l0_bur", [1, 2048], BF16)
    bd_c = kb.sb("l0_bdc", [128, 2], F32)
    bo_c = kb.sb("l0_boc", [128, CD], F32)
    eye_t = kb.sb("l0_eye", [128, 128], BF16)

    Xf = kb.sb("l0_Xf", [128, CD, 512], F32)
    Xb = kb.sb("l0_Xb", [128, CD, 512], BF16)
    KVL = kb.sb("l0_kvl", [128, 2, 512], BF16)
    Qt = [kb.sb(f"l0_qt{i}", [128, 1024], BF16) for i in range(2)]
    K2 = [kb.sb(f"l0_k2{i}", [128, 2048], BF16) for i in range(2)]
    V2 = [kb.sb(f"l0_v2{i}", [128, 2048], BF16) for i in range(2)]
    PR = kb.sb("l0_pr", [128, 8, 1024], BF16)
    PR2 = kb.sb("l0_pr2", [128, 8, 1024], BF16)
    Fold2 = kb.sb("l0_fold2", [128, 128], F32)
    Fold = kb.sb("l0_fold", [128, 128], F32)
    Sc = kb.sb("l0_s", [128, 256], F32)
    Am = kb.sb("l0_a", [128, 256], BF16)
    Lh = kb.sb("l0_l", [128, 16], F32)
    Rh = kb.sb("l0_r", [128, 16], F32)
    Ot = kb.sb("l0_ot", [128, 1024], F32)
    Ot2 = kb.sb("l0_ot2", [128, 1024], F32)
    Ob = [kb.sb(f"l0_ob{i}", [128, 1024], BF16) for i in range(2)]
    OT = kb.sb("l0_oT", [128, CD, 512], BF16)
    Yr = kb.sb("l0_y", [128, CD, 512], F32)
    Yb = kb.sb("l0_yb", [128, CD, 512], BF16)
    Ysq = kb.sb("l0_ysq", [128, CD, 512], BF16)

    bk = [kb.ps(f"l0_b{i}", [128, 512], F32) for i in range(4)]
    tp0 = kb.ps("l0_tp0", [128, 128], BF16)
    tp1 = kb.ps("l0_tp1", [128, 128], BF16)
    ps_ra = kb.ps("l0_rowa", [1, 512], F32)
    ps_rb = kb.ps("l0_rowb", [1, 512], F32)

    def f_wl(g):
        return [g.dma_start(out=Wq[:], in_=io["l0_wq"].rearrange("(c p) n -> p c n", p=128)),
                g.dma_start(out=Wd[:], in_=io["l0_wd"].rearrange("(c p) n -> p c n", p=128)),
                g.dma_start(out=Wu[:], in_=io["l0_wu"].rearrange("(c p) n -> p c n", p=128)),
                g.dma_start(out=Wo[:], in_=io["l0_wo"].rearrange("(c p) n -> p c n", p=128)),
                g.dma_start(out=bq_r[:], in_=io["l0_bq"][None, :]),
                g.dma_start(out=bu_r[:], in_=io["l0_bu"][None, :])]
    ev_wbf = p.add("gpsimd", f_wl, inc="poold", n_dma=6)

    def f_wl2(s):
        out = [s.dma_start(out=bd_c[:, c:c + 1], in_=io["l0_bd"][c * 128:(c + 1) * 128, None])
               for c in range(2)]
        out += [s.dma_start(out=bo_c[:, c:c + 1], in_=io["l0_bo"][c * 128:(c + 1) * 128, None])
                for c in range(CD)]
        out.append(s.dma_start(out=eye_t[:], in_=io["EYE_BF"]))
        return out
    ev_wsp = p.add("sync", f_wl2, inc="spd", n_dma=11)

    st = {}
    lu = {"Xf": None, "Xb": None, "OT": None, "Yr": None,
          "bk": [None] * 4, "tp": [None, None],
          "QK2": [None, None], "V2": [None, None], "Am": [],
          "Ob": [None, None]}
    pend_tp = []

    def emit_tp(item):
        ev_ob_, s0_, Ob_ = item
        ev_t = None
        for cd in range(CD):
            tp = (tp0, tp1)[cd % 2]

            def f_tp(t, cd=cd, tp=tp, ev_ob_=ev_ob_, Ob_=Ob_):
                return t.transpose(tp[:], Ob_[:, cd * 128:(cd + 1) * 128], eye_t[:])
            ev_t = p.add("tensor", f_tp,
                         waits=[ev_ob_, ev_wsp, lu["tp"][cd % 2]], inc="pe")

            def f_tpe(sc, cd=cd, tp=tp, s0_=s0_):
                return sc.activation(OT[:, cd, s0_:s0_ + 128], tp[:], AF.Identity,
                                     bias=cst["zero_c"][:, 0:1], scale=1.0)
            ev_te = p.add("scalar", f_tpe, waits=[ev_t, lu["OT"]], inc="act")
            lu["tp"][cd % 2] = ev_te
        lu["Ob"][0 if Ob_ is Ob[0] else 1] = ev_t
        return ev_te
    ev_store = None

    for tb in range(NTB):
        t0 = tb * 512

        def f_lx(s, t0=t0):
            return [s.dma_start(out=Xf[:], in_=xT[:, t0:t0 + 512]
                                .rearrange("(c p) n -> p c n", p=128))]
        ev_xf = p.add("sync", f_lx, waits=[lu["Xf"]], inc="spd", n_dma=1)

        def f_lxb(g, t0=t0):
            return [g.dma_start(out=Xb[:], in_=xT[:, t0:t0 + 512]
                                .rearrange("(c p) n -> p c n", p=128))]
        ev_xb = p.add("gpsimd", f_lxb, waits=[lu["Xb"]], inc="poold", n_dma=1)

        def f_kvl(t):
            last = None
            for cl in range(2):
                for cd in range(CD):
                    last = t.matmul(bk[cl][:], Wd[:, cd, cl * 128:(cl + 1) * 128],
                                    Xb[:, cd, :], start=(cd == 0), stop=(cd == CD - 1))
            return last
        ev_kvl_mm = p.add("tensor", f_kvl,
                          waits=[ev_wbf, ev_wsp, ev_xb, lu["bk"][0], lu["bk"][1]],
                          inc="pe")

        def f_kvl_ev(sc):
            sc.activation(KVL[:, 0, :], bk[0][:], AF.Identity, bias=bd_c[:, 0:1], scale=1.0)
            return sc.activation(KVL[:, 1, :], bk[1][:], AF.Identity,
                                 bias=bd_c[:, 1:2], scale=1.0)
        ev_kvl = p.add("scalar", f_kvl_ev, waits=[ev_kvl_mm], inc="act")
        lu["bk"][0] = lu["bk"][1] = ev_kvl

        ev_ot_tiles = []
        for tt4 in range(4):
            s0 = tt4 * 128
            tp_ = (tb * 4 + tt4) % 2
            Qt_, K2_, V2_ = Qt[tp_], K2[tp_], V2[tp_]

            def f_qk(t, s0=s0):
                for o in range(2):
                    for cd in range(CD):
                        t.matmul(bk[o][:], Xb[:, cd, s0:s0 + 128],
                                 Wq[:, cd, o * 512:(o + 1) * 512],
                                 start=(cd == 0), stop=False)
                    t.matmul(bk[o][:], cst["ones_r"][:], bq_r[0:1, o * 512:(o + 1) * 512],
                             start=False, stop=True)
                for o in range(2):
                    for cl in range(2):
                        t.matmul(bk[2 + o][:], KVL[:, cl, s0:s0 + 128],
                                 Wu[:, cl, o * 512:(o + 1) * 512],
                                 start=(cl == 0), stop=False)
                    last = t.matmul(bk[2 + o][:], cst["ones_r"][:],
                                    bu_r[0:1, o * 512:(o + 1) * 512], start=False, stop=True)
                return last
            ev_qk_mm = p.add("tensor", f_qk,
                             waits=[ev_kvl, cst["ev"]] + lu["bk"], inc="pe")

            def f_qk_ev(sc, Qt_=Qt_, K2_=K2_):
                sc.activation(Qt_[:, 0:512], bk[0][:], AF.Identity,
                              bias=cst["zero_c"][:, 0:1], scale=1.0)
                sc.activation(Qt_[:, 512:1024], bk[1][:], AF.Identity,
                              bias=cst["zero_c"][:, 0:1], scale=1.0)
                sc.activation(K2_[:, 0:512], bk[2][:], AF.Identity,
                              bias=cst["zero_c"][:, 0:1], scale=1.0)
                sc.activation(K2_[:, 1024:1536], bk[2][:], AF.Identity,
                              bias=cst["zero_c"][:, 0:1], scale=1.0)
                sc.activation(K2_[:, 512:1024], bk[3][:], AF.Identity,
                              bias=cst["zero_c"][:, 0:1], scale=1.0)
                return sc.activation(K2_[:, 1536:2048], bk[3][:], AF.Identity,
                                     bias=cst["zero_c"][:, 0:1], scale=1.0)
            ev_qk = p.add("scalar", f_qk_ev, waits=[ev_qk_mm] + lu["QK2"][tp_] if isinstance(lu["QK2"][tp_], list) else [ev_qk_mm, lu["QK2"][tp_]], inc="act")
            for i in range(4):
                lu["bk"][i] = ev_qk

            def f_vmm(t, s0=s0):
                for o in range(2):
                    for cl in range(2):
                        t.matmul(bk[o][:], KVL[:, cl, s0:s0 + 128],
                                 Wu[:, cl, (2 + o) * 512:(3 + o) * 512],
                                 start=(cl == 0), stop=False)
                    last = t.matmul(bk[o][:], cst["ones_r"][:],
                                    bu_r[0:1, (2 + o) * 512:(3 + o) * 512],
                                    start=False, stop=True)
                return last
            ev_v_mm = p.add("tensor", f_vmm, waits=[ev_qk], inc="pe")

            def f_v_ev(sc, V2_=V2_):
                sc.activation(V2_[:, 0:512], bk[0][:], AF.Identity,
                              bias=cst["zero_c"][:, 0:1], scale=1.0)
                sc.activation(V2_[:, 1024:1536], bk[0][:], AF.Identity,
                              bias=cst["zero_c"][:, 0:1], scale=1.0)
                sc.activation(V2_[:, 512:1024], bk[1][:], AF.Identity,
                              bias=cst["zero_c"][:, 0:1], scale=1.0)
                return sc.activation(V2_[:, 1536:2048], bk[1][:], AF.Identity,
                                     bias=cst["zero_c"][:, 0:1], scale=1.0)
            ev_v = p.add("scalar", f_v_ev, waits=[ev_v_mm] + (lu["V2"][tp_] if isinstance(lu["V2"][tp_], list) else [lu["V2"][tp_]]), inc="act")
            lu["bk"][0] = lu["bk"][1] = ev_v

            if tb == 0 and tt4 == 0 and "d_qt" in io:
                def f_dbg1(s):
                    return [s.dma_start(out=io["d_kvl"], in_=KVL[:]),
                            s.dma_start(out=io["d_qt"], in_=Qt[:]),
                            s.dma_start(out=io["d_k2"], in_=K2[:]),
                            s.dma_start(out=io["d_v2"], in_=V2[:])]
                p.add("sync", f_dbg1, waits=[ev_qk, ev_v, ev_kvl], inc="spd", n_dma=4)
            ev_s_both = []
            for half in range(2):
                p0 = half * 8
                eng, ch, ich = ("vector", "dve", "dve_i")
                PRb = PR if half == 0 else PR2

                def f_s(v, p0=p0, ich=ich, PRb=PRb, Qt_=Qt_, K2_=K2_):
                    qv = Qt_[:].unsqueeze(1).broadcast_to((128, 8, 1024))
                    kv = ap_win(K2_[:, p0 * 64:], 8, 64, 1024)
                    ins = v.tensor_tensor(PRb[:], qv, kv, ALU.mult)
                    prv = PRb[:].rearrange("p a (h d) -> p a h d", d=64)
                    w = 64
                    while w > 2:
                        kb.edge(ich, v, ins)
                        ins = v.tensor_tensor(prv[:, :, :, 0:w // 2], prv[:, :, :, 0:w // 2],
                                              prv[:, :, :, w // 2:w], ALU.add)
                        w //= 2
                    kb.edge(ich, v, ins)
                    sv = Sc[:, p0 * 16:(p0 + 8) * 16].rearrange("p (a h) -> p a h", h=16)
                    return v.tensor_tensor(sv, prv[:, :, :, 0:1].squeeze(3),
                                           prv[:, :, :, 1:2].squeeze(3), ALU.add)
                ev_s_both.append(p.add(eng, f_s, waits=[ev_qk, ev_v], inc=ch))
            ev_s = ev_s_both[0]
            lu["QK2"][tp_] = list(ev_s_both)

            if tb == 0 and tt4 == 0 and "d_sc" in io:
                def f_dbg2(s):
                    return [s.dma_start(out=io["d_sc"], in_=Sc[:])]
                p.add("sync", f_dbg2, waits=[ev_s], inc="spd", n_dma=1)

            def f_exp(sc):
                return sc.activation(Am[:], Sc[:], AF.Exp, bias=cst["cl0_c"][:, 0:1],
                                     scale=0.125)
            ev_am = p.add("scalar", f_exp, waits=ev_s_both + lu["Am"], inc="act")

            def f_l(v):
                ins = v.tensor_tensor(Fold[:], Am[:, 0:128], Am[:, 128:256], ALU.add)
                kb.edge("dve_i", v, ins)
                ins = v.tensor_tensor(Fold[:, 0:64], Fold[:, 0:64], Fold[:, 64:128], ALU.add)
                kb.edge("dve_i", v, ins)
                ins = v.tensor_tensor(Fold[:, 0:32], Fold[:, 0:32], Fold[:, 32:64], ALU.add)
                kb.edge("dve_i", v, ins)
                ins = v.tensor_tensor(Lh[:], Fold[:, 0:16], Fold[:, 16:32], ALU.add)
                kb.edge("dve_i", v, ins)
                return v.reciprocal(Rh[:], Lh[:])
            ev_r = p.add("vector", f_l, waits=[ev_am], inc="dve")

            if tb == 0 and tt4 == 0 and "d_am" in io:
                def f_dbg3(s):
                    return [s.dma_start(out=io["d_am"], in_=Am[:]),
                            s.dma_start(out=io["d_lh"], in_=Lh[:])]
                p.add("sync", f_dbg3, waits=[ev_am, ev_r], inc="spd", n_dma=2)
            ev_av_both = []
            for half in range(2):
                p0 = half * 8
                eng, ch, ich = ("vector", "dve", "dve_i")
                PRb = PR if half == 0 else PR2

                def f_av(v, p0=p0, half=half, ich=ich, PRb=PRb, V2_=V2_):
                    av = (Am[:, p0 * 16:(p0 + 8) * 16]
                          .rearrange("p (a h) -> p a h", h=16)
                          .unsqueeze(3).broadcast_to((128, 8, 16, 64)))
                    vv = ap_win(V2_[:, p0 * 64:], 8, 64, 1024).rearrange(
                        "p a (h d) -> p a h d", d=64)
                    pr4 = PRb[:].rearrange("p a (h d) -> p a h d", d=64)
                    ins = v.tensor_tensor(pr4, av, vv, ALU.mult)
                    kb.edge(ich, v, ins)
                    ins = v.tensor_tensor(PRb[:, 0:4, :], PRb[:, 0:4, :], PRb[:, 4:8, :],
                                          ALU.add)
                    kb.edge(ich, v, ins)
                    ins = v.tensor_tensor(PRb[:, 0:2, :], PRb[:, 0:2, :], PRb[:, 2:4, :],
                                          ALU.add)
                    kb.edge(ich, v, ins)
                    dst = Ot if half == 0 else Ot2
                    return v.tensor_tensor(dst[:], PRb[:, 0, :], PRb[:, 1, :], ALU.add)
                ev_av_both.append(p.add(eng, f_av, waits=[ev_r], inc=ch))
            ev_av = ev_av_both[0]
            lu["V2"][tp_] = list(ev_av_both)
            lu["Am"] = list(ev_av_both)

            Ob_ = Ob[tp_]

            def f_onorm(v, Ob_=Ob_):
                ins = v.tensor_tensor(Ot[:], Ot[:], Ot2[:], ALU.add)
                kb.edge("dve_i", v, ins)
                ot3 = Ot[:].rearrange("p (h d) -> p h d", d=64)
                ob3 = Ob_[:].rearrange("p (h d) -> p h d", d=64)
                rb = Rh[:].unsqueeze(2).broadcast_to((128, 16, 64))
                return v.tensor_tensor(ob3, ot3, rb, ALU.mult)
            ev_ob = p.add("vector", f_onorm,
                          waits=ev_av_both + [lu["Ob"][tp_]], inc="dve")
            pend_tp.append((ev_ob, s0, Ob_))
            if len(pend_tp) > 1:
                emit_tp(pend_tp.pop(0))

        while pend_tp:
            emit_tp(pend_tp.pop(0))
        ev_ot_tiles = [lu["tp"][0], lu["tp"][1]]
        ev_y = None
        for wave in range(2):
            def f_op(t, wave=wave):
                last = None
                for i in range(4):
                    co = wave * 4 + i
                    for cd in range(CD):
                        last = t.matmul(bk[i][:], Wo[:, cd, co * 128:(co + 1) * 128],
                                        OT[:, cd, :], start=(cd == 0), stop=(cd == CD - 1))
                return last
            ev_op = p.add("tensor", f_op,
                          waits=ev_ot_tiles + lu["bk"] + [ev_xf, ev_wsp], inc="pe")

            def f_op_ev(v, wave=wave):
                last = None
                for i in range(4):
                    co = wave * 4 + i
                    ins = v.tensor_scalar(Yr[:, co, :], bk[i][:], bo_c[:, co:co + 1], None,
                                          ALU.add)
                    kb.edge("dve_i", v, ins)
                    last = v.tensor_tensor(Yr[:, co, :], Yr[:, co, :], Xf[:, co, :],
                                           ALU.add)
                return last
            ev_y = p.add("vector", f_op_ev, waits=[ev_op, lu["Yr"]], inc="dve")
            for i in range(4):
                lu["bk"][i] = ev_y
        lu["Xf"] = ev_y
        lu["Xb"] = ev_s
        lu["OT"] = ev_y
        lu["Am"] = lu["Am"]

        if tb == 0 and "d_y" in io:
            def f_dbg5(s):
                return [s.dma_start(out=io["d_ot8"], in_=OT[:]),
                        s.dma_start(out=io["d_y"], in_=Yr[:])]
            p.add("sync", f_dbg5, waits=[ev_y], inc="spd", n_dma=2)
        ev_apply, ev_store = emit_ln_tail(
            p, kb, cst, st, pref="l0ln", Yr=Yr, Yb=Yb, Ysq=Ysq,
            b_m=bk[0], b_s=bk[1], ps_sum=ps_ra[0:1, :],
            ps_sq=ps_rb[0:1, :], outT=x1T, t0=t0, ev_y=ev_y)
        lu["Yr"] = ev_apply
        lu["bk"][0] = st["b_m"]
        lu["bk"][1] = st["b_s"]
    return ev_store


def emit_l0_ffn(p, kb, cst, io):
    x1T, x2T = io["x1T_in"], io["x2T"]
    NCH = 24
    SC127 = 1.0 / 127.0

    Wd = kb.sb("f0_Wd", [128, NCH, 1024], BF16)
    Wg = [kb.sb(f"f0_Wg{i}", [128, CD, 128], BF16) for i in range(2)]
    Wu = [kb.sb(f"f0_Wu{i}", [128, CD, 128], BF16) for i in range(2)]
    Xf = kb.sb("f0_Xf", [128, CD, 512], F32)
    Xb = kb.sb("f0_Xb", [128, CD, 512], BF16)
    Hh = kb.sb("f0_h", [128, NCH, 512], BF16)
    Sg = [kb.sb(f"f0_sg{i}", [128, 512], BF16) for i in range(2)]
    Yr = kb.sb("f0_y", [128, CD, 512], F32)
    Yb = kb.sb("f0_yb", [128, CD, 512], BF16)
    Ysq = kb.sb("f0_ysq", [128, CD, 512], BF16)

    bk = [kb.ps(f"f0_b{i}", [128, 512], F32) for i in range(4)]
    ps_ra = kb.ps("f0_rowa", [1, 512], F32)
    ps_rb = kb.ps("f0_rowb", [1, 512], F32)

    def f_wd(g):
        return [g.dma_start(out=Wd[:, c * 6:(c + 1) * 6, :],
                            in_=io["f0_wd"][:, :].rearrange("(c p) n -> p c n", p=128)
                            [:, c * 6:(c + 1) * 6, :])
                for c in range(4)]
    ev_wd = p.add("gpsimd", f_wd, inc="poold", n_dma=4)

    st = {}
    lu = {"Xf": None, "Xb": None, "Yr": None, "bk": [None] * 4,
          "Wg": [None, None], "Wu": [None, None], "Sg": [None, None], "Hh": None}
    ev_store = None
    ev_wbuf = {}

    for tb in range(NTB):
        t0 = tb * 512

        def f_lx(s, t0=t0):
            return [s.dma_start(out=Xf[:], in_=x1T[:, t0:t0 + 512]
                                .rearrange("(c p) n -> p c n", p=128))]
        ev_xf = p.add("sync", f_lx, waits=[lu["Xf"]], inc="spd", n_dma=1)

        def f_lxb(g, t0=t0):
            return [g.dma_start(out=Xb[:], in_=x1T[:, t0:t0 + 512]
                                .rearrange("(c p) n -> p c n", p=128))]
        ev_xb = p.add("gpsimd", f_lxb, waits=[lu["Xb"]], inc="poold", n_dma=1)

        for ch in range(NCH):
            par = ch % 2

            def f_wld(g, ch=ch, par=par):
                return [g.dma_start(out=Wg[par][:], in_=io["f0_wg"][ch]),
                        g.dma_start(out=Wu[par][:], in_=io["f0_wu"][ch])]
            ev_w = p.add("gpsimd", f_wld,
                         waits=[lu["Wg"][par], lu["Wu"][par]], inc="poold", n_dma=2)
            ev_wbuf[par] = ev_w

            b0, b1 = 2 * par, 2 * par + 1

            def f_mm(t, par=par, b0=b0, b1=b1):
                for cd in range(CD):
                    t.matmul(bk[b0][:], Wg[par][:, cd, :], Xb[:, cd, :],
                             start=(cd == 0), stop=(cd == CD - 1))
                last = None
                for cd in range(CD):
                    last = t.matmul(bk[b1][:], Wu[par][:, cd, :], Xb[:, cd, :],
                                    start=(cd == 0), stop=(cd == CD - 1))
                return last
            ev_mm = p.add("tensor", f_mm,
                          waits=[ev_w, ev_xb, lu["bk"][b0], lu["bk"][b1]], inc="pe")
            lu["Wg"][par] = ev_mm
            lu["Wu"][par] = ev_mm

            def f_silu(sc, par=par, b0=b0):
                return sc.activation(Sg[par][:], bk[b0][:], AF.Silu, scale=SC127)
            ev_silu = p.add("scalar", f_silu, waits=[ev_mm, lu["Sg"][par]], inc="act")
            lu["bk"][b0] = ev_silu

            def f_hmul(v, ch=ch, par=par, b1=b1):
                return v.scalar_tensor_tensor(Hh[:, ch, :], bk[b1][:], SC127, Sg[par][:],
                                              ALU.mult, ALU.mult)
            ev_h = p.add("vector", f_hmul, waits=[ev_silu, lu["Hh"]], inc="dve")
            lu["bk"][b1] = ev_h
            lu["Sg"][par] = ev_h
        lu["Xb"] = ev_mm if False else lu["Wg"][par]
        lu["Xb"] = ev_h
        ev_hdone = ev_h

        ev_y = None
        for wave in range(2):
            def f_dp(t, wave=wave):
                last = None
                for i in range(4):
                    co = wave * 4 + i
                    for ch in range(NCH):
                        last = t.matmul(bk[i][:], Wd[:, ch, co * 128:(co + 1) * 128],
                                        Hh[:, ch, :], start=(ch == 0),
                                        stop=(ch == NCH - 1))
                return last
            ev_dp = p.add("tensor", f_dp,
                          waits=[ev_hdone, ev_wd, ev_xf] + lu["bk"], inc="pe")

            def f_dp_ev(v, wave=wave):
                last = None
                for i in range(4):
                    co = wave * 4 + i
                    ins = v.tensor_scalar(Yr[:, co, :], bk[i][:], SC127, None, ALU.mult)
                    kb.edge("dve_i", v, ins)
                    last = v.tensor_tensor(Yr[:, co, :], Yr[:, co, :], Xf[:, co, :],
                                           ALU.add)
                return last
            ev_y = p.add("vector", f_dp_ev, waits=[ev_dp, lu["Yr"]], inc="dve")
            for i in range(4):
                lu["bk"][i] = ev_y
        lu["Xf"] = ev_y
        lu["Hh"] = ev_dp

        ev_apply, ev_store = emit_ln_tail(
            p, kb, cst, st, pref="f0ln", Yr=Yr, Yb=Yb, Ysq=Ysq,
            b_m=bk[0], b_s=bk[1], ps_sum=ps_ra[0:1, :],
            ps_sq=ps_rb[0:1, :], outT=x2T, t0=t0, ev_y=ev_y)
        if "x2bf_cc_in" in io:
            def f_bfst(g, t0=t0, ev_apply=ev_apply):
                return [g.dma_start(out=io["x2bf_cc_in"][:, t0:t0 + 512]
                                    .rearrange("(c p) n -> p c n", p=128), in_=Yr[:])]
            ev_store = p.add("gpsimd", f_bfst, waits=[ev_apply, ev_store],
                             inc="poold", n_dma=1)
            io.setdefault("x2bf_evs", []).append(ev_store)
        lu["Yr"] = ev_apply
        lu["bk"][0] = st["b_m"]
        lu["bk"][1] = st["b_s"]
    return ev_store


def emit_l1_attn(p, kb, cst, io):
    x2T = io.get("x2T_in")
    x2G = io.get("x2bf_cc_out")
    ev_g = io.get("ev_x2g")
    o_out = io["o_out"]
    NB = T_FULL // 512
    NKT = T_FULL // 128
    NCC = 4

    Wq = kb.sb("a1_Wq", [128, CD, 512], BF16)
    Wk = kb.sb("a1_Wk", [128, CD, 512], BF16)
    Wv = kb.sb("a1_Wv", [128, CD, 512], BF16)
    Xb = [kb.sb(f"a1_Xb{i}", [128, CD, 512], BF16) for i in range(2)]
    QT = kb.sb("a1_QT", [128, NCC, T_FULL], BF16)
    KT = kb.sb("a1_KT", [128, NCC, T_FULL], BF16)
    Vt = kb.sb("a1_V", [128, NKT, 8 * 65], BF16)
    Pt = [kb.sb(f"a1_P{i}", [128, 1024], BF16) for i in range(2)]
    Fsb = [kb.sb(f"a1_F{i}", [64, 1024], F32) for i in range(2)]
    frow = [kb.sb(f"a1_frow{i}", [1, 1024], F32) for i in range(2)]
    mask = kb.sb("a1_mask", [128, 896], F32)
    OT = kb.sb("a1_OT", [128, NCC, T_FULL], BF16)

    SP = [kb.ps(f"a1_s{i}", [128, 1024], F32) for i in range(2)]
    O = [kb.ps(f"a1_o{i}", [128, 512], F32) for i in range(4)]
    Sv = [SP[0][:, 0:512], SP[0][:, 512:1024], SP[1][:, 0:512], SP[1][:, 512:1024]]

    def f_wl(g):
        return [g.dma_start(out=Wq[:], in_=io["a1_wq"].rearrange("(c p) n -> p c n", p=128)),
                g.dma_start(out=Wk[:], in_=io["a1_wk"].rearrange("(c p) n -> p c n", p=128)),
                g.dma_start(out=Wv[:], in_=io["a1_wv"].rearrange("(c p) n -> p c n", p=128))]
    ev_w = p.add("gpsimd", f_wl, inc="poold", n_dma=3)

    def f_ml(s):
        return [s.dma_start(out=mask[:], in_=io["MEGAMASK"])]
    ev_mask = p.add("sync", f_ml, inc="spd", n_dma=1)

    def f_ones(v):
        return v.memset(Vt[:].rearrange("p k (h e) -> p k h e", e=65)[:, :, :, 64:65], 1.0)
    ev_vones = p.add("vector", f_ones, inc="dve")

    lu = {"Xb": [None, None], "S": [None] * 4, "O": [None] * 4, "QT": None,
          "KT": None, "Vt": ev_vones, "Pt": [None] * 2, "Fsb": None, "frow": None,
          "OT": None}
    for blk in range(NB):
        par = blk % 2
        t0 = blk * 512

        if x2G is None:
            def f_lx(g, t0=t0, par=par):
                return [g.dma_start(out=Xb[par][:], in_=x2T[:, t0:t0 + 512]
                                    .rearrange("(c p) n -> p c n", p=128))]
        else:
            half, tl = t0 // TOK, t0 % TOK
            def f_lx(g, half=half, tl=tl, par=par):
                return [g.dma_start(out=Xb[par][:],
                                    in_=x2G[half * D:(half + 1) * D, tl:tl + 512]
                                    .rearrange("(c p) n -> p c n", p=128))]
        ev_x = p.add("gpsimd", f_lx, waits=[lu["Xb"][par], ev_g], inc="poold", n_dma=1)

        def f_qmm(t, par=par):
            last = None
            for o in range(4):
                for cd in range(CD):
                    last = t.matmul(Sv[o], Wq[:, cd, o * 128:(o + 1) * 128],
                                    Xb[par][:, cd, :],
                                    start=(cd == 0), stop=(cd == CD - 1))
            return last
        ev_qmm = p.add("tensor", f_qmm, waits=[ev_w, ev_x] + lu["S"], inc="pe")

        def f_qev(sc, t0=t0):
            last = None
            for o in range(4):
                last = sc.activation(QT[:, o, t0:t0 + 512], Sv[o], AF.Identity,
                                     bias=cst["zero_c"][:, 0:1], scale=1.0)
            return last
        ev_qev = p.add("scalar", f_qev, waits=[ev_qmm, lu["QT"]], inc="act")
        for i in range(4):
            lu["S"][i] = ev_qev

        def f_kmm(t, par=par):
            last = None
            for o in range(4):
                for cd in range(CD):
                    last = t.matmul(Sv[o], Wk[:, cd, o * 128:(o + 1) * 128],
                                    Xb[par][:, cd, :],
                                    start=(cd == 0), stop=(cd == CD - 1))
            return last
        ev_kmm = p.add("tensor", f_kmm, waits=[ev_qev], inc="pe")

        def f_kev(sc, t0=t0):
            last = None
            for o in range(4):
                last = sc.activation(KT[:, o, t0:t0 + 512], Sv[o], AF.Identity,
                                     bias=cst["zero_c"][:, 0:1], scale=1.0)
            return last
        ev_kev = p.add("scalar", f_kev, waits=[ev_kmm, lu["KT"]], inc="act")
        for i in range(4):
            lu["S"][i] = ev_kev

        ev_vev = None
        for tt4 in range(4):
            kt = blk * 4 + tt4
            s0 = tt4 * 128

            def f_vmm(t, par=par, s0=s0, tt4=tt4):
                last = None
                for cd in range(CD):
                    last = t.matmul(O[tt4 % 4][:], Xb[par][:, cd, s0:s0 + 128],
                                    Wv[:, cd, :], start=(cd == 0), stop=(cd == CD - 1))
                return last
            ev_vmm = p.add("tensor", f_vmm,
                           waits=[ev_kev, lu["O"][tt4 % 4]], inc="pe")

            def f_vev(sc, kt=kt, tt4=tt4):
                vv = Vt[:].rearrange("p k (h e) -> p k h e", e=65)
                return sc.activation(vv[:, kt, :, 0:64], O[tt4 % 4][:]
                                     .rearrange("p (h d) -> p h d", d=64),
                                     AF.Identity, bias=cst["zero_c"][:, 0:1], scale=1.0)
            ev_vev = p.add("scalar", f_vev, waits=[ev_vmm, lu["Vt"]], inc="act")
            lu["O"][tt4 % 4] = ev_vev
        lu["Vt"] = ev_vev
        lu["Xb"][par] = ev_vev

    ev_nm = None
    for cc in range(NCC):
        for qb in range(NB):
            q0 = qb * 512
            kts = list(range(4 * qb, 4 * qb + 4)) + list(range(0, 4 * qb))
            ob = (qb % 2) * 2
            ev_exp_of = {}
            pending = None
            for ki, kt in enumerate(kts + [None]):
                if kt is not None:
                    par = ki % 2
                    diag = kt - 4 * qb

                    def f_smm(t, cc=cc, q0=q0, kt=kt, par=par):
                        t.matmul(SP[par][:, 0:512],
                                 KT[0:64, cc, kt * 128:(kt + 1) * 128],
                                 QT[0:64, cc, q0:q0 + 512], start=True, stop=True,
                                 tile_position=(0, 0))
                        return t.matmul(SP[par][:, 512:1024],
                                        KT[64:128, cc, kt * 128:(kt + 1) * 128],
                                        QT[64:128, cc, q0:q0 + 512], start=True,
                                        stop=True, tile_position=(64, 0))
                    ev_smm = p.add("tensor", f_smm,
                                   waits=[lu["S"][par * 2], lu["S"][par * 2 + 1],
                                          ev_kev, ev_mask], inc="pe")

                    ev_pre = ev_smm
                    if diag >= 0:
                        off = 384 - 128 * diag

                        def f_mask(v, par=par, off=off):
                            sp3 = SP[par].rearrange("p (h n) -> p h n", h=2)
                            mk = mask[:, off:off + 512].unsqueeze(1) \
                                .broadcast_to((128, 2, 512))
                            return v.tensor_tensor(sp3, sp3, mk, ALU.add)
                        ev_pre = p.add("vector", f_mask, waits=[ev_smm], inc="dve")

                    def f_exp(sc, par=par):
                        return sc.activation(Pt[par][:], SP[par][:], AF.Exp,
                                             bias=cst["cl1_c"][:, 0:1], scale=1.0 / 32.0)
                    ev_exp = p.add("scalar", f_exp,
                                   waits=[ev_pre, lu["Pt"][par]], inc="act")
                    ev_exp_of[kt] = ev_exp
                    lu["S"][par * 2] = ev_exp
                    lu["S"][par * 2 + 1] = ev_exp

                if pending is not None:
                    pk, ppar = pending
                    first, lastk = (pk == kts[0]), (pk == kts[-1])

                    def f_av(t, cc=cc, pk=pk, ppar=ppar, first=first, lastk=lastk, ob=ob):
                        vv = Vt[:, pk, cc * 130:cc * 130 + 130]
                        t.matmul(O[ob][0:65, :], vv[:, 0:65], Pt[ppar][:, 0:512],
                                 start=first, stop=lastk)
                        return t.matmul(O[ob + 1][0:65, :], vv[:, 65:130],
                                        Pt[ppar][:, 512:1024], start=first, stop=lastk)
                    w_extra = [lu["O"][ob], lu["O"][ob + 1]] if first else []
                    ev_av = p.add("tensor", f_av,
                                  waits=[ev_exp_of[pk]] + w_extra, inc="pe")
                    lu["Pt"][ppar] = ev_av
                pending = (kt, ki % 2) if kt is not None else None
            ev_avl = ev_av

            fpar = (cc * NB + qb) % 2

            def f_recip(v, ob=ob, fpar=fpar):
                ins = v.reciprocal(frow[fpar][0:1, 0:512], O[ob][64:65, :])
                kb.edge("dve_i", v, ins)
                return v.reciprocal(frow[fpar][0:1, 512:1024], O[ob + 1][64:65, :])
            ev_rc = p.add("vector", f_recip, waits=[ev_avl, lu["frow"]], inc="dve")

            def f_fb(t, fpar=fpar):
                t.matmul(SP[0][0:64, 0:512], cst["onesf_r"][0:1, 0:64],
                         frow[fpar][0:1, 0:512], start=True, stop=True)
                return t.matmul(SP[0][0:64, 512:1024], cst["onesf_r"][0:1, 0:64],
                                frow[fpar][0:1, 512:1024], start=True, stop=True)
            ev_fb = p.add("tensor", f_fb,
                          waits=[ev_rc, lu["S"][0], lu["S"][1]], inc="pe")

            def f_fcp(sc, fpar=fpar):
                return sc.activation(Fsb[fpar][:], SP[0][0:64, :], AF.Identity,
                                     bias=cst["zero_c"][0:64, 0:1], scale=1.0)
            ev_fcp = p.add("scalar", f_fcp, waits=[ev_fb, lu["Fsb"]], inc="act")
            lu["S"][0] = ev_fcp
            lu["S"][1] = ev_fcp
            lu["frow"] = ev_fcp

            def f_norm(v, cc=cc, q0=q0, ob=ob, fpar=fpar):
                ins = v.tensor_tensor(OT[0:64, cc, q0:q0 + 512], O[ob][0:64, :],
                                      Fsb[fpar][:, 0:512], ALU.mult)
                kb.edge("dve_i", v, ins)
                return v.tensor_tensor(OT[64:128, cc, q0:q0 + 512], O[ob + 1][0:64, :],
                                       Fsb[fpar][:, 512:1024], ALU.mult)
            ev_nm = p.add("vector", f_norm, waits=[ev_fcp], inc="dve")
            lu["O"][ob] = ev_nm
            lu["O"][ob + 1] = ev_nm
            lu["Fsb"] = ev_nm
            lu["OT"] = ev_nm

    def f_store(s):
        return [s.dma_start(out=o_out.rearrange("(c p) n -> p c n", p=128), in_=OT[:])]
    ev_st = p.add("sync", f_store, waits=[lu["OT"]], inc="spd", n_dma=1)
    return ev_st


def emit_l1_outproj(p, kb, cst, io):
    oT, x2T, x3T = io.get("oT_own"), io["x2T_own"], io["x3T"]
    oG = io.get("o_cc_out")
    ev_og = io.get("ev_og")

    Wo = kb.sb("a2_Wo", [128, CD, 1024], BF16)
    OTb = kb.sb("a2_OT", [128, CD, 512], BF16)
    if oG is not None:
        OT2 = kb.sb("a2_OT2", [128, CD, 512], BF16)
        Msel = kb.sb("a2_msel", [128, 2], F32)
    Xf = kb.sb("a2_Xf", [128, CD, 512], F32)
    Yr = kb.sb("a2_y", [128, CD, 512], F32)
    Yb = kb.sb("a2_yb", [128, CD, 512], BF16)
    Ysq = kb.sb("a2_ysq", [128, CD, 512], BF16)
    bk = [kb.ps(f"a2_b{i}", [128, 512], F32) for i in range(4)]
    ps_ra = kb.ps("a2_rowa", [1, 512], F32)
    ps_rb = kb.ps("a2_rowb", [1, 512], F32)

    def f_wl(g):
        out = [g.dma_start(out=Wo[:], in_=io["a1_wo"].rearrange("(c p) n -> p c n", p=128))]
        if oG is not None:
            out.append(g.dma_start(out=Msel[:], in_=io["msel"]))
        return out
    ev_w = p.add("gpsimd", f_wl, inc="poold", n_dma=2 if oG is not None else 1)

    st = {}
    lu = {"Xf": None, "OTb": None, "Yr": None, "bk": [None] * 4}
    ev_store = None
    for tb in range(NTB):
        t0 = tb * 512

        def f_lx(s, t0=t0):
            return [s.dma_start(out=Xf[:], in_=x2T[:, t0:t0 + 512]
                                .rearrange("(c p) n -> p c n", p=128))]
        ev_xf = p.add("sync", f_lx, waits=[lu["Xf"]], inc="spd", n_dma=1)

        if oG is None:
            def f_lo(g, t0=t0):
                return [g.dma_start(out=OTb[:], in_=oT[:, t0:t0 + 512]
                                    .rearrange("(c p) n -> p c n", p=128))]
            ev_o = p.add("gpsimd", f_lo, waits=[lu["OTb"]], inc="poold", n_dma=1)
        else:
            def f_lo(g, t0=t0):
                return [g.dma_start(out=OTb[:], in_=oG[:, t0:t0 + 512]
                                    .rearrange("(c p) n -> p c n", p=128)),
                        g.dma_start(out=OT2[:], in_=oG[:, TOK + t0:TOK + t0 + 512]
                                    .rearrange("(c p) n -> p c n", p=128))]
            ev_o2 = p.add("gpsimd", f_lo, waits=[lu["OTb"], ev_og], inc="poold", n_dma=2)

            def f_sel(v):
                ins = v.tensor_scalar(OTb[:], OTb[:], Msel[:, 0:1], None, ALU.mult)
                kb.edge("dve_i", v, ins)
                return v.scalar_tensor_tensor(OTb[:], OT2[:], Msel[:, 1:2], OTb[:],
                                              ALU.mult, ALU.add)
            ev_o = p.add("vector", f_sel, waits=[ev_o2, ev_w], inc="dve")

        ev_y = None
        for wave in range(2):
            def f_op(t, wave=wave):
                last = None
                for i in range(4):
                    co = wave * 4 + i
                    for cd in range(CD):
                        last = t.matmul(bk[i][:], Wo[:, cd, co * 128:(co + 1) * 128],
                                        OTb[:, cd, :], start=(cd == 0),
                                        stop=(cd == CD - 1))
                return last
            ev_op = p.add("tensor", f_op,
                          waits=[ev_w, ev_o, ev_xf] + lu["bk"], inc="pe")

            def f_op_ev(v, wave=wave):
                last = None
                for i in range(4):
                    co = wave * 4 + i
                    last = v.tensor_tensor(Yr[:, co, :], bk[i][:], Xf[:, co, :], ALU.add)
                return last
            ev_y = p.add("vector", f_op_ev, waits=[ev_op, lu["Yr"]], inc="dve")
            for i in range(4):
                lu["bk"][i] = ev_y
        lu["Xf"] = ev_y
        lu["OTb"] = ev_y

        ev_apply, ev_store = emit_ln_tail(
            p, kb, cst, st, pref="a2ln", Yr=Yr, Yb=Yb, Ysq=Ysq,
            b_m=bk[0], b_s=bk[1], ps_sum=ps_ra[0:1, :],
            ps_sq=ps_rb[0:1, :], outT=x3T, t0=t0, ev_y=ev_y)
        lu["Yr"] = ev_apply
        lu["bk"][0] = st["b_m"]
        lu["bk"][1] = st["b_s"]
    return ev_store


def emit_l1_ffn(p, kb, cst, io):
    x3T = io["x3T_in"]
    out_tok = io["out_tok"]
    g1h, g2h = io["g1_hbm"], io["g2_hbm"]
    NH = 32

    Hbuf = kb.sb("f1_h", [128, NH, 2048], BF16)
    arena = kb.sb("f1_arena", [128, 16384], BF16)
    Xb = arena[:, 0:16384].rearrange("p (c n) -> p c n", n=2048)
    Yr = arena[:, 0:8192].bitcast(F32).rearrange("p (c n) -> p c n", n=512)
    Yb = arena[:, 8192:12288].rearrange("p (c n) -> p c n", n=512)
    Ysq = arena[:, 12288:16384].rearrange("p (c n) -> p c n", n=512)
    arena2 = kb.sb("f1_arena2", [128, 4096], BF16)
    W1b = [arena2[:, i * 1024:(i + 1) * 1024].rearrange("p (c n) -> p c n", n=128)
           for i in range(2)]
    W2b = [arena2[:, 2048 + i * 1024:2048 + (i + 1) * 1024]
           .rearrange("p (c n) -> p c n", n=128) for i in range(2)]
    Hw = [kb.sb(f"f1_hw{i}", [128, NH, 128], BF16) for i in range(2)]
    Xf = kb.sb("f1_Xf", [128, CD, 512], F32)
    Sg = kb.sb("f1_sg", [128, 512], BF16)
    Gt = [kb.sb(f"f1_g{i}", [128, 512], BF16) for i in range(2)]
    Sc1 = kb.sb("f1_s1", [128, NH], F32)
    Sc2 = kb.sb("f1_s2", [128, NH], F32)
    Sh1 = kb.sb("f1_sh1", [128, NH], F32)
    Sh2 = kb.sb("f1_sh2", [128, NH], F32)
    Sc3 = kb.sb("f1_s3", [128, CD], F32)
    eyef = kb.sb("f1_eyef", [128, 128], F32)
    XTok = arena2[:, 0:2048].bitcast(F32)
    kb.cache["f1ln"] = (arena2[0:1, 2048:3072].bitcast(F32),
                        arena2[0:1, 3072:4096].bitcast(F32))

    bk = [kb.ps(f"f1_b{i}", [128, 512], F32) for i in range(4)]
    tpf = [kb.ps(f"f1_tp{i}", [128, 128], F32) for i in range(2)]
    ps_ra = kb.ps("f1_rowa", [1, 512], F32)
    ps_rb = kb.ps("f1_rowb", [1, 512], F32)

    def f_scl(s):
        out = []
        for (tile, name, n) in [(Sc1, "f1_s1v", NH), (Sc2, "f1_s2v", NH),
                                (Sh1, "f1_sh1v", NH), (Sh2, "f1_sh2v", NH),
                                (Sc3, "f1_s3v", CD)]:
            for ch in range(n):
                out.append(s.dma_start(out=tile[:, ch:ch + 1],
                                       in_=io[name][ch * 128:(ch + 1) * 128, None]))
        out.append(s.dma_start(out=eyef[:], in_=io["EYE_F"]))
        return out
    ev_scl = p.add("sync", f_scl, inc="spd", n_dma=4 * NH + CD + 1)

    def f_xb(g):
        return [g.dma_start(out=Xb, in_=x3T.rearrange("(c p) n -> p c n", p=128))]
    ev_xb = p.add("gpsimd", f_xb, inc="poold", n_dma=1)

    lu = {"W1b": [None, None], "W2b": [None, None], "Hw": [None, None],
          "bk": [None] * 4, "Sg": None, "Gt": [None, None], "Hbuf": None,
          "Xf": None, "Yr": None, "tpf": [None, None], "XTok": [None] * 8}
    st = {}

    ev_h = None
    ev_mm = None
    for ch in range(NH):
        par = ch % 2

        def f_wl(g, ch=ch, par=par):
            return [g.dma_start(out=W1b[par], in_=io["f1_w1"][ch]),
                    g.dma_start(out=W2b[par], in_=io["f1_w2"][ch])]
        ev_w = p.add("gpsimd", f_wl, waits=[lu["W1b"][par], lu["W2b"][par]],
                     inc="poold", n_dma=2)

        for tb in range(NTB):
            def f_mm(t, par=par, tb=tb):
                last = None
                for cd in range(CD):
                    t.matmul(bk[0][:], W1b[par][:, cd, :],
                             Xb[:, cd, tb * 512:(tb + 1) * 512],
                             start=(cd == 0), stop=(cd == CD - 1))
                for cd in range(CD):
                    last = t.matmul(bk[1][:], W2b[par][:, cd, :],
                                    Xb[:, cd, tb * 512:(tb + 1) * 512],
                                    start=(cd == 0), stop=(cd == CD - 1))
                return last
            ev_mm = p.add("tensor", f_mm,
                          waits=[ev_w, ev_xb, lu["bk"][0], lu["bk"][1]], inc="pe")

            def f_pre(v, ch=ch):
                return v.tensor_scalar(Sg[:], bk[0][:], Sc1[:, ch:ch + 1], None, ALU.mult)
            ev_pre = p.add("vector", f_pre, waits=[ev_mm, lu["Sg"], ev_scl], inc="dve")

            def f_silu(sc):
                return sc.activation(Sg[:], Sg[:], AF.Silu)
            ev_silu = p.add("scalar", f_silu, waits=[ev_pre], inc="act")
            lu["bk"][0] = ev_silu

            def f_hm(v, ch=ch, tb=tb):
                ins = v.tensor_scalar(Gt[0][:], bk[1][:], Sc2[:, ch:ch + 1], None, ALU.mult)
                kb.edge("dve_i", v, ins)
                return v.tensor_tensor(Hbuf[:, ch, tb * 512:(tb + 1) * 512], Gt[0][:],
                                       Sg[:], ALU.mult)
            ev_h = p.add("vector", f_hm,
                         waits=[ev_silu, lu["Gt"][0], lu["Hbuf"]], inc="dve")
            lu["bk"][1] = ev_h
            lu["Sg"] = ev_h
            lu["Gt"][0] = ev_h
        lu["W1b"][par] = ev_mm
        lu["W2b"][par] = ev_mm
    ev_hdone = ev_h

    def matmul_4096(src_ev, wname, scol, dsth):
        ev_s = None
        lu["Gt"] = [lu["Gt"][0], lu["Gt"][1]]
        for co in range(NH):
            par = co % 2

            def f_wl(g, co=co, par=par):
                return [g.dma_start(out=Hw[par][:], in_=io[wname][co])]
            ev_w = p.add("gpsimd", f_wl, waits=[lu["Hw"][par]], inc="poold", n_dma=1)

            for tb in range(NTB):
                bki = tb % 2

                def f_mm(t, par=par, tb=tb, bki=bki):
                    last = None
                    for ci in range(NH):
                        last = t.matmul(bk[bki][:], Hw[par][:, ci, :],
                                        Hbuf[:, ci, tb * 512:(tb + 1) * 512],
                                        start=(ci == 0), stop=(ci == NH - 1))
                    return last
                ev_mm2 = p.add("tensor", f_mm,
                               waits=[ev_w, src_ev, lu["bk"][bki]], inc="pe")

                def f_pre(v, co=co, bki=bki):
                    return v.tensor_scalar(Gt[bki][:], bk[bki][:], scol[:, co:co + 1],
                                           None, ALU.mult)
                ev_pre = p.add("vector", f_pre,
                               waits=[ev_mm2, lu["Gt"][bki], ev_scl], inc="dve")

                def f_gelu(sc, bki=bki):
                    return sc.activation(Gt[bki][:], Gt[bki][:], AF.Gelu)
                ev_g = p.add("scalar", f_gelu, waits=[ev_pre], inc="act")
                lu["bk"][bki] = ev_g

                def f_st(s, co=co, tb=tb, bki=bki):
                    return [s.dma_start(out=dsth[co * 128:(co + 1) * 128,
                                                 tb * 512:(tb + 1) * 512],
                                        in_=Gt[bki][:])]
                ev_s = p.add("sync", f_st, waits=[ev_g], inc="spd", n_dma=1)
                lu["Gt"][bki] = ev_s
            lu["Hw"][par] = ev_mm2
        return ev_s

    ev_g1 = matmul_4096(ev_hdone, "f1_h1", Sh1, g1h)

    def f_lg1(g):
        return [g.dma_start(out=Hbuf[:, c * 8:(c + 1) * 8, :],
                            in_=g1h.rearrange("(c p) n -> p c n", p=128)
                            [:, c * 8:(c + 1) * 8, :]) for c in range(4)]
    ev_lg1 = p.add("gpsimd", f_lg1, waits=[ev_g1], inc="poold", n_dma=4)

    ev_g2 = matmul_4096(ev_lg1, "f1_h2", Sh2, g2h)

    def f_lg2(g):
        return [g.dma_start(out=Hbuf[:, c * 8:(c + 1) * 8, :],
                            in_=g2h.rearrange("(c p) n -> p c n", p=128)
                            [:, c * 8:(c + 1) * 8, :]) for c in range(4)]
    ev_lg2 = p.add("gpsimd", f_lg2, waits=[ev_g2], inc="poold", n_dma=4)

    ev_fin = None
    lu["bk"] = [None] * 4
    lu["Yr"] = ev_hdone
    lu["Hw"] = [lu["Hw"][0], lu["Hw"][1]]
    for tb in range(NTB):
        t0 = tb * 512

        def f_lx(s, t0=t0):
            return [s.dma_start(out=Xf[:], in_=x3T[:, t0:t0 + 512]
                                .rearrange("(c p) n -> p c n", p=128))]
        ev_xf = p.add("sync", f_lx, waits=[lu["Xf"]], inc="spd", n_dma=1)

        ev_y = None
        for co in range(CD):
            par = co % 2

            def f_w3(g, co=co, par=par, tb=tb):
                return [g.dma_start(out=Hw[par][:], in_=io["f1_w3"][co])]
            ev_w3 = p.add("gpsimd", f_w3, waits=[lu["Hw"][par]], inc="poold", n_dma=1)

            def f_dp(t, par=par, tb=tb):
                last = None
                for ci in range(NH):
                    last = t.matmul(bk[par][:], Hw[par][:, ci, :],
                                    Hbuf[:, ci, tb * 512:(tb + 1) * 512],
                                    start=(ci == 0), stop=(ci == NH - 1))
                return last
            ev_dp = p.add("tensor", f_dp,
                          waits=[ev_w3, ev_lg2, ev_xf, lu["bk"][par]], inc="pe")
            lu["Hw"][par] = ev_dp

            def f_ev(v, co=co, par=par):
                ins = v.tensor_scalar(Yr[:, co, :], bk[par][:], Sc3[:, co:co + 1],
                                      None, ALU.mult)
                kb.edge("dve_i", v, ins)
                return v.tensor_tensor(Yr[:, co, :], Yr[:, co, :], Xf[:, co, :], ALU.add)
            ev_y = p.add("vector", f_ev, waits=[ev_dp, lu["Yr"], ev_scl], inc="dve")
            lu["bk"][par] = ev_y
        lu["Xf"] = ev_y
        lu["Yr"] = None

        ev_apply, _ = emit_ln_tail(
            p, kb, cst, st, pref="f1ln", Yr=Yr, Yb=Yb, Ysq=Ysq,
            b_m=bk[2], b_s=bk[3], ps_sum=ps_ra[0:1, :],
            ps_sq=ps_rb[0:1, :], outT=None, t0=t0, ev_y=ev_y)
        lu["Yr"] = ev_apply

        for tt in range(4):
            for cd in range(CD):
                tp = tpf[cd % 2]

                def f_tp(t, tt=tt, cd=cd, tp=tp):
                    return t.transpose(tp[:], Yr[:, cd, tt * 128:(tt + 1) * 128], eyef[:])
                ev_t = p.add("tensor", f_tp,
                             waits=[ev_apply, ev_scl, lu["tpf"][cd % 2]], inc="pe")

                def f_te(sc, cd=cd, tp=tp):
                    return sc.activation(XTok[:, cd * 128:(cd + 1) * 128], tp[:],
                                         AF.Identity, bias=cst["zero_c"][:, 0:1],
                                         scale=1.0)
                ev_te = p.add("scalar", f_te,
                              waits=[ev_t, lu["XTok"][cd]], inc="act")
                lu["tpf"][cd % 2] = ev_te

            def f_sto(s, t0=t0, tt=tt):
                return [s.dma_start(out=out_tok[t0 + tt * 128:t0 + (tt + 1) * 128, :],
                                    in_=XTok)]
            ev_fin = p.add("sync", f_sto,
                           waits=[lu["tpf"][0], lu["tpf"][1]], inc="spd", n_dma=1)
            for cd in range(CD):
                lu["XTok"][cd] = ev_fin
    return ev_fin


def build_l0attn_nc():
    nc = bass.Bass("TRN2", target_bir_lowering=False, detect_race_conditions=False)
    io = {
        "xT": nc.dram_tensor("xT", [D, TOK], F32, kind="ExternalInput").ap(),
        "x1T": nc.dram_tensor("x1T", [D, TOK], F32, kind="ExternalOutput").ap(),
        "l0_wq": nc.dram_tensor("l0_wq", [D, D], F32, kind="ExternalInput").ap(),
        "l0_bq": nc.dram_tensor("l0_bq", [D], F32, kind="ExternalInput").ap(),
        "l0_wd": nc.dram_tensor("l0_wd", [D, 256], F32, kind="ExternalInput").ap(),
        "l0_bd": nc.dram_tensor("l0_bd", [256], F32, kind="ExternalInput").ap(),
        "l0_wu": nc.dram_tensor("l0_wu", [256, 2048], F32, kind="ExternalInput").ap(),
        "l0_bu": nc.dram_tensor("l0_bu", [2048], F32, kind="ExternalInput").ap(),
        "l0_wo": nc.dram_tensor("l0_wo", [D, D], F32, kind="ExternalInput").ap(),
        "l0_bo": nc.dram_tensor("l0_bo", [D], F32, kind="ExternalInput").ap(),
        "EYE_BF": nc.dram_tensor("EYE_BF", [128, 128], BF16, kind="ExternalInput").ap(),
    }
    import os
    if os.environ.get("KDEBUG"):
        for nm, shape, dt in [("d_kvl", [128, 2, 512], BF16), ("d_qt", [128, 1024], BF16),
                              ("d_k2", [128, 2048], BF16), ("d_v2", [128, 2048], BF16),
                              ("d_sc", [128, 256], F32), ("d_am", [128, 256], BF16),
                              ("d_lh", [128, 16], F32), ("d_ob", [128, 1024], BF16),
                              ("d_ot8", [128, CD, 512], BF16), ("d_y", [128, CD, 512], F32)]:
            io[nm] = nc.dram_tensor(nm, shape, dt, kind="ExternalOutput").ap()
    kb = KB(nc)
    p = Prog()
    cst = emit_consts(p, kb)
    ev_store = emit_l0_attn(p, kb, cst, io)

    def f_fin(s):
        return None
    p.add("sync", lambda s: None, waits=[ev_store])
    kb.realize(p)
    return nc


def l0attn_host_inputs(inputs):
    wu = np.ascontiguousarray(inputs["L0_attn_Wup"])
    bu = np.ascontiguousarray(inputs["L0_attn_bup"])
    perm = np.empty(2048, dtype=np.int64)
    g = np.arange(16)
    d = np.arange(64)
    perm[(g[:, None] * 64 + d[None, :]).ravel()] = (g[:, None] * 128 + d[None, :]).ravel()
    perm[(1024 + g[:, None] * 64 + d[None, :]).ravel()] = \
        (g[:, None] * 128 + 64 + d[None, :]).ravel()
    cc = host_constants()
    return {
        "l0_wq": np.ascontiguousarray(inputs["L0_attn_Wq"]),
        "l0_bq": np.ascontiguousarray(inputs["L0_attn_bq"]),
        "l0_wd": np.ascontiguousarray(inputs["L0_attn_Wdown"]),
        "l0_bd": np.ascontiguousarray(inputs["L0_attn_bdown"]),
        "l0_wu": np.ascontiguousarray(wu[:, perm]),
        "l0_bu": np.ascontiguousarray(bu[perm]),
        "l0_wo": np.ascontiguousarray(inputs["L0_attn_Wout"]),
        "l0_bo": np.ascontiguousarray(inputs["L0_attn_bout"]),
        "EYE_BF": cc["EYE_BF"],
    }


def run_l0attn(inputs, x_cur):
    shared = l0attn_host_inputs(inputs)
    nc = build_l0attn_nc()
    in_maps = []
    for c in range(N_CORES):
        b, h = c // 2, c % 2
        xT = np.ascontiguousarray(x_cur[b, h * TOK:(h + 1) * TOK, :].T)
        m = dict(shared)
        m["xT"] = xT
        in_maps.append(m)
    res = run_bass_kernel_spmd(nc, in_maps, core_ids=list(range(N_CORES)))
    out = np.empty((B, T_FULL, D), dtype=np.float32)
    for c in range(N_CORES):
        b, h = c // 2, c % 2
        out[b, h * TOK:(h + 1) * TOK, :] = res.results[c]["x1T"].T
    return out


def build_l0ffn_nc():
    nc = bass.Bass("TRN2", target_bir_lowering=False, detect_race_conditions=False)
    io = {
        "x1T_in": nc.dram_tensor("x1T_in", [D, TOK], F32, kind="ExternalInput").ap(),
        "x2T": nc.dram_tensor("x2T", [D, TOK], F32, kind="ExternalOutput").ap(),
        "f0_wg": nc.dram_tensor("f0_wg", [24, 128, CD, 128], I32, kind="ExternalInput").ap(),
        "f0_wu": nc.dram_tensor("f0_wu", [24, 128, CD, 128], I32, kind="ExternalInput").ap(),
        "f0_wd": nc.dram_tensor("f0_wd", [3072, D], I32, kind="ExternalInput").ap(),
    }
    kb = KB(nc)
    p = Prog()
    cst = emit_consts(p, kb)
    ev_store = emit_l0_ffn(p, kb, cst, io)
    p.add("sync", lambda s: None, waits=[ev_store])
    kb.realize(p)
    return nc


def l0ffn_host_inputs(inputs):
    return {
        "f0_wg": chunk_major(np.ascontiguousarray(inputs["L0_ffn_w_gate"].T), 24),
        "f0_wu": chunk_major(np.ascontiguousarray(inputs["L0_ffn_w_up"].T), 24),
        "f0_wd": np.ascontiguousarray(inputs["L0_ffn_w_down"].T),
    }


def run_l0ffn(inputs, x1):
    shared = l0ffn_host_inputs(inputs)
    nc = build_l0ffn_nc()
    in_maps = []
    for c in range(N_CORES):
        b, h = c // 2, c % 2
        m = dict(shared)
        m["x1T_in"] = np.ascontiguousarray(x1[b, h * TOK:(h + 1) * TOK, :].T)
        in_maps.append(m)
    res = run_bass_kernel_spmd(nc, in_maps, core_ids=list(range(N_CORES)))
    out = np.empty((B, T_FULL, D), dtype=np.float32)
    for c in range(N_CORES):
        b, h = c // 2, c % 2
        out[b, h * TOK:(h + 1) * TOK, :] = res.results[c]["x2T"].T
    return out


def build_l1attn_nc():
    nc = bass.Bass("TRN2", target_bir_lowering=False, detect_race_conditions=False)
    io = {
        "x2T_in": nc.dram_tensor("x2T_in", [D, T_FULL], F32, kind="ExternalInput").ap(),
        "o_out": nc.dram_tensor("o_out", [512, T_FULL], BF16, kind="ExternalOutput").ap(),
        "a1_wq": nc.dram_tensor("a1_wq", [D, 512], F32, kind="ExternalInput").ap(),
        "a1_wk": nc.dram_tensor("a1_wk", [D, 512], F32, kind="ExternalInput").ap(),
        "a1_wv": nc.dram_tensor("a1_wv", [D, 512], F32, kind="ExternalInput").ap(),
        "MEGAMASK": nc.dram_tensor("MEGAMASK", [128, 896], F32, kind="ExternalInput").ap(),
    }
    kb = KB(nc)
    p = Prog()
    cst = emit_consts(p, kb)
    ev = emit_l1_attn(p, kb, cst, io)
    p.add("sync", lambda s: None, waits=[ev])
    kb.realize(p)
    return nc


def build_l1out_nc():
    nc = bass.Bass("TRN2", target_bir_lowering=False, detect_race_conditions=False)
    io = {
        "oT_own": nc.dram_tensor("oT_own", [D, TOK], BF16, kind="ExternalInput").ap(),
        "x2T_own": nc.dram_tensor("x2T_own", [D, TOK], F32, kind="ExternalInput").ap(),
        "x3T": nc.dram_tensor("x3T", [D, TOK], F32, kind="ExternalOutput").ap(),
        "a1_wo": nc.dram_tensor("a1_wo", [D, D], F32, kind="ExternalInput").ap(),
    }
    kb = KB(nc)
    p = Prog()
    cst = emit_consts(p, kb)
    ev = emit_l1_outproj(p, kb, cst, io)
    p.add("sync", lambda s: None, waits=[ev])
    kb.realize(p)
    return nc


def l1_head_weight_slices(inputs):
    wqkv = inputs["L1_attn_w_qkv"]
    out = {}
    for hc in range(2):
        cols_q, cols_k, cols_v = [], [], []
        for h in range(8 * hc, 8 * hc + 8):
            cols_q += list(range(h * 192, h * 192 + 64))
            cols_k += list(range(h * 192 + 64, h * 192 + 128))
            cols_v += list(range(h * 192 + 128, h * 192 + 192))
        out[hc] = (np.ascontiguousarray(wqkv[:, cols_q]),
                   np.ascontiguousarray(wqkv[:, cols_k]),
                   np.ascontiguousarray(wqkv[:, cols_v]))
    return out


def run_l1attn(inputs, x2):
    cc = host_constants()
    wslices = l1_head_weight_slices(inputs)
    nc = build_l1attn_nc()
    in_maps = []
    x2T = [np.ascontiguousarray(x2[b].T) for b in range(B)]
    for c in range(N_CORES):
        b, hc = c // 2, c % 2
        wq, wk, wv = wslices[hc]
        in_maps.append(dict(x2T_in=x2T[b], a1_wq=wq, a1_wk=wk, a1_wv=wv,
                            MEGAMASK=cc["MEGAMASK"]))
    res = run_bass_kernel_spmd(nc, in_maps, core_ids=list(range(N_CORES)))
    o_full = []
    for b in range(B):
        o_full.append(np.concatenate([res.results[2 * b]["o_out"],
                                      res.results[2 * b + 1]["o_out"]], axis=0))
    return o_full


def run_l1out(inputs, x2, o_full):
    nc = build_l1out_nc()
    wo = np.ascontiguousarray(inputs["L1_attn_w_o"])
    in_maps = []
    for c in range(N_CORES):
        b, h = c // 2, c % 2
        in_maps.append(dict(
            oT_own=np.ascontiguousarray(o_full[b][:, h * TOK:(h + 1) * TOK]),
            x2T_own=np.ascontiguousarray(x2[b, h * TOK:(h + 1) * TOK, :].T),
            a1_wo=wo))
    res = run_bass_kernel_spmd(nc, in_maps, core_ids=list(range(N_CORES)))
    out = np.empty((B, T_FULL, D), dtype=np.float32)
    for c in range(N_CORES):
        b, h = c // 2, c % 2
        out[b, h * TOK:(h + 1) * TOK, :] = res.results[c]["x3T"].T
    return out


def build_l1ffn_nc():
    nc = bass.Bass("TRN2", target_bir_lowering=False, detect_race_conditions=False)
    io = {
        "x3T_in": nc.dram_tensor("x3T_in", [D, TOK], F32, kind="ExternalInput").ap(),
        "out_tok": nc.dram_tensor("out_tok", [TOK, D], F32, kind="ExternalOutput").ap(),
        "g1_hbm": nc.dram_tensor("g1_hbm", [4096, TOK], BF16, kind="Internal").ap(),
        "g2_hbm": nc.dram_tensor("g2_hbm", [4096, TOK], BF16, kind="Internal").ap(),
        "f1_w1": nc.dram_tensor("f1_w1", [32, 128, CD, 128], F32, kind="ExternalInput").ap(),
        "f1_w2": nc.dram_tensor("f1_w2", [32, 128, CD, 128], F32, kind="ExternalInput").ap(),
        "f1_h1": nc.dram_tensor("f1_h1", [32, 128, 32, 128], F32, kind="ExternalInput").ap(),
        "f1_h2": nc.dram_tensor("f1_h2", [32, 128, 32, 128], F32, kind="ExternalInput").ap(),
        "f1_w3": nc.dram_tensor("f1_w3", [8, 128, 32, 128], F32, kind="ExternalInput").ap(),
        "f1_s1v": nc.dram_tensor("f1_s1v", [4096], F32, kind="ExternalInput").ap(),
        "f1_s2v": nc.dram_tensor("f1_s2v", [4096], F32, kind="ExternalInput").ap(),
        "f1_sh1v": nc.dram_tensor("f1_sh1v", [4096], F32, kind="ExternalInput").ap(),
        "f1_sh2v": nc.dram_tensor("f1_sh2v", [4096], F32, kind="ExternalInput").ap(),
        "f1_s3v": nc.dram_tensor("f1_s3v", [D], F32, kind="ExternalInput").ap(),
        "EYE_F": nc.dram_tensor("EYE_F", [128, 128], F32, kind="ExternalInput").ap(),
    }
    kb = KB(nc)
    p = Prog()
    cst = emit_consts(p, kb)
    ev = emit_l1_ffn(p, kb, cst, io)
    p.add("sync", lambda s: None, waits=[ev])
    kb.realize(p)
    return nc


def chunk_major(wT, n_out_chunks):
    K, N = wT.shape
    return np.ascontiguousarray(
        wT.reshape(K // 128, 128, n_out_chunks, N // n_out_chunks)
        .transpose(2, 1, 0, 3))


def l1ffn_host_inputs(inputs):
    cc = host_constants()
    return {
        "f1_w1": chunk_major(np.ascontiguousarray(inputs["L1_ffn_w1_w"].T), 32),
        "f1_w2": chunk_major(np.ascontiguousarray(inputs["L1_ffn_w2_w"].T), 32),
        "f1_h1": chunk_major(np.ascontiguousarray(inputs["L1_ffn_h1_w"].T), 32),
        "f1_h2": chunk_major(np.ascontiguousarray(inputs["L1_ffn_h2_w"].T), 32),
        "f1_w3": chunk_major(np.ascontiguousarray(inputs["L1_ffn_w3_w"].T), 8),
        "f1_s1v": np.ascontiguousarray(inputs["L1_ffn_w1_s"]),
        "f1_s2v": np.ascontiguousarray(inputs["L1_ffn_w2_s"]),
        "f1_sh1v": np.ascontiguousarray(inputs["L1_ffn_h1_s"]),
        "f1_sh2v": np.ascontiguousarray(inputs["L1_ffn_h2_s"]),
        "f1_s3v": np.ascontiguousarray(inputs["L1_ffn_w3_s"]),
        "EYE_F": cc["EYE_F"],
    }


def run_l1ffn(inputs, x3):
    shared = l1ffn_host_inputs(inputs)
    nc = build_l1ffn_nc()
    in_maps = []
    for c in range(N_CORES):
        b, h = c // 2, c % 2
        m = dict(shared)
        m["x3T_in"] = np.ascontiguousarray(x3[b, h * TOK:(h + 1) * TOK, :].T)
        in_maps.append(m)
    res = run_bass_kernel_spmd(nc, in_maps, core_ids=list(range(N_CORES)))
    out = np.empty((B, T_FULL, D), dtype=np.float32)
    for c in range(N_CORES):
        b, h = c // 2, c % 2
        out[b, h * TOK:(h + 1) * TOK, :] = res.results[c]["out_tok"]
    return out


PAIRS = [[0, 1], [2, 3], [4, 5], [6, 7]]


def build_fused_nc():
    nc = bass.Bass("TRN2", target_bir_lowering=False, detect_race_conditions=False)
    ext_f32 = lambda n, s: nc.dram_tensor(n, s, F32, kind="ExternalInput").ap()
    io = {
        "xT": ext_f32("xT", [D, TOK]),
        "l0_wq": ext_f32("l0_wq", [D, D]), "l0_bq": ext_f32("l0_bq", [D]),
        "l0_wd": ext_f32("l0_wd", [D, 256]), "l0_bd": ext_f32("l0_bd", [256]),
        "l0_wu": ext_f32("l0_wu", [256, 2048]), "l0_bu": ext_f32("l0_bu", [2048]),
        "l0_wo": ext_f32("l0_wo", [D, D]), "l0_bo": ext_f32("l0_bo", [D]),
        "EYE_BF": nc.dram_tensor("EYE_BF", [128, 128], BF16, kind="ExternalInput").ap(),
        "f0_wg": nc.dram_tensor("f0_wg", [24, 128, CD, 128], I32, kind="ExternalInput").ap(),
        "f0_wu": nc.dram_tensor("f0_wu", [24, 128, CD, 128], I32, kind="ExternalInput").ap(),
        "f0_wd": nc.dram_tensor("f0_wd", [3072, D], I32, kind="ExternalInput").ap(),
        "a1_wq": ext_f32("a1_wq", [D, 512]), "a1_wk": ext_f32("a1_wk", [D, 512]),
        "a1_wv": ext_f32("a1_wv", [D, 512]), "a1_wo": ext_f32("a1_wo", [D, D]),
        "MEGAMASK": ext_f32("MEGAMASK", [128, 896]),
        "msel": ext_f32("msel", [128, 2]),
        "f1_w1": nc.dram_tensor("f1_w1", [32, 128, CD, 128], F32, kind="ExternalInput").ap(), "f1_w2": nc.dram_tensor("f1_w2", [32, 128, CD, 128], F32, kind="ExternalInput").ap(),
        "f1_h1": nc.dram_tensor("f1_h1", [32, 128, 32, 128], F32, kind="ExternalInput").ap(),
        "f1_h2": nc.dram_tensor("f1_h2", [32, 128, 32, 128], F32, kind="ExternalInput").ap(),
        "f1_w3": nc.dram_tensor("f1_w3", [8, 128, 32, 128], F32, kind="ExternalInput").ap(),
        "f1_s1v": ext_f32("f1_s1v", [4096]), "f1_s2v": ext_f32("f1_s2v", [4096]),
        "f1_sh1v": ext_f32("f1_sh1v", [4096]), "f1_sh2v": ext_f32("f1_sh2v", [4096]),
        "f1_s3v": ext_f32("f1_s3v", [D]),
        "EYE_F": ext_f32("EYE_F", [128, 128]),
        "out_tok": nc.dram_tensor("out_tok", [TOK, D], F32, kind="ExternalOutput").ap(),
        "x1T": nc.dram_tensor("x1T", [D, TOK], F32, kind="Internal").ap(),
        "x3T": nc.dram_tensor("x3T", [D, TOK], F32, kind="Internal").ap(),
        "g1_hbm": nc.dram_tensor("g1_hbm", [4096, TOK], BF16, kind="Internal").ap(),
        "g2_hbm": nc.dram_tensor("g2_hbm", [4096, TOK], BF16, kind="Internal").ap(),
    }
    x2T = nc.dram_tensor("x2T", [D, TOK], F32, kind="Internal")
    io["x2T"] = x2T.ap()
    x2bf_in = nc.dram_tensor("x2bf_in", [D, TOK], BF16)
    x2bf_out = nc.dram_tensor("x2bf_out", [2 * D, TOK], BF16)
    o_cc_in = nc.dram_tensor("o_cc_in", [512, T_FULL], BF16)
    o_cc_out = nc.dram_tensor("o_cc_out", [D, T_FULL], BF16)
    io["x2bf_cc_in"] = x2bf_in.ap()
    io["x2bf_cc_out"] = x2bf_out.ap()
    io["o_out"] = o_cc_in.ap()
    io["o_cc_out"] = o_cc_out.ap()

    kb = KB(nc, fused=True)
    p = Prog()
    cst = emit_consts(p, kb)

    io["x1T_in"] = io["x1T"]
    io["x2T_own"] = io["x2T"]
    io["x3T_in"] = io["x3T"]

    p.scope = "B0_l0attn"
    emit_l0_attn(p, kb, cst, io)
    kb.block_boundary(p)
    p.scope = "B1_l0ffn"
    emit_l0_ffn(p, kb, cst, io)
    kb.block_boundary(p)
    p.scope = "B2_cc1"

    def f_cc1(g):
        return g.collective_compute("AllGather", ALU.bypass, replica_groups=PAIRS,
                                    ins=[x2bf_in.ap().opt()],
                                    outs=[x2bf_out.ap().opt()])
    io["ev_x2g"] = p.add("gpsimd", f_cc1, waits=io.get("x2bf_evs", []), inc="poolc")

    p.scope = "B3_l1attn"
    emit_l1_attn(p, kb, cst, io)
    ev_ost = None
    kb.block_boundary(p)
    p.scope = "B4_cc2"

    def f_cc2(g):
        return g.collective_compute("AllGather", ALU.bypass, replica_groups=PAIRS,
                                    ins=[o_cc_in.ap().opt()],
                                    outs=[o_cc_out.ap().opt()])
    io["ev_og"] = p.add("gpsimd", f_cc2, inc="poolc")

    p.scope = "B5_l1out"
    emit_l1_outproj(p, kb, cst, io)
    kb.block_boundary(p)
    p.scope = "B6_l1ffn"
    ev_fin = emit_l1_ffn(p, kb, cst, io)
    p.add("sync", lambda s: None, waits=[ev_fin])
    kb.realize(p)
    return nc


def fused_in_maps(inputs):
    shared = {}
    shared.update(l0attn_host_inputs(inputs))
    shared.update(l0ffn_host_inputs(inputs))
    shared.update(l1ffn_host_inputs(inputs))
    shared["a1_wo"] = np.ascontiguousarray(inputs["L1_attn_w_o"])
    cc = host_constants()
    shared["MEGAMASK"] = cc["MEGAMASK"]
    wslices = l1_head_weight_slices(inputs)
    x = np.asarray(inputs["L0_attn_x"], dtype=np.float32)
    in_maps = []
    for c in range(N_CORES):
        b, hc = c // 2, c % 2
        m = dict(shared)
        wq, wk, wv = wslices[hc]
        m["a1_wq"], m["a1_wk"], m["a1_wv"] = wq, wk, wv
        msel = np.zeros((128, 2), dtype=np.float32)
        msel[:, 0] = 1.0 - hc
        msel[:, 1] = hc
        m["msel"] = msel
        m["xT"] = np.ascontiguousarray(x[b, hc * TOK:(hc + 1) * TOK, :].T)
        in_maps.append(m)
    return in_maps


def kernel_fused(**inputs):
    nc = build_fused_nc()
    in_maps = fused_in_maps(inputs)
    res = run_bass_kernel_spmd(nc, in_maps, core_ids=list(range(N_CORES)))
    out = np.empty((B, T_FULL, D), dtype=np.float32)
    for c in range(N_CORES):
        b, hc = c // 2, c % 2
        out[b, hc * TOK:(hc + 1) * TOK, :] = res.results[c]["out_tok"]
    return out


def kernel_unfused(**inputs):
    x = np.asarray(inputs["L0_attn_x"], dtype=np.float32)
    x1 = run_l0attn(inputs, x)
    x2 = run_l0ffn(inputs, x1)
    o_full = run_l1attn(inputs, x2)
    x3 = run_l1out(inputs, x2, o_full)
    return run_l1ffn(inputs, x3)


def kernel(**inputs):
    return kernel_fused(**inputs)


# version 10 (final)
# speedup vs baseline: 1.0522x; 1.0293x over previous
import contextlib

import numpy as np
import ml_dtypes

import concourse.bass as bass
import concourse.mybir as mybir
from concourse.bass_utils import run_bass_kernel_spmd

F32 = mybir.dt.float32
BF16 = mybir.dt.bfloat16
I32 = mybir.dt.int32
AF = mybir.ActivationFunctionType
ALU = mybir.AluOpType
AX = mybir.AxisListType
BF = ml_dtypes.bfloat16

N_CORES = 8
D = 1024
B = 4
T_FULL = 4096
TOK = 2048
H = 16
DH = 64
CD = D // 128
NTB = TOK // 512
C0_L0 = 8.0
C1_L1 = 4.0
NEG_BIG = -30000.0
EPS = 1e-5

ENGS = ("tensor", "scalar", "vector", "gpsimd", "sync")


def ap_win(base2d, nwin, stride, width):
    ap = base2d.ap
    return bass.AP(base2d.tensor, base2d.offset,
                   [list(ap[0]), [stride, nwin], [1, width]])


class Prog:

    def __init__(self):
        self.tasks = {e: [] for e in ENGS}
        self.cnt = {c: 0 for c in ("pe", "act", "dve", "poolc", "spd", "poold", "scd")}
        self.n_added = 0
        self.barrier = {}
        self.scope = None

    def ev(self, ch):
        return (ch, self.cnt[ch])

    def add(self, eng, fn, waits=(), inc=None, n_dma=0):
        waits = [w for w in waits if w is not None] + list(self.barrier.items())
        ev = None
        if inc is not None:
            if n_dma:
                self.cnt[inc] += 16 * n_dma
            else:
                self.cnt[inc] += 1
            ev = (inc, self.cnt[inc])
        snap = dict(self.cnt)
        if inc is not None:
            snap[inc] = self.cnt[inc] - (16 * n_dma if n_dma else 1)
        self.tasks[eng].append((fn, tuple(waits), inc, n_dma, self.n_added, snap,
                                self.scope))
        self.n_added += 1
        return ev

    def realize(self, kb):
        waited = {}
        for ename in ENGS:
            tlist = self.tasks[ename]
            if not tlist:
                continue
            dec = getattr(kb.blk, ename)

            import os
            trunc = int(os.environ.get("KTRUNC", "1000000000"))
            serial = bool(os.environ.get("KSERIAL"))

            def body(eng, tlist=tlist, ename=ename, trunc=trunc, serial=serial):
                cur_scope = None
                cur_sid = None
                for fn, waits, inc, n_dma, idx, snap, scope in tlist:
                    if idx >= trunc:
                        continue
                    if scope != cur_scope:
                        if cur_scope is not None:
                            kb.nc.leave_named_scope(cur_scope, cur_sid, False)
                        if scope is not None:
                            cur_sid, _ = kb.nc.enter_named_scope(scope, False)
                        cur_scope = scope
                    if serial:
                        waits = tuple((ch, v) for ch, v in snap.items() if v > 0)
                    for (ch, val) in waits:
                        key = (ename, ch)
                        if waited.get(key, -1) < val:
                            eng.wait_ge(kb.sem[ch], val)
                            waited[key] = val
                    out = fn(eng)
                    if inc is not None:
                        if n_dma:
                            assert isinstance(out, (list, tuple)) and len(out) == n_dma
                            for ins in out:
                                ins.then_inc(kb.sem[inc], 16)
                        else:
                            assert out is not None
                            out.then_inc(kb.sem[inc], 1)
                if cur_scope is not None:
                    kb.nc.leave_named_scope(cur_scope, cur_sid, False)

            dec(body)


class KB:
    def __init__(self, nc, fused=False, arena_el=105792):
        self.nc = nc
        self.stack = contextlib.ExitStack()
        self.sem = {}
        for ch in ("pe", "act", "dve", "poolc", "spd", "poold", "scd",
                   "dve_i", "act_i", "pool_i"):
            self.sem[ch] = self.stack.enter_context(nc.semaphore(ch))
        self.icnt = {"dve_i": 0, "act_i": 0, "pool_i": 0}
        self.cache = {}
        self.blk = None
        self.fused = fused
        if fused:
            self.arena = self.stack.enter_context(
                nc.sbuf_tensor("ARENA", [128, arena_el], BF16))
            self.abump = 0
            self.parena = self.stack.enter_context(
                nc.psum_tensor("PSARENA", [128, 4096], F32))
            self.pbump = 0

    def block_boundary(self, prog):
        self.abump = 0
        self.pbump = 0
        prog.barrier = dict(prog.cnt)

    def edge(self, ch, eng, ins):
        ins.then_inc(self.sem[ch], 1)
        self.icnt[ch] += 1
        eng.wait_ge(self.sem[ch], self.icnt[ch])

    def sb(self, name, shape, dtype):
        if not self.fused or name.startswith("c_"):
            return self.stack.enter_context(self.nc.sbuf_tensor(name, shape, dtype))
        import math as _m
        n_el = int(np.prod(shape[1:])) * mybir.dt.size(dtype) // 2
        off = (self.abump + 31) & ~31
        self.abump = off + n_el
        assert self.abump <= self.arena.shape[1], f"arena OOM at {name}: {self.abump}"
        ap = self.arena[:, off:off + n_el]
        if dtype != BF16:
            ap = ap.bitcast(dtype)
        if len(shape) == 2:
            ap = ap[0:shape[0], :] if shape[0] != 128 else ap
        elif len(shape) == 3:
            ap = ap.rearrange("p (a b) -> p a b", a=shape[1])
            if shape[0] != 128:
                ap = ap[0:shape[0]]
        else:
            raise ValueError(shape)
        return _ArenaTensor(ap)

    def ps(self, name, shape, dtype=F32):
        if not self.fused:
            return self.stack.enter_context(self.nc.psum_tensor(name, shape, dtype))
        assert len(shape) == 2
        el = mybir.dt.size(dtype)
        nbanks = (shape[1] * el + 2047) // 2048
        off = self.pbump * 512
        self.pbump += nbanks
        assert self.pbump <= 8, f"psum OOM at {name}"
        ap = self.parena.ap()[:, off:off + nbanks * 512]
        if dtype != F32:
            ap = ap.bitcast(dtype)
        return _ArenaTensor(ap[0:shape[0], 0:shape[1]])

    def realize(self, prog):
        self.blk = self.stack.enter_context(self.nc.Block())
        prog.realize(self)
        self.stack.close()


class _ArenaTensor:

    def __init__(self, ap):
        self._ap = ap
        self.tensor = ap.tensor
        self.shape = ap.shape

    def __getitem__(self, key):
        return self._ap[key]

    def ap(self):
        return self._ap

    def rearrange(self, *a, **kw):
        return self._ap.rearrange(*a, **kw)

    def bitcast(self, dt):
        return self._ap.bitcast(dt)


def host_constants():
    mega = np.full((128, 896), NEG_BIG, dtype=np.float32)
    kk = np.arange(128)[:, None]
    uu = np.arange(896)[None, :]
    mega[kk <= uu - 384] = 0.0
    return dict(EYE_BF=np.eye(128, dtype=BF), EYE_F=np.eye(128, dtype=np.float32),
                MEGAMASK=mega)


def emit_consts(p, kb):
    ones_r = kb.sb("c_ones_r", [1, 128], BF16)
    onesf_r = kb.sb("c_onesf_r", [1, 128], F32)
    ones_c = kb.sb("c_ones_c", [128, 1], BF16)
    zero_c = kb.sb("c_zero_c", [128, 1], F32)
    eps_c = kb.sb("c_eps_c", [1, 1], F32)
    cl0_c = kb.sb("c_cl0", [128, 1], F32)
    cl1_c = kb.sb("c_cl1", [128, 1], F32)

    def f(v):
        v.memset(ones_r[:], 1.0)
        v.memset(onesf_r[:], 1.0)
        v.memset(ones_c[:], 1.0)
        v.memset(zero_c[:], 0.0)
        v.memset(eps_c[:], EPS)
        v.memset(cl0_c[:], -C0_L0)
        return v.memset(cl1_c[:], -C1_L1)

    ev = p.add("vector", f, inc="dve")
    return dict(ones_r=ones_r, onesf_r=onesf_r, ones_c=ones_c, zero_c=zero_c,
                eps_c=eps_c, cl0_c=cl0_c, cl1_c=cl1_c, ev=ev)


def emit_ln_tail(p, kb, cst, st, *, pref, Yr, Yb, Ysq, b_m, b_s, ps_sum, ps_sq,
                 outT, t0, ev_y, Xbf_out=None):
    if pref not in kb.cache:
        kb.cache[pref] = (kb.sb(f"{pref}_mean", [1, 512], F32),
                          kb.sb(f"{pref}_rstd", [1, 512], F32))
    mean_r, rstd_r = kb.cache[pref]

    def f_cast(g):
        return g.tensor_copy(Yb[:], Yr[:])
    ev_cast = p.add("gpsimd", f_cast, waits=[ev_y, st.get("Yb")], inc="poolc")

    def f_sq(sc):
        return sc.activation(Ysq[:], Yr[:], AF.Square)
    ev_sq = p.add("scalar", f_sq, waits=[ev_y, st.get("Ysq")], inc="act")

    def f_stats(t):
        for cd in range(CD):
            t.matmul(ps_sum, cst["ones_c"][:], Yb[:, cd, :],
                     start=(cd == 0), stop=(cd == CD - 1))
        for cd in range(CD):
            last = t.matmul(ps_sq, cst["ones_c"][:], Ysq[:, cd, :],
                            start=(cd == 0), stop=(cd == CD - 1))
        return last
    ev_stats = p.add("tensor", f_stats,
                     waits=[ev_cast, ev_sq, st.get("ps_row")], inc="pe")

    def f_rows(v):
        ins = v.tensor_scalar(mean_r[:], ps_sum, 1.0 / D, None, ALU.mult)
        v.tensor_scalar(rstd_r[:], ps_sq, 1.0 / D, None, ALU.mult)
        kb.edge("dve_i", v, ins)
        ins = v.tensor_tensor(ps_sum, mean_r[:], mean_r[:], ALU.mult)
        kb.edge("dve_i", v, ins)
        return v.tensor_tensor(rstd_r[:], rstd_r[:], ps_sum, ALU.subtract)
    ev_rows = p.add("vector", f_rows, waits=[ev_stats], inc="dve")
    st["ps_row"] = ev_rows

    def f_sqrt(sc):
        return sc.activation(rstd_r[:], rstd_r[:], AF.Sqrt, bias=cst["eps_c"][0:1, 0:1],
                             scale=1.0)
    ev_sqrt = p.add("scalar", f_sqrt, waits=[ev_rows], inc="act")

    def f_recip(v):
        return v.reciprocal(rstd_r[:], rstd_r[:])
    ev_recip = p.add("vector", f_recip, waits=[ev_sqrt], inc="dve")

    def f_bcast(t):
        t.matmul(b_m[:], cst["onesf_r"][:], mean_r[:], start=True, stop=True)
        return t.matmul(b_s[:], cst["onesf_r"][:], rstd_r[:], start=True, stop=True)
    ev_bcast = p.add("tensor", f_bcast,
                     waits=[ev_recip, st.get("b_m"), st.get("b_s")], inc="pe")

    def f_apply(v):
        mb = b_m[:].unsqueeze(1).broadcast_to((128, CD, 512))
        rb = b_s[:].unsqueeze(1).broadcast_to((128, CD, 512))
        ins = v.tensor_tensor(Yr[:], Yr[:], mb, ALU.subtract)
        kb.edge("dve_i", v, ins)
        out = v.tensor_tensor(Yr[:], Yr[:], rb, ALU.mult)
        if Xbf_out is not None:
            kb.edge("dve_i", v, out)
            out = v.tensor_copy(Xbf_out, Yr[:])
        return out
    ev_apply = p.add("vector", f_apply, waits=[ev_bcast], inc="dve")
    st["b_m"] = ev_apply
    st["b_s"] = ev_apply
    st["Yb"] = ev_stats
    st["Ysq"] = ev_stats

    if outT is None:
        return ev_apply, ev_apply

    def f_store(s):
        return [s.dma_start(out=outT[:, t0:t0 + 512].rearrange("(c p) n -> p c n", p=128),
                            in_=Yr[:])]
    ev_store = p.add("sync", f_store, waits=[ev_apply], inc="spd", n_dma=1)
    return ev_apply, ev_store


def emit_l0_attn(p, kb, cst, io):
    xT, x1T = io["xT"], io["x1T"]

    Wq = kb.sb("l0_Wq", [128, CD, 1024], BF16)
    Wd = kb.sb("l0_Wd", [128, CD, 256], BF16)
    Wu = kb.sb("l0_Wu", [128, 2, 2048], BF16)
    Wo = kb.sb("l0_Wo", [128, CD, 1024], BF16)
    bq_r = kb.sb("l0_bqr", [1, 1024], BF16)
    bu_r = kb.sb("l0_bur", [1, 2048], BF16)
    bd_c = kb.sb("l0_bdc", [128, 2], F32)
    bo_c = kb.sb("l0_boc", [128, CD], F32)
    eye_t = kb.sb("l0_eye", [128, 128], BF16)

    Xf = kb.sb("l0_Xf", [128, CD, 512], F32)
    Xb = kb.sb("l0_Xb", [128, CD, 512], BF16)
    KVL = kb.sb("l0_kvl", [128, 2, 512], BF16)
    Qt = [kb.sb(f"l0_qt{i}", [128, 1024], BF16) for i in range(2)]
    K2 = [kb.sb(f"l0_k2{i}", [128, 2048], BF16) for i in range(2)]
    V2 = [kb.sb(f"l0_v2{i}", [128, 2048], BF16) for i in range(2)]
    PR = kb.sb("l0_pr", [128, 8, 1024], BF16)
    PR2 = kb.sb("l0_pr2", [128, 8, 1024], BF16)
    Fold2 = kb.sb("l0_fold2", [128, 128], F32)
    Fold = kb.sb("l0_fold", [128, 128], F32)
    Sc = kb.sb("l0_s", [128, 256], F32)
    Am = kb.sb("l0_a", [128, 256], BF16)
    Lh = kb.sb("l0_l", [128, 16], F32)
    Rh = kb.sb("l0_r", [128, 16], F32)
    Ot = kb.sb("l0_ot", [128, 1024], F32)
    Ot2 = kb.sb("l0_ot2", [128, 1024], F32)
    Ob = [kb.sb(f"l0_ob{i}", [128, 1024], BF16) for i in range(2)]
    OT = kb.sb("l0_oT", [128, CD, 512], BF16)
    Yr = kb.sb("l0_y", [128, CD, 512], F32)
    Yb = kb.sb("l0_yb", [128, CD, 512], BF16)
    Ysq = kb.sb("l0_ysq", [128, CD, 512], BF16)

    bk = [kb.ps(f"l0_b{i}", [128, 512], F32) for i in range(4)]
    tp0 = kb.ps("l0_tp0", [128, 128], BF16)
    tp1 = kb.ps("l0_tp1", [128, 128], BF16)
    ps_ra = kb.ps("l0_rowa", [1, 512], F32)
    ps_rb = kb.ps("l0_rowb", [1, 512], F32)

    def f_wl(g):
        return [g.dma_start(out=Wq[:], in_=io["l0_wq"].rearrange("(c p) n -> p c n", p=128)),
                g.dma_start(out=Wd[:], in_=io["l0_wd"].rearrange("(c p) n -> p c n", p=128)),
                g.dma_start(out=Wu[:], in_=io["l0_wu"].rearrange("(c p) n -> p c n", p=128)),
                g.dma_start(out=Wo[:], in_=io["l0_wo"].rearrange("(c p) n -> p c n", p=128)),
                g.dma_start(out=bq_r[:], in_=io["l0_bq"][None, :]),
                g.dma_start(out=bu_r[:], in_=io["l0_bu"][None, :])]
    ev_wbf = p.add("gpsimd", f_wl, inc="poold", n_dma=6)

    def f_wl2(s):
        out = [s.dma_start(out=bd_c[:, c:c + 1], in_=io["l0_bd"][c * 128:(c + 1) * 128, None])
               for c in range(2)]
        out += [s.dma_start(out=bo_c[:, c:c + 1], in_=io["l0_bo"][c * 128:(c + 1) * 128, None])
                for c in range(CD)]
        out.append(s.dma_start(out=eye_t[:], in_=io["EYE_BF"]))
        return out
    ev_wsp = p.add("sync", f_wl2, inc="spd", n_dma=11)

    st = {}
    lu = {"Xf": None, "Xb": None, "OT": None, "Yr": None,
          "bk": [None] * 4, "tp": [None, None],
          "QK2": [None, None], "V2": [None, None], "Am": [],
          "Ob": [None, None]}
    pend_tp = []

    def emit_tp(item):
        ev_ob_, s0_, Ob_ = item
        ev_t = None
        for cd in range(CD):
            tp = (tp0, tp1)[cd % 2]

            def f_tp(t, cd=cd, tp=tp, ev_ob_=ev_ob_, Ob_=Ob_):
                return t.transpose(tp[:], Ob_[:, cd * 128:(cd + 1) * 128], eye_t[:])
            ev_t = p.add("tensor", f_tp,
                         waits=[ev_ob_, ev_wsp, lu["tp"][cd % 2]], inc="pe")

            def f_tpe(sc, cd=cd, tp=tp, s0_=s0_):
                return sc.activation(OT[:, cd, s0_:s0_ + 128], tp[:], AF.Identity,
                                     bias=cst["zero_c"][:, 0:1], scale=1.0)
            ev_te = p.add("scalar", f_tpe, waits=[ev_t, lu["OT"]], inc="act")
            lu["tp"][cd % 2] = ev_te
        lu["Ob"][0 if Ob_ is Ob[0] else 1] = ev_t
        return ev_te
    ev_store = None

    for tb in range(NTB):
        t0 = tb * 512

        def f_lx(s, t0=t0):
            return [s.dma_start(out=Xf[:], in_=xT[:, t0:t0 + 512]
                                .rearrange("(c p) n -> p c n", p=128))]
        ev_xf = p.add("sync", f_lx, waits=[lu["Xf"]], inc="spd", n_dma=1)

        def f_lxb(g, t0=t0):
            return [g.dma_start(out=Xb[:], in_=xT[:, t0:t0 + 512]
                                .rearrange("(c p) n -> p c n", p=128))]
        ev_xb = p.add("gpsimd", f_lxb, waits=[lu["Xb"]], inc="poold", n_dma=1)

        def f_kvl(t):
            last = None
            for cl in range(2):
                for cd in range(CD):
                    last = t.matmul(bk[cl][:], Wd[:, cd, cl * 128:(cl + 1) * 128],
                                    Xb[:, cd, :], start=(cd == 0), stop=(cd == CD - 1))
            return last
        ev_kvl_mm = p.add("tensor", f_kvl,
                          waits=[ev_wbf, ev_wsp, ev_xb, lu["bk"][0], lu["bk"][1]],
                          inc="pe")

        def f_kvl_ev(sc):
            sc.activation(KVL[:, 0, :], bk[0][:], AF.Identity, bias=bd_c[:, 0:1], scale=1.0)
            return sc.activation(KVL[:, 1, :], bk[1][:], AF.Identity,
                                 bias=bd_c[:, 1:2], scale=1.0)
        ev_kvl = p.add("scalar", f_kvl_ev, waits=[ev_kvl_mm], inc="act")
        lu["bk"][0] = lu["bk"][1] = ev_kvl

        ev_ot_tiles = []
        for tt4 in range(4):
            s0 = tt4 * 128
            tp_ = (tb * 4 + tt4) % 2
            Qt_, K2_, V2_ = Qt[tp_], K2[tp_], V2[tp_]

            def f_qk(t, s0=s0):
                for o in range(2):
                    for cd in range(CD):
                        t.matmul(bk[o][:], Xb[:, cd, s0:s0 + 128],
                                 Wq[:, cd, o * 512:(o + 1) * 512],
                                 start=(cd == 0), stop=False)
                    t.matmul(bk[o][:], cst["ones_r"][:], bq_r[0:1, o * 512:(o + 1) * 512],
                             start=False, stop=True)
                for o in range(2):
                    for cl in range(2):
                        t.matmul(bk[2 + o][:], KVL[:, cl, s0:s0 + 128],
                                 Wu[:, cl, o * 512:(o + 1) * 512],
                                 start=(cl == 0), stop=False)
                    last = t.matmul(bk[2 + o][:], cst["ones_r"][:],
                                    bu_r[0:1, o * 512:(o + 1) * 512], start=False, stop=True)
                return last
            ev_qk_mm = p.add("tensor", f_qk,
                             waits=[ev_kvl, cst["ev"]] + lu["bk"], inc="pe")

            def f_qk_ev(sc, Qt_=Qt_, K2_=K2_):
                sc.activation(Qt_[:, 0:512], bk[0][:], AF.Identity,
                              bias=cst["zero_c"][:, 0:1], scale=1.0)
                sc.activation(Qt_[:, 512:1024], bk[1][:], AF.Identity,
                              bias=cst["zero_c"][:, 0:1], scale=1.0)
                sc.activation(K2_[:, 0:512], bk[2][:], AF.Identity,
                              bias=cst["zero_c"][:, 0:1], scale=1.0)
                sc.activation(K2_[:, 1024:1536], bk[2][:], AF.Identity,
                              bias=cst["zero_c"][:, 0:1], scale=1.0)
                sc.activation(K2_[:, 512:1024], bk[3][:], AF.Identity,
                              bias=cst["zero_c"][:, 0:1], scale=1.0)
                return sc.activation(K2_[:, 1536:2048], bk[3][:], AF.Identity,
                                     bias=cst["zero_c"][:, 0:1], scale=1.0)
            ev_qk = p.add("scalar", f_qk_ev, waits=[ev_qk_mm] + lu["QK2"][tp_] if isinstance(lu["QK2"][tp_], list) else [ev_qk_mm, lu["QK2"][tp_]], inc="act")
            for i in range(4):
                lu["bk"][i] = ev_qk

            def f_vmm(t, s0=s0):
                for o in range(2):
                    for cl in range(2):
                        t.matmul(bk[o][:], KVL[:, cl, s0:s0 + 128],
                                 Wu[:, cl, (2 + o) * 512:(3 + o) * 512],
                                 start=(cl == 0), stop=False)
                    last = t.matmul(bk[o][:], cst["ones_r"][:],
                                    bu_r[0:1, (2 + o) * 512:(3 + o) * 512],
                                    start=False, stop=True)
                return last
            ev_v_mm = p.add("tensor", f_vmm, waits=[ev_qk], inc="pe")

            def f_v_ev(sc, V2_=V2_):
                sc.activation(V2_[:, 0:512], bk[0][:], AF.Identity,
                              bias=cst["zero_c"][:, 0:1], scale=1.0)
                sc.activation(V2_[:, 1024:1536], bk[0][:], AF.Identity,
                              bias=cst["zero_c"][:, 0:1], scale=1.0)
                sc.activation(V2_[:, 512:1024], bk[1][:], AF.Identity,
                              bias=cst["zero_c"][:, 0:1], scale=1.0)
                return sc.activation(V2_[:, 1536:2048], bk[1][:], AF.Identity,
                                     bias=cst["zero_c"][:, 0:1], scale=1.0)
            ev_v = p.add("scalar", f_v_ev, waits=[ev_v_mm] + (lu["V2"][tp_] if isinstance(lu["V2"][tp_], list) else [lu["V2"][tp_]]), inc="act")
            lu["bk"][0] = lu["bk"][1] = ev_v

            if tb == 0 and tt4 == 0 and "d_qt" in io:
                def f_dbg1(s):
                    return [s.dma_start(out=io["d_kvl"], in_=KVL[:]),
                            s.dma_start(out=io["d_qt"], in_=Qt[:]),
                            s.dma_start(out=io["d_k2"], in_=K2[:]),
                            s.dma_start(out=io["d_v2"], in_=V2[:])]
                p.add("sync", f_dbg1, waits=[ev_qk, ev_v, ev_kvl], inc="spd", n_dma=4)
            ev_s_both = []
            for half in range(2):
                p0 = half * 8
                eng, ch, ich = ("vector", "dve", "dve_i")
                PRb = PR if half == 0 else PR2

                def f_s(v, p0=p0, ich=ich, PRb=PRb, Qt_=Qt_, K2_=K2_):
                    qv = Qt_[:].unsqueeze(1).broadcast_to((128, 8, 1024))
                    kv = ap_win(K2_[:, p0 * 64:], 8, 64, 1024)
                    ins = v.tensor_tensor(PRb[:], qv, kv, ALU.mult)
                    prv = PRb[:].rearrange("p a (h d) -> p a h d", d=64)
                    w = 64
                    while w > 2:
                        kb.edge(ich, v, ins)
                        ins = v.tensor_tensor(prv[:, :, :, 0:w // 2], prv[:, :, :, 0:w // 2],
                                              prv[:, :, :, w // 2:w], ALU.add)
                        w //= 2
                    kb.edge(ich, v, ins)
                    sv = Sc[:, p0 * 16:(p0 + 8) * 16].rearrange("p (a h) -> p a h", h=16)
                    return v.tensor_tensor(sv, prv[:, :, :, 0:1].squeeze(3),
                                           prv[:, :, :, 1:2].squeeze(3), ALU.add)
                ev_s_both.append(p.add(eng, f_s, waits=[ev_qk, ev_v], inc=ch))
            ev_s = ev_s_both[0]
            lu["QK2"][tp_] = list(ev_s_both)

            if tb == 0 and tt4 == 0 and "d_sc" in io:
                def f_dbg2(s):
                    return [s.dma_start(out=io["d_sc"], in_=Sc[:])]
                p.add("sync", f_dbg2, waits=[ev_s], inc="spd", n_dma=1)

            def f_exp(sc):
                return sc.activation(Am[:], Sc[:], AF.Exp, bias=cst["cl0_c"][:, 0:1],
                                     scale=0.125)
            ev_am = p.add("scalar", f_exp, waits=ev_s_both + lu["Am"], inc="act")

            def f_l(v):
                ins = v.tensor_tensor(Fold[:], Am[:, 0:128], Am[:, 128:256], ALU.add)
                kb.edge("dve_i", v, ins)
                ins = v.tensor_tensor(Fold[:, 0:64], Fold[:, 0:64], Fold[:, 64:128], ALU.add)
                kb.edge("dve_i", v, ins)
                ins = v.tensor_tensor(Fold[:, 0:32], Fold[:, 0:32], Fold[:, 32:64], ALU.add)
                kb.edge("dve_i", v, ins)
                ins = v.tensor_tensor(Lh[:], Fold[:, 0:16], Fold[:, 16:32], ALU.add)
                kb.edge("dve_i", v, ins)
                return v.reciprocal(Rh[:], Lh[:])
            ev_r = p.add("vector", f_l, waits=[ev_am], inc="dve")

            if tb == 0 and tt4 == 0 and "d_am" in io:
                def f_dbg3(s):
                    return [s.dma_start(out=io["d_am"], in_=Am[:]),
                            s.dma_start(out=io["d_lh"], in_=Lh[:])]
                p.add("sync", f_dbg3, waits=[ev_am, ev_r], inc="spd", n_dma=2)
            ev_av_both = []
            for half in range(2):
                p0 = half * 8
                eng, ch, ich = ("vector", "dve", "dve_i")
                PRb = PR if half == 0 else PR2

                def f_av(v, p0=p0, half=half, ich=ich, PRb=PRb, V2_=V2_):
                    av = (Am[:, p0 * 16:(p0 + 8) * 16]
                          .rearrange("p (a h) -> p a h", h=16)
                          .unsqueeze(3).broadcast_to((128, 8, 16, 64)))
                    vv = ap_win(V2_[:, p0 * 64:], 8, 64, 1024).rearrange(
                        "p a (h d) -> p a h d", d=64)
                    pr4 = PRb[:].rearrange("p a (h d) -> p a h d", d=64)
                    ins = v.tensor_tensor(pr4, av, vv, ALU.mult)
                    kb.edge(ich, v, ins)
                    ins = v.tensor_tensor(PRb[:, 0:4, :], PRb[:, 0:4, :], PRb[:, 4:8, :],
                                          ALU.add)
                    kb.edge(ich, v, ins)
                    ins = v.tensor_tensor(PRb[:, 0:2, :], PRb[:, 0:2, :], PRb[:, 2:4, :],
                                          ALU.add)
                    kb.edge(ich, v, ins)
                    dst = Ot if half == 0 else Ot2
                    return v.tensor_tensor(dst[:], PRb[:, 0, :], PRb[:, 1, :], ALU.add)
                ev_av_both.append(p.add(eng, f_av, waits=[ev_r], inc=ch))
            ev_av = ev_av_both[0]
            lu["V2"][tp_] = list(ev_av_both)
            lu["Am"] = list(ev_av_both)

            Ob_ = Ob[tp_]

            def f_onorm(v, Ob_=Ob_):
                ins = v.tensor_tensor(Ot[:], Ot[:], Ot2[:], ALU.add)
                kb.edge("dve_i", v, ins)
                ot3 = Ot[:].rearrange("p (h d) -> p h d", d=64)
                ob3 = Ob_[:].rearrange("p (h d) -> p h d", d=64)
                rb = Rh[:].unsqueeze(2).broadcast_to((128, 16, 64))
                return v.tensor_tensor(ob3, ot3, rb, ALU.mult)
            ev_ob = p.add("vector", f_onorm,
                          waits=ev_av_both + [lu["Ob"][tp_]], inc="dve")
            pend_tp.append((ev_ob, s0, Ob_))
            if len(pend_tp) > 1:
                emit_tp(pend_tp.pop(0))

        while pend_tp:
            emit_tp(pend_tp.pop(0))
        ev_ot_tiles = [lu["tp"][0], lu["tp"][1]]
        ev_y = None
        for wave in range(2):
            def f_op(t, wave=wave):
                last = None
                for i in range(4):
                    co = wave * 4 + i
                    for cd in range(CD):
                        last = t.matmul(bk[i][:], Wo[:, cd, co * 128:(co + 1) * 128],
                                        OT[:, cd, :], start=(cd == 0), stop=(cd == CD - 1))
                return last
            ev_op = p.add("tensor", f_op,
                          waits=ev_ot_tiles + lu["bk"] + [ev_xf, ev_wsp], inc="pe")

            def f_op_ev(v, wave=wave):
                last = None
                for i in range(4):
                    co = wave * 4 + i
                    ins = v.tensor_scalar(Yr[:, co, :], bk[i][:], bo_c[:, co:co + 1], None,
                                          ALU.add)
                    kb.edge("dve_i", v, ins)
                    last = v.tensor_tensor(Yr[:, co, :], Yr[:, co, :], Xf[:, co, :],
                                           ALU.add)
                return last
            ev_y = p.add("vector", f_op_ev, waits=[ev_op, lu["Yr"]], inc="dve")
            for i in range(4):
                lu["bk"][i] = ev_y
        lu["Xf"] = ev_y
        lu["Xb"] = ev_s
        lu["OT"] = ev_y
        lu["Am"] = lu["Am"]

        if tb == 0 and "d_y" in io:
            def f_dbg5(s):
                return [s.dma_start(out=io["d_ot8"], in_=OT[:]),
                        s.dma_start(out=io["d_y"], in_=Yr[:])]
            p.add("sync", f_dbg5, waits=[ev_y], inc="spd", n_dma=2)
        ev_apply, ev_store = emit_ln_tail(
            p, kb, cst, st, pref="l0ln", Yr=Yr, Yb=Yb, Ysq=Ysq,
            b_m=bk[0], b_s=bk[1], ps_sum=ps_ra[0:1, :],
            ps_sq=ps_rb[0:1, :], outT=x1T, t0=t0, ev_y=ev_y)
        lu["Yr"] = ev_apply
        lu["bk"][0] = st["b_m"]
        lu["bk"][1] = st["b_s"]
    return ev_store


def emit_l0_ffn(p, kb, cst, io):
    x1T, x2T = io["x1T_in"], io["x2T"]
    NCH = 24
    SC127 = 1.0 / 127.0

    Wd = kb.sb("f0_Wd", [128, NCH, 1024], BF16)
    Wg = [kb.sb(f"f0_Wg{i}", [128, CD, 128], BF16) for i in range(2)]
    Wu = [kb.sb(f"f0_Wu{i}", [128, CD, 128], BF16) for i in range(2)]
    Xf = kb.sb("f0_Xf", [128, CD, 512], F32)
    Xb = kb.sb("f0_Xb", [128, CD, 512], BF16)
    Hh = kb.sb("f0_h", [128, NCH, 512], BF16)
    Sg = [kb.sb(f"f0_sg{i}", [128, 512], BF16) for i in range(2)]
    Yr = kb.sb("f0_y", [128, CD, 512], F32)
    Yb = kb.sb("f0_yb", [128, CD, 512], BF16)
    Ysq = kb.sb("f0_ysq", [128, CD, 512], BF16)

    bk = [kb.ps(f"f0_b{i}", [128, 512], F32) for i in range(4)]
    dp = [kb.ps(f"f0_dp{i}", [128, 512], F32) for i in range(2)]
    bmt = kb.ps("f0_bm", [128, 512], F32)
    bst = kb.ps("f0_bs", [128, 512], F32)
    ps_ra = dp[0][0:1, 0:512]
    ps_rb = dp[1][0:1, 0:512]

    def f_wd(g):
        return [g.dma_start(out=Wd[:, c * 6:(c + 1) * 6, :],
                            in_=io["f0_wd"][:, :].rearrange("(c p) n -> p c n", p=128)
                            [:, c * 6:(c + 1) * 6, :])
                for c in range(4)]
    ev_wd = p.add("gpsimd", f_wd, inc="poold", n_dma=4)

    st = {}
    lu = {"Xf": None, "Xb": None, "Yr": None, "bk": [None] * 4,
          "Wg": [None, None], "Wu": [None, None], "Sg": [None, None], "Hh": None,
          "dp": [None, None]}
    ev_store = None
    ev_wbuf = {}

    for tb in range(NTB):
        t0 = tb * 512

        def f_lx(s, t0=t0):
            return [s.dma_start(out=Xf[:], in_=x1T[:, t0:t0 + 512]
                                .rearrange("(c p) n -> p c n", p=128))]
        ev_xf = p.add("sync", f_lx, waits=[lu["Xf"]], inc="spd", n_dma=1)

        def f_lxb(g, t0=t0):
            return [g.dma_start(out=Xb[:], in_=x1T[:, t0:t0 + 512]
                                .rearrange("(c p) n -> p c n", p=128))]
        ev_xb = p.add("gpsimd", f_lxb, waits=[lu["Xb"]], inc="poold", n_dma=1)

        for ch in range(NCH):
            par = ch % 2

            def f_wld(g, ch=ch, par=par):
                return [g.dma_start(out=Wg[par][:], in_=io["f0_wg"][ch]),
                        g.dma_start(out=Wu[par][:], in_=io["f0_wu"][ch])]
            ev_w = p.add("gpsimd", f_wld,
                         waits=[lu["Wg"][par], lu["Wu"][par]], inc="poold", n_dma=2)
            ev_wbuf[par] = ev_w

            b0, b1 = 2 * par, 2 * par + 1

            def f_mm(t, par=par, b0=b0, b1=b1):
                for cd in range(CD):
                    t.matmul(bk[b0][:], Wg[par][:, cd, :], Xb[:, cd, :],
                             start=(cd == 0), stop=(cd == CD - 1))
                last = None
                for cd in range(CD):
                    last = t.matmul(bk[b1][:], Wu[par][:, cd, :], Xb[:, cd, :],
                                    start=(cd == 0), stop=(cd == CD - 1))
                return last
            ev_mm = p.add("tensor", f_mm,
                          waits=[ev_w, ev_xb, lu["bk"][b0], lu["bk"][b1]], inc="pe")
            lu["Wg"][par] = ev_mm
            lu["Wu"][par] = ev_mm

            def f_silu(sc, par=par, b0=b0):
                return sc.activation(Sg[par][:], bk[b0][:], AF.Silu, scale=SC127)
            ev_silu = p.add("scalar", f_silu, waits=[ev_mm, lu["Sg"][par]], inc="act")
            lu["bk"][b0] = ev_silu

            def f_hmul(v, ch=ch, par=par, b1=b1):
                return v.scalar_tensor_tensor(Hh[:, ch, :], bk[b1][:], SC127, Sg[par][:],
                                              ALU.mult, ALU.mult)
            ev_h = p.add("vector", f_hmul, waits=[ev_silu, lu["Hh"]], inc="dve")
            lu["bk"][b1] = ev_h
            lu["Sg"][par] = ev_h
        lu["Xb"] = ev_mm if False else lu["Wg"][par]
        lu["Xb"] = ev_h
        ev_hdone = ev_h

        ev_y = None
        ev_dp = None
        for co in range(CD):
            di = co % 2

            def f_dp(t, co=co, di=di):
                last = None
                for ch in range(NCH):
                    last = t.matmul(dp[di][:], Wd[:, ch, co * 128:(co + 1) * 128],
                                    Hh[:, ch, :], start=(ch == 0),
                                    stop=(ch == NCH - 1))
                return last
            ev_dp = p.add("tensor", f_dp,
                          waits=[ev_hdone, ev_wd, ev_xf, lu["dp"][di]], inc="pe")

            def f_dp_ev(v, co=co, di=di):
                ins = v.tensor_scalar(Yr[:, co, :], dp[di][:], SC127, None, ALU.mult)
                kb.edge("dve_i", v, ins)
                return v.tensor_tensor(Yr[:, co, :], Yr[:, co, :], Xf[:, co, :],
                                       ALU.add)
            ev_y = p.add("vector", f_dp_ev,
                         waits=[ev_dp, lu["Yr"] if co < 2 else None], inc="dve")
            lu["dp"][di] = ev_y
        lu["Xf"] = ev_y
        lu["Hh"] = ev_dp

        ev_apply, ev_store = emit_ln_tail(
            p, kb, cst, st, pref="f0ln", Yr=Yr, Yb=Yb, Ysq=Ysq,
            b_m=bmt, b_s=bst, ps_sum=ps_ra,
            ps_sq=ps_rb, outT=x2T, t0=t0, ev_y=ev_y)
        lu["dp"][0] = st["ps_row"]
        lu["dp"][1] = st["ps_row"]
        if "x2bf_cc_in" in io:
            def f_bfst(g, t0=t0, ev_apply=ev_apply):
                return [g.dma_start(out=io["x2bf_cc_in"][:, t0:t0 + 512]
                                    .rearrange("(c p) n -> p c n", p=128), in_=Yr[:])]
            ev_store = p.add("gpsimd", f_bfst, waits=[ev_apply, ev_store],
                             inc="poold", n_dma=1)
            io.setdefault("x2bf_evs", []).append(ev_store)
        lu["Yr"] = ev_apply
        lu["bk"][0] = st["b_m"]
        lu["bk"][1] = st["b_s"]
    return ev_store


def emit_l1_attn(p, kb, cst, io):
    x2T = io.get("x2T_in")
    x2G = io.get("x2bf_cc_out")
    ev_g = io.get("ev_x2g")
    o_out = io["o_out"]
    NB = T_FULL // 512
    NKT = T_FULL // 128
    NCC = 4

    Wq = kb.sb("a1_Wq", [128, CD, 512], BF16)
    Wk = kb.sb("a1_Wk", [128, CD, 512], BF16)
    Wv = kb.sb("a1_Wv", [128, CD, 512], BF16)
    Xb = [kb.sb(f"a1_Xb{i}", [128, CD, 512], BF16) for i in range(2)]
    QT = kb.sb("a1_QT", [128, NCC, T_FULL], BF16)
    KT = kb.sb("a1_KT", [128, NCC, T_FULL], BF16)
    Vt = kb.sb("a1_V", [128, NKT, 8 * 65], BF16)
    Pt = [kb.sb(f"a1_P{i}", [128, 1024], BF16) for i in range(2)]
    Fsb = [kb.sb(f"a1_F{i}", [64, 1024], F32) for i in range(2)]
    frow = [kb.sb(f"a1_frow{i}", [1, 1024], F32) for i in range(2)]
    mask = kb.sb("a1_mask", [128, 896], F32)
    OT = kb.sb("a1_OT", [128, NCC, T_FULL], BF16)

    SP = [kb.ps(f"a1_s{i}", [128, 1024], F32) for i in range(2)]
    O = [kb.ps(f"a1_o{i}", [128, 512], F32) for i in range(4)]
    Sv = [SP[0][:, 0:512], SP[0][:, 512:1024], SP[1][:, 0:512], SP[1][:, 512:1024]]

    def f_wl(g):
        return [g.dma_start(out=Wq[:], in_=io["a1_wq"].rearrange("(c p) n -> p c n", p=128)),
                g.dma_start(out=Wk[:], in_=io["a1_wk"].rearrange("(c p) n -> p c n", p=128)),
                g.dma_start(out=Wv[:], in_=io["a1_wv"].rearrange("(c p) n -> p c n", p=128))]
    ev_w = p.add("gpsimd", f_wl, inc="poold", n_dma=3)

    def f_ml(s):
        return [s.dma_start(out=mask[:], in_=io["MEGAMASK"])]
    ev_mask = p.add("sync", f_ml, inc="spd", n_dma=1)

    def f_ones(v):
        return v.memset(Vt[:].rearrange("p k (h e) -> p k h e", e=65)[:, :, :, 64:65], 1.0)
    ev_vones = p.add("vector", f_ones, inc="dve")

    lu = {"Xb": [None, None], "S": [None] * 4, "O": [None] * 4, "QT": None,
          "KT": None, "Vt": ev_vones, "Pt": [None] * 2, "Fsb": None, "frow": None,
          "OT": None}
    for blk in range(NB):
        par = blk % 2
        t0 = blk * 512

        if x2G is None:
            def f_lx(g, t0=t0, par=par):
                return [g.dma_start(out=Xb[par][:], in_=x2T[:, t0:t0 + 512]
                                    .rearrange("(c p) n -> p c n", p=128))]
        else:
            half, tl = t0 // TOK, t0 % TOK
            def f_lx(g, half=half, tl=tl, par=par):
                return [g.dma_start(out=Xb[par][:],
                                    in_=x2G[half * D:(half + 1) * D, tl:tl + 512]
                                    .rearrange("(c p) n -> p c n", p=128))]
        ev_x = p.add("gpsimd", f_lx, waits=[lu["Xb"][par], ev_g], inc="poold", n_dma=1)

        def f_qmm(t, par=par):
            last = None
            for o in range(4):
                for cd in range(CD):
                    last = t.matmul(Sv[o], Wq[:, cd, o * 128:(o + 1) * 128],
                                    Xb[par][:, cd, :],
                                    start=(cd == 0), stop=(cd == CD - 1))
            return last
        ev_qmm = p.add("tensor", f_qmm, waits=[ev_w, ev_x] + lu["S"], inc="pe")

        def f_qev(sc, t0=t0):
            last = None
            for o in range(4):
                last = sc.activation(QT[:, o, t0:t0 + 512], Sv[o], AF.Identity,
                                     bias=cst["zero_c"][:, 0:1], scale=1.0)
            return last
        ev_qev = p.add("scalar", f_qev, waits=[ev_qmm, lu["QT"]], inc="act")
        for i in range(4):
            lu["S"][i] = ev_qev

        def f_kmm(t, par=par):
            last = None
            for o in range(4):
                for cd in range(CD):
                    last = t.matmul(Sv[o], Wk[:, cd, o * 128:(o + 1) * 128],
                                    Xb[par][:, cd, :],
                                    start=(cd == 0), stop=(cd == CD - 1))
            return last
        ev_kmm = p.add("tensor", f_kmm, waits=[ev_qev], inc="pe")

        def f_kev(sc, t0=t0):
            last = None
            for o in range(4):
                last = sc.activation(KT[:, o, t0:t0 + 512], Sv[o], AF.Identity,
                                     bias=cst["zero_c"][:, 0:1], scale=1.0)
            return last
        ev_kev = p.add("scalar", f_kev, waits=[ev_kmm, lu["KT"]], inc="act")
        for i in range(4):
            lu["S"][i] = ev_kev

        ev_vev = None
        for tt4 in range(4):
            kt = blk * 4 + tt4
            s0 = tt4 * 128

            def f_vmm(t, par=par, s0=s0, tt4=tt4):
                last = None
                for cd in range(CD):
                    last = t.matmul(O[tt4 % 4][:], Xb[par][:, cd, s0:s0 + 128],
                                    Wv[:, cd, :], start=(cd == 0), stop=(cd == CD - 1))
                return last
            ev_vmm = p.add("tensor", f_vmm,
                           waits=[ev_kev, lu["O"][tt4 % 4]], inc="pe")

            def f_vev(sc, kt=kt, tt4=tt4):
                vv = Vt[:].rearrange("p k (h e) -> p k h e", e=65)
                return sc.activation(vv[:, kt, :, 0:64], O[tt4 % 4][:]
                                     .rearrange("p (h d) -> p h d", d=64),
                                     AF.Identity, bias=cst["zero_c"][:, 0:1], scale=1.0)
            ev_vev = p.add("scalar", f_vev, waits=[ev_vmm, lu["Vt"]], inc="act")
            lu["O"][tt4 % 4] = ev_vev
        lu["Vt"] = ev_vev
        lu["Xb"][par] = ev_vev

    ev_nm = None
    pend_fix = []

    def emit_fixup(args):
        cc, qb, q0, ob, ev_avl = args
        fpar = (cc * NB + qb) % 2
        fsp = qb % 2

        def f_recip(v, ob=ob, fpar=fpar):
            ins = v.reciprocal(frow[fpar][0:1, 0:512], O[ob][64:65, :])
            kb.edge("dve_i", v, ins)
            return v.reciprocal(frow[fpar][0:1, 512:1024], O[ob + 1][64:65, :])
        ev_rc = p.add("vector", f_recip, waits=[ev_avl, lu["frow"]], inc="dve")

        def f_fb(t, fpar=fpar, fsp=fsp):
            t.matmul(SP[fsp][0:64, 0:512], cst["onesf_r"][0:1, 0:64],
                     frow[fpar][0:1, 0:512], start=True, stop=True)
            return t.matmul(SP[fsp][0:64, 512:1024], cst["onesf_r"][0:1, 0:64],
                            frow[fpar][0:1, 512:1024], start=True, stop=True)
        ev_fb = p.add("tensor", f_fb,
                      waits=[ev_rc, lu["S"][2 * fsp], lu["S"][2 * fsp + 1]],
                      inc="pe")

        def f_fcp(sc, fpar=fpar, fsp=fsp):
            return sc.activation(Fsb[fpar][:], SP[fsp][0:64, :], AF.Identity,
                                 bias=cst["zero_c"][0:64, 0:1], scale=1.0)
        ev_fcp = p.add("scalar", f_fcp, waits=[ev_fb, lu["Fsb"]], inc="act")
        lu["S"][2 * fsp] = ev_fcp
        lu["S"][2 * fsp + 1] = ev_fcp
        lu["frow"] = ev_fcp

        def f_norm(v, cc=cc, q0=q0, ob=ob, fpar=fpar):
            ins = v.tensor_tensor(OT[0:64, cc, q0:q0 + 512], O[ob][0:64, :],
                                  Fsb[fpar][:, 0:512], ALU.mult)
            kb.edge("dve_i", v, ins)
            return v.tensor_tensor(OT[64:128, cc, q0:q0 + 512], O[ob + 1][0:64, :],
                                   Fsb[fpar][:, 512:1024], ALU.mult)
        ev_nm2 = p.add("vector", f_norm, waits=[ev_fcp], inc="dve")
        lu["O"][ob] = ev_nm2
        lu["O"][ob + 1] = ev_nm2
        lu["Fsb"] = ev_nm2
        lu["OT"] = ev_nm2
        return ev_nm2

    for cc in range(NCC):
        for qb in range(NB):
            q0 = qb * 512
            kts = list(range(4 * qb, 4 * qb + 4)) + list(range(0, 4 * qb))
            ob = (qb % 2) * 2
            ev_exp_of = {}
            pending = None
            for ki, kt in enumerate(kts + [None]):
                if ki == 2 and pend_fix:
                    ev_nm = emit_fixup(pend_fix.pop(0))
                if kt is not None:
                    par = (ki + qb) % 2
                    diag = kt - 4 * qb

                    def f_smm(t, cc=cc, q0=q0, kt=kt, par=par):
                        t.matmul(SP[par][:, 0:512],
                                 KT[0:64, cc, kt * 128:(kt + 1) * 128],
                                 QT[0:64, cc, q0:q0 + 512], start=True, stop=True,
                                 tile_position=(0, 0))
                        return t.matmul(SP[par][:, 512:1024],
                                        KT[64:128, cc, kt * 128:(kt + 1) * 128],
                                        QT[64:128, cc, q0:q0 + 512], start=True,
                                        stop=True, tile_position=(64, 0))
                    ev_smm = p.add("tensor", f_smm,
                                   waits=[lu["S"][par * 2], lu["S"][par * 2 + 1],
                                          ev_kev, ev_mask], inc="pe")

                    ev_pre = ev_smm
                    if diag >= 0:
                        off = 384 - 128 * diag

                        def f_mask(v, par=par, off=off):
                            sp3 = SP[par].rearrange("p (h n) -> p h n", h=2)
                            mk = mask[:, off:off + 512].unsqueeze(1) \
                                .broadcast_to((128, 2, 512))
                            return v.tensor_tensor(sp3, sp3, mk, ALU.add)
                        ev_pre = p.add("vector", f_mask, waits=[ev_smm], inc="dve")

                    def f_exp(sc, par=par):
                        return sc.activation(Pt[par][:], SP[par][:], AF.Exp,
                                             bias=cst["cl1_c"][:, 0:1], scale=1.0 / 32.0)
                    ev_exp = p.add("scalar", f_exp,
                                   waits=[ev_pre, lu["Pt"][par]], inc="act")
                    ev_exp_of[kt] = ev_exp
                    lu["S"][par * 2] = ev_exp
                    lu["S"][par * 2 + 1] = ev_exp

                if pending is not None:
                    pk, ppar = pending
                    first, lastk = (pk == kts[0]), (pk == kts[-1])

                    def f_av(t, cc=cc, pk=pk, ppar=ppar, first=first, lastk=lastk, ob=ob):
                        vv = Vt[:, pk, cc * 130:cc * 130 + 130]
                        t.matmul(O[ob][0:65, :], vv[:, 0:65], Pt[ppar][:, 0:512],
                                 start=first, stop=lastk)
                        return t.matmul(O[ob + 1][0:65, :], vv[:, 65:130],
                                        Pt[ppar][:, 512:1024], start=first, stop=lastk)
                    w_extra = [lu["O"][ob], lu["O"][ob + 1]] if first else []
                    ev_av = p.add("tensor", f_av,
                                  waits=[ev_exp_of[pk]] + w_extra, inc="pe")
                    lu["Pt"][ppar] = ev_av
                pending = (kt, (ki + qb) % 2) if kt is not None else None
            pend_fix.append((cc, qb, q0, ob, ev_av))

    while pend_fix:
        ev_nm = emit_fixup(pend_fix.pop(0))

    def f_store(s):
        return [s.dma_start(out=o_out.rearrange("(c p) n -> p c n", p=128), in_=OT[:])]
    ev_st = p.add("sync", f_store, waits=[lu["OT"]], inc="spd", n_dma=1)
    return ev_st


def emit_l1_outproj(p, kb, cst, io):
    oT, x2T, x3T = io.get("oT_own"), io["x2T_own"], io["x3T"]
    oG = io.get("o_cc_out")
    ev_og = io.get("ev_og")

    Wo = kb.sb("a2_Wo", [128, CD, 1024], BF16)
    OTb_ = [kb.sb(f"a2_OT_{i}", [128, CD, 512], BF16) for i in range(2)]
    if oG is not None:
        OT2_ = [kb.sb(f"a2_OT2_{i}", [128, CD, 512], BF16) for i in range(2)]
        Msel = kb.sb("a2_msel", [128, 2], F32)
    Xf_ = [kb.sb(f"a2_Xf_{i}", [128, CD, 512], F32) for i in range(2)]
    Yr_ = [kb.sb(f"a2_y_{i}", [128, CD, 512], F32) for i in range(2)]
    Yb = kb.sb("a2_yb", [128, CD, 512], BF16)
    Ysq = kb.sb("a2_ysq", [128, CD, 512], BF16)
    bk = [kb.ps(f"a2_b{i}", [128, 512], F32) for i in range(4)]
    bm2 = kb.ps("a2_bm", [128, 512], F32)
    bs2 = kb.ps("a2_bs", [128, 512], F32)
    ps_ra = bm2[0:1, 0:512]
    ps_rb = bs2[0:1, 0:512]

    def f_wl(g):
        out = [g.dma_start(out=Wo[:], in_=io["a1_wo"].rearrange("(c p) n -> p c n", p=128))]
        if oG is not None:
            out.append(g.dma_start(out=Msel[:], in_=io["msel"]))
        return out
    ev_w = p.add("gpsimd", f_wl, inc="poold", n_dma=2 if oG is not None else 1)

    st = {}
    lu = {"Xf": [None, None], "OTb": [None, None], "Yr": [None, None],
          "bk": [None] * 4}
    ev_store = None
    for tb in range(NTB):
        t0 = tb * 512
        tpar = tb % 2
        OTb, Xf, Yr = OTb_[tpar], Xf_[tpar], Yr_[tpar]
        if oG is not None:
            OT2 = OT2_[tpar]

        def f_lx(s, t0=t0, Xf=Xf):
            return [s.dma_start(out=Xf[:], in_=x2T[:, t0:t0 + 512]
                                .rearrange("(c p) n -> p c n", p=128))]
        ev_xf = p.add("sync", f_lx, waits=[lu["Xf"][tpar]], inc="spd", n_dma=1)

        if oG is None:
            def f_lo(g, t0=t0, OTb=OTb):
                return [g.dma_start(out=OTb[:], in_=oT[:, t0:t0 + 512]
                                    .rearrange("(c p) n -> p c n", p=128))]
            ev_o = p.add("gpsimd", f_lo, waits=[lu["OTb"][tpar]], inc="poold", n_dma=1)
        else:
            def f_lo(g, t0=t0, OTb=OTb, OT2=OT2):
                return [g.dma_start(out=OTb[:], in_=oG[:, t0:t0 + 512]
                                    .rearrange("(c p) n -> p c n", p=128)),
                        g.dma_start(out=OT2[:], in_=oG[:, TOK + t0:TOK + t0 + 512]
                                    .rearrange("(c p) n -> p c n", p=128))]
            ev_o2 = p.add("gpsimd", f_lo, waits=[lu["OTb"][tpar], ev_og],
                          inc="poold", n_dma=2)

            def f_sel(v, OTb=OTb, OT2=OT2):
                ins = v.tensor_scalar(OTb[:], OTb[:], Msel[:, 0:1], None, ALU.mult)
                kb.edge("dve_i", v, ins)
                return v.scalar_tensor_tensor(OTb[:], OT2[:], Msel[:, 1:2], OTb[:],
                                              ALU.mult, ALU.add)
            ev_o = p.add("vector", f_sel, waits=[ev_o2, ev_w], inc="dve")

        ev_y = None
        for wave in range(2):
            def f_op(t, wave=wave, OTb=OTb):
                last = None
                for i in range(4):
                    co = wave * 4 + i
                    for cd in range(CD):
                        last = t.matmul(bk[i][:], Wo[:, cd, co * 128:(co + 1) * 128],
                                        OTb[:, cd, :], start=(cd == 0),
                                        stop=(cd == CD - 1))
                return last
            ev_op = p.add("tensor", f_op,
                          waits=[ev_w, ev_o, ev_xf] + lu["bk"], inc="pe")

            def f_op_ev(v, wave=wave, Yr=Yr, Xf=Xf):
                last = None
                for i in range(4):
                    co = wave * 4 + i
                    last = v.tensor_tensor(Yr[:, co, :], bk[i][:], Xf[:, co, :], ALU.add)
                return last
            ev_y = p.add("vector", f_op_ev, waits=[ev_op, lu["Yr"][tpar]], inc="dve")
            for i in range(4):
                lu["bk"][i] = ev_y
        lu["Xf"][tpar] = ev_y
        lu["OTb"][tpar] = ev_y

        ev_apply, ev_store = emit_ln_tail(
            p, kb, cst, st, pref="a2ln", Yr=Yr, Yb=Yb, Ysq=Ysq,
            b_m=bm2, b_s=bs2, ps_sum=ps_ra,
            ps_sq=ps_rb, outT=x3T, t0=t0, ev_y=ev_y)
        lu["Yr"][tpar] = ev_apply
    return ev_store


def emit_l1_ffn(p, kb, cst, io):
    x3T = io["x3T_in"]
    out_tok = io["out_tok"]
    g1h, g2h = io["g1_hbm"], io["g2_hbm"]
    NH = 32

    Hbuf = kb.sb("f1_h", [128, NH, 2048], BF16)
    arena = kb.sb("f1_arena", [128, 16384], BF16)
    Xb = arena[:, 0:16384].rearrange("p (c n) -> p c n", n=2048)
    Yr = arena[:, 0:8192].bitcast(F32).rearrange("p (c n) -> p c n", n=512)
    Yb = arena[:, 8192:12288].rearrange("p (c n) -> p c n", n=512)
    Ysq = arena[:, 12288:16384].rearrange("p (c n) -> p c n", n=512)
    arena2 = kb.sb("f1_arena2", [128, 4096], BF16)
    W1b = [arena2[:, i * 1024:(i + 1) * 1024].rearrange("p (c n) -> p c n", n=128)
           for i in range(2)]
    W2b = [arena2[:, 2048 + i * 1024:2048 + (i + 1) * 1024]
           .rearrange("p (c n) -> p c n", n=128) for i in range(2)]
    Hw = [kb.sb(f"f1_hw{i}", [128, NH, 128], BF16) for i in range(2)]
    Xf = kb.sb("f1_Xf", [128, CD, 512], F32)
    Sg = kb.sb("f1_sg", [128, 512], BF16)
    Gt = [kb.sb(f"f1_g{i}", [128, 512], BF16) for i in range(4)]
    Sc1 = kb.sb("f1_s1", [128, NH], F32)
    Sc2 = kb.sb("f1_s2", [128, NH], F32)
    Sh1 = kb.sb("f1_sh1", [128, NH], F32)
    Sh2 = kb.sb("f1_sh2", [128, NH], F32)
    Sc3 = kb.sb("f1_s3", [128, CD], F32)
    eyef = kb.sb("f1_eyef", [128, 128], F32)
    XTok = arena2[:, 0:2048].bitcast(F32)
    kb.cache["f1ln"] = (arena2[0:1, 2048:3072].bitcast(F32),
                        arena2[0:1, 3072:4096].bitcast(F32))

    bk = [kb.ps(f"f1_b{i}", [128, 512], F32) for i in range(4)]
    tpf = [kb.ps(f"f1_tp{i}", [128, 128], F32) for i in range(2)]
    ps_ra = kb.ps("f1_rowa", [1, 512], F32)
    ps_rb = kb.ps("f1_rowb", [1, 512], F32)

    def f_scl(s):
        out = []
        for (tile, name, n) in [(Sc1, "f1_s1v", NH), (Sc2, "f1_s2v", NH),
                                (Sh1, "f1_sh1v", NH), (Sh2, "f1_sh2v", NH),
                                (Sc3, "f1_s3v", CD)]:
            for ch in range(n):
                out.append(s.dma_start(out=tile[:, ch:ch + 1],
                                       in_=io[name][ch * 128:(ch + 1) * 128, None]))
        out.append(s.dma_start(out=eyef[:], in_=io["EYE_F"]))
        return out
    ev_scl = p.add("sync", f_scl, inc="spd", n_dma=4 * NH + CD + 1)

    def f_xb(g):
        return [g.dma_start(out=Xb, in_=x3T.rearrange("(c p) n -> p c n", p=128))]
    ev_xb = p.add("gpsimd", f_xb, inc="poold", n_dma=1)

    lu = {"W1b": [None, None], "W2b": [None, None], "Hw": [None, None],
          "bk": [None] * 4, "Sg": None, "Gt": [None] * 4, "Hbuf": None,
          "Xf": None, "Yr": None, "tpf": [None, None], "XTok": [None] * 8}
    st = {}

    ev_h = None
    ev_mm = None
    for ch in range(NH):
        par = ch % 2

        def f_wl(g, ch=ch, par=par):
            return [g.dma_start(out=W1b[par], in_=io["f1_w1"][ch]),
                    g.dma_start(out=W2b[par], in_=io["f1_w2"][ch])]
        ev_w = p.add("gpsimd", f_wl, waits=[lu["W1b"][par], lu["W2b"][par]],
                     inc="poold", n_dma=2)

        for tb in range(NTB):
            def f_mm(t, par=par, tb=tb):
                last = None
                for cd in range(CD):
                    t.matmul(bk[0][:], W1b[par][:, cd, :],
                             Xb[:, cd, tb * 512:(tb + 1) * 512],
                             start=(cd == 0), stop=(cd == CD - 1))
                for cd in range(CD):
                    last = t.matmul(bk[1][:], W2b[par][:, cd, :],
                                    Xb[:, cd, tb * 512:(tb + 1) * 512],
                                    start=(cd == 0), stop=(cd == CD - 1))
                return last
            ev_mm = p.add("tensor", f_mm,
                          waits=[ev_w, ev_xb, lu["bk"][0], lu["bk"][1]], inc="pe")

            def f_pre(v, ch=ch):
                return v.tensor_scalar(Sg[:], bk[0][:], Sc1[:, ch:ch + 1], None, ALU.mult)
            ev_pre = p.add("vector", f_pre, waits=[ev_mm, lu["Sg"], ev_scl], inc="dve")

            def f_silu(sc):
                return sc.activation(Sg[:], Sg[:], AF.Silu)
            ev_silu = p.add("scalar", f_silu, waits=[ev_pre], inc="act")
            lu["bk"][0] = ev_silu

            def f_hm(v, ch=ch, tb=tb):
                ins = v.tensor_scalar(Gt[0][:], bk[1][:], Sc2[:, ch:ch + 1], None, ALU.mult)
                kb.edge("dve_i", v, ins)
                return v.tensor_tensor(Hbuf[:, ch, tb * 512:(tb + 1) * 512], Gt[0][:],
                                       Sg[:], ALU.mult)
            ev_h = p.add("vector", f_hm,
                         waits=[ev_silu, lu["Gt"][0], lu["Hbuf"]], inc="dve")
            lu["bk"][1] = ev_h
            lu["Sg"] = ev_h
            lu["Gt"][0] = ev_h
        lu["W1b"][par] = ev_mm
        lu["W2b"][par] = ev_mm
    ev_hdone = ev_h

    def matmul_4096(src_ev, wname, scol, dsth):
        ev_s = None
        for co in range(NH):
            par = co % 2

            def f_wl(g, co=co, par=par):
                return [g.dma_start(out=Hw[par][:], in_=io[wname][co])]
            ev_w = p.add("gpsimd", f_wl, waits=[lu["Hw"][par]], inc="poold", n_dma=1)

            for tb in range(NTB):
                bki = tb

                def f_mm(t, par=par, tb=tb, bki=bki):
                    last = None
                    for ci in range(NH):
                        last = t.matmul(bk[bki][:], Hw[par][:, ci, :],
                                        Hbuf[:, ci, tb * 512:(tb + 1) * 512],
                                        start=(ci == 0), stop=(ci == NH - 1))
                    return last
                ev_mm2 = p.add("tensor", f_mm,
                               waits=[ev_w, src_ev, lu["bk"][bki]], inc="pe")

                def f_pre(v, co=co, bki=bki):
                    return v.tensor_scalar(Gt[bki][:], bk[bki][:], scol[:, co:co + 1],
                                           None, ALU.mult)
                ev_pre = p.add("vector", f_pre,
                               waits=[ev_mm2, lu["Gt"][bki], ev_scl], inc="dve")

                def f_gelu(sc, bki=bki):
                    return sc.activation(Gt[bki][:], Gt[bki][:], AF.Gelu)
                ev_g = p.add("scalar", f_gelu, waits=[ev_pre], inc="act")
                lu["bk"][bki] = ev_g

                def f_st(s, co=co, tb=tb, bki=bki):
                    return [s.dma_start(out=dsth[co * 128:(co + 1) * 128,
                                                 tb * 512:(tb + 1) * 512],
                                        in_=Gt[bki][:])]
                ev_s = p.add("sync", f_st, waits=[ev_g], inc="spd", n_dma=1)
                lu["Gt"][bki] = ev_s
            lu["Hw"][par] = ev_mm2
        return ev_s

    ev_g1 = matmul_4096(ev_hdone, "f1_h1", Sh1, g1h)

    def f_lg1(g):
        return [g.dma_start(out=Hbuf[:, c * 8:(c + 1) * 8, :],
                            in_=g1h.rearrange("(c p) n -> p c n", p=128)
                            [:, c * 8:(c + 1) * 8, :]) for c in range(4)]
    ev_lg1 = p.add("gpsimd", f_lg1, waits=[ev_g1], inc="poold", n_dma=4)

    ev_g2 = matmul_4096(ev_lg1, "f1_h2", Sh2, g2h)

    def f_lg2(g):
        return [g.dma_start(out=Hbuf[:, c * 8:(c + 1) * 8, :],
                            in_=g2h.rearrange("(c p) n -> p c n", p=128)
                            [:, c * 8:(c + 1) * 8, :]) for c in range(4)]
    ev_lg2 = p.add("gpsimd", f_lg2, waits=[ev_g2], inc="poold", n_dma=4)

    ev_fin = None
    lu["bk"] = [None] * 4
    lu["Yr"] = ev_hdone
    lu["Hw"] = [lu["Hw"][0], lu["Hw"][1]]
    for tb in range(NTB):
        t0 = tb * 512

        def f_lx(s, t0=t0):
            return [s.dma_start(out=Xf[:], in_=x3T[:, t0:t0 + 512]
                                .rearrange("(c p) n -> p c n", p=128))]
        ev_xf = p.add("sync", f_lx, waits=[lu["Xf"]], inc="spd", n_dma=1)

        ev_y = None
        for co in range(CD):
            par = co % 2

            def f_w3(g, co=co, par=par, tb=tb):
                return [g.dma_start(out=Hw[par][:], in_=io["f1_w3"][co])]
            ev_w3 = p.add("gpsimd", f_w3, waits=[lu["Hw"][par]], inc="poold", n_dma=1)

            def f_dp(t, par=par, tb=tb):
                last = None
                for ci in range(NH):
                    last = t.matmul(bk[par][:], Hw[par][:, ci, :],
                                    Hbuf[:, ci, tb * 512:(tb + 1) * 512],
                                    start=(ci == 0), stop=(ci == NH - 1))
                return last
            ev_dp = p.add("tensor", f_dp,
                          waits=[ev_w3, ev_lg2, ev_xf, lu["bk"][par]], inc="pe")
            lu["Hw"][par] = ev_dp

            def f_ev(v, co=co, par=par):
                ins = v.tensor_scalar(Yr[:, co, :], bk[par][:], Sc3[:, co:co + 1],
                                      None, ALU.mult)
                kb.edge("dve_i", v, ins)
                return v.tensor_tensor(Yr[:, co, :], Yr[:, co, :], Xf[:, co, :], ALU.add)
            ev_y = p.add("vector", f_ev, waits=[ev_dp, lu["Yr"], ev_scl], inc="dve")
            lu["bk"][par] = ev_y
        lu["Xf"] = ev_y
        lu["Yr"] = None

        ev_apply, _ = emit_ln_tail(
            p, kb, cst, st, pref="f1ln", Yr=Yr, Yb=Yb, Ysq=Ysq,
            b_m=bk[2], b_s=bk[3], ps_sum=ps_ra[0:1, :],
            ps_sq=ps_rb[0:1, :], outT=None, t0=t0, ev_y=ev_y)
        lu["Yr"] = ev_apply

        for tt in range(4):
            for cd in range(CD):
                tp = tpf[cd % 2]

                def f_tp(t, tt=tt, cd=cd, tp=tp):
                    return t.transpose(tp[:], Yr[:, cd, tt * 128:(tt + 1) * 128], eyef[:])
                ev_t = p.add("tensor", f_tp,
                             waits=[ev_apply, ev_scl, lu["tpf"][cd % 2]], inc="pe")

                def f_te(sc, cd=cd, tp=tp):
                    return sc.activation(XTok[:, cd * 128:(cd + 1) * 128], tp[:],
                                         AF.Identity, bias=cst["zero_c"][:, 0:1],
                                         scale=1.0)
                ev_te = p.add("scalar", f_te,
                              waits=[ev_t, lu["XTok"][cd]], inc="act")
                lu["tpf"][cd % 2] = ev_te

            def f_sto(s, t0=t0, tt=tt):
                return [s.dma_start(out=out_tok[t0 + tt * 128:t0 + (tt + 1) * 128, :],
                                    in_=XTok)]
            ev_fin = p.add("sync", f_sto,
                           waits=[lu["tpf"][0], lu["tpf"][1]], inc="spd", n_dma=1)
            for cd in range(CD):
                lu["XTok"][cd] = ev_fin
    return ev_fin


def build_l0attn_nc():
    nc = bass.Bass("TRN2", target_bir_lowering=False, detect_race_conditions=False)
    io = {
        "xT": nc.dram_tensor("xT", [D, TOK], F32, kind="ExternalInput").ap(),
        "x1T": nc.dram_tensor("x1T", [D, TOK], F32, kind="ExternalOutput").ap(),
        "l0_wq": nc.dram_tensor("l0_wq", [D, D], F32, kind="ExternalInput").ap(),
        "l0_bq": nc.dram_tensor("l0_bq", [D], F32, kind="ExternalInput").ap(),
        "l0_wd": nc.dram_tensor("l0_wd", [D, 256], F32, kind="ExternalInput").ap(),
        "l0_bd": nc.dram_tensor("l0_bd", [256], F32, kind="ExternalInput").ap(),
        "l0_wu": nc.dram_tensor("l0_wu", [256, 2048], F32, kind="ExternalInput").ap(),
        "l0_bu": nc.dram_tensor("l0_bu", [2048], F32, kind="ExternalInput").ap(),
        "l0_wo": nc.dram_tensor("l0_wo", [D, D], F32, kind="ExternalInput").ap(),
        "l0_bo": nc.dram_tensor("l0_bo", [D], F32, kind="ExternalInput").ap(),
        "EYE_BF": nc.dram_tensor("EYE_BF", [128, 128], BF16, kind="ExternalInput").ap(),
    }
    import os
    if os.environ.get("KDEBUG"):
        for nm, shape, dt in [("d_kvl", [128, 2, 512], BF16), ("d_qt", [128, 1024], BF16),
                              ("d_k2", [128, 2048], BF16), ("d_v2", [128, 2048], BF16),
                              ("d_sc", [128, 256], F32), ("d_am", [128, 256], BF16),
                              ("d_lh", [128, 16], F32), ("d_ob", [128, 1024], BF16),
                              ("d_ot8", [128, CD, 512], BF16), ("d_y", [128, CD, 512], F32)]:
            io[nm] = nc.dram_tensor(nm, shape, dt, kind="ExternalOutput").ap()
    kb = KB(nc)
    p = Prog()
    cst = emit_consts(p, kb)
    ev_store = emit_l0_attn(p, kb, cst, io)

    def f_fin(s):
        return None
    p.add("sync", lambda s: None, waits=[ev_store])
    kb.realize(p)
    return nc


def l0attn_host_inputs(inputs):
    wu = np.ascontiguousarray(inputs["L0_attn_Wup"])
    bu = np.ascontiguousarray(inputs["L0_attn_bup"])
    perm = np.empty(2048, dtype=np.int64)
    g = np.arange(16)
    d = np.arange(64)
    perm[(g[:, None] * 64 + d[None, :]).ravel()] = (g[:, None] * 128 + d[None, :]).ravel()
    perm[(1024 + g[:, None] * 64 + d[None, :]).ravel()] = \
        (g[:, None] * 128 + 64 + d[None, :]).ravel()
    cc = host_constants()
    return {
        "l0_wq": np.ascontiguousarray(inputs["L0_attn_Wq"]),
        "l0_bq": np.ascontiguousarray(inputs["L0_attn_bq"]),
        "l0_wd": np.ascontiguousarray(inputs["L0_attn_Wdown"]),
        "l0_bd": np.ascontiguousarray(inputs["L0_attn_bdown"]),
        "l0_wu": np.ascontiguousarray(wu[:, perm]),
        "l0_bu": np.ascontiguousarray(bu[perm]),
        "l0_wo": np.ascontiguousarray(inputs["L0_attn_Wout"]),
        "l0_bo": np.ascontiguousarray(inputs["L0_attn_bout"]),
        "EYE_BF": cc["EYE_BF"],
    }


def run_l0attn(inputs, x_cur):
    shared = l0attn_host_inputs(inputs)
    nc = build_l0attn_nc()
    in_maps = []
    for c in range(N_CORES):
        b, h = c // 2, c % 2
        xT = np.ascontiguousarray(x_cur[b, h * TOK:(h + 1) * TOK, :].T)
        m = dict(shared)
        m["xT"] = xT
        in_maps.append(m)
    res = run_bass_kernel_spmd(nc, in_maps, core_ids=list(range(N_CORES)))
    out = np.empty((B, T_FULL, D), dtype=np.float32)
    for c in range(N_CORES):
        b, h = c // 2, c % 2
        out[b, h * TOK:(h + 1) * TOK, :] = res.results[c]["x1T"].T
    return out


def build_l0ffn_nc():
    nc = bass.Bass("TRN2", target_bir_lowering=False, detect_race_conditions=False)
    io = {
        "x1T_in": nc.dram_tensor("x1T_in", [D, TOK], F32, kind="ExternalInput").ap(),
        "x2T": nc.dram_tensor("x2T", [D, TOK], F32, kind="ExternalOutput").ap(),
        "f0_wg": nc.dram_tensor("f0_wg", [24, 128, CD, 128], I32, kind="ExternalInput").ap(),
        "f0_wu": nc.dram_tensor("f0_wu", [24, 128, CD, 128], I32, kind="ExternalInput").ap(),
        "f0_wd": nc.dram_tensor("f0_wd", [3072, D], I32, kind="ExternalInput").ap(),
    }
    kb = KB(nc)
    p = Prog()
    cst = emit_consts(p, kb)
    ev_store = emit_l0_ffn(p, kb, cst, io)
    p.add("sync", lambda s: None, waits=[ev_store])
    kb.realize(p)
    return nc


def l0ffn_host_inputs(inputs):
    return {
        "f0_wg": chunk_major(np.ascontiguousarray(inputs["L0_ffn_w_gate"].T), 24),
        "f0_wu": chunk_major(np.ascontiguousarray(inputs["L0_ffn_w_up"].T), 24),
        "f0_wd": np.ascontiguousarray(inputs["L0_ffn_w_down"].T),
    }


def run_l0ffn(inputs, x1):
    shared = l0ffn_host_inputs(inputs)
    nc = build_l0ffn_nc()
    in_maps = []
    for c in range(N_CORES):
        b, h = c // 2, c % 2
        m = dict(shared)
        m["x1T_in"] = np.ascontiguousarray(x1[b, h * TOK:(h + 1) * TOK, :].T)
        in_maps.append(m)
    res = run_bass_kernel_spmd(nc, in_maps, core_ids=list(range(N_CORES)))
    out = np.empty((B, T_FULL, D), dtype=np.float32)
    for c in range(N_CORES):
        b, h = c // 2, c % 2
        out[b, h * TOK:(h + 1) * TOK, :] = res.results[c]["x2T"].T
    return out


def build_l1attn_nc():
    nc = bass.Bass("TRN2", target_bir_lowering=False, detect_race_conditions=False)
    io = {
        "x2T_in": nc.dram_tensor("x2T_in", [D, T_FULL], F32, kind="ExternalInput").ap(),
        "o_out": nc.dram_tensor("o_out", [512, T_FULL], BF16, kind="ExternalOutput").ap(),
        "a1_wq": nc.dram_tensor("a1_wq", [D, 512], F32, kind="ExternalInput").ap(),
        "a1_wk": nc.dram_tensor("a1_wk", [D, 512], F32, kind="ExternalInput").ap(),
        "a1_wv": nc.dram_tensor("a1_wv", [D, 512], F32, kind="ExternalInput").ap(),
        "MEGAMASK": nc.dram_tensor("MEGAMASK", [128, 896], F32, kind="ExternalInput").ap(),
    }
    kb = KB(nc)
    p = Prog()
    cst = emit_consts(p, kb)
    ev = emit_l1_attn(p, kb, cst, io)
    p.add("sync", lambda s: None, waits=[ev])
    kb.realize(p)
    return nc


def build_l1out_nc():
    nc = bass.Bass("TRN2", target_bir_lowering=False, detect_race_conditions=False)
    io = {
        "oT_own": nc.dram_tensor("oT_own", [D, TOK], BF16, kind="ExternalInput").ap(),
        "x2T_own": nc.dram_tensor("x2T_own", [D, TOK], F32, kind="ExternalInput").ap(),
        "x3T": nc.dram_tensor("x3T", [D, TOK], F32, kind="ExternalOutput").ap(),
        "a1_wo": nc.dram_tensor("a1_wo", [D, D], F32, kind="ExternalInput").ap(),
    }
    kb = KB(nc)
    p = Prog()
    cst = emit_consts(p, kb)
    ev = emit_l1_outproj(p, kb, cst, io)
    p.add("sync", lambda s: None, waits=[ev])
    kb.realize(p)
    return nc


def l1_head_weight_slices(inputs):
    wqkv = inputs["L1_attn_w_qkv"]
    out = {}
    for hc in range(2):
        cols_q, cols_k, cols_v = [], [], []
        for h in range(8 * hc, 8 * hc + 8):
            cols_q += list(range(h * 192, h * 192 + 64))
            cols_k += list(range(h * 192 + 64, h * 192 + 128))
            cols_v += list(range(h * 192 + 128, h * 192 + 192))
        out[hc] = (np.ascontiguousarray(wqkv[:, cols_q]),
                   np.ascontiguousarray(wqkv[:, cols_k]),
                   np.ascontiguousarray(wqkv[:, cols_v]))
    return out


def run_l1attn(inputs, x2):
    cc = host_constants()
    wslices = l1_head_weight_slices(inputs)
    nc = build_l1attn_nc()
    in_maps = []
    x2T = [np.ascontiguousarray(x2[b].T) for b in range(B)]
    for c in range(N_CORES):
        b, hc = c // 2, c % 2
        wq, wk, wv = wslices[hc]
        in_maps.append(dict(x2T_in=x2T[b], a1_wq=wq, a1_wk=wk, a1_wv=wv,
                            MEGAMASK=cc["MEGAMASK"]))
    res = run_bass_kernel_spmd(nc, in_maps, core_ids=list(range(N_CORES)))
    o_full = []
    for b in range(B):
        o_full.append(np.concatenate([res.results[2 * b]["o_out"],
                                      res.results[2 * b + 1]["o_out"]], axis=0))
    return o_full


def run_l1out(inputs, x2, o_full):
    nc = build_l1out_nc()
    wo = np.ascontiguousarray(inputs["L1_attn_w_o"])
    in_maps = []
    for c in range(N_CORES):
        b, h = c // 2, c % 2
        in_maps.append(dict(
            oT_own=np.ascontiguousarray(o_full[b][:, h * TOK:(h + 1) * TOK]),
            x2T_own=np.ascontiguousarray(x2[b, h * TOK:(h + 1) * TOK, :].T),
            a1_wo=wo))
    res = run_bass_kernel_spmd(nc, in_maps, core_ids=list(range(N_CORES)))
    out = np.empty((B, T_FULL, D), dtype=np.float32)
    for c in range(N_CORES):
        b, h = c // 2, c % 2
        out[b, h * TOK:(h + 1) * TOK, :] = res.results[c]["x3T"].T
    return out


def build_l1ffn_nc():
    nc = bass.Bass("TRN2", target_bir_lowering=False, detect_race_conditions=False)
    io = {
        "x3T_in": nc.dram_tensor("x3T_in", [D, TOK], F32, kind="ExternalInput").ap(),
        "out_tok": nc.dram_tensor("out_tok", [TOK, D], F32, kind="ExternalOutput").ap(),
        "g1_hbm": nc.dram_tensor("g1_hbm", [4096, TOK], BF16, kind="Internal").ap(),
        "g2_hbm": nc.dram_tensor("g2_hbm", [4096, TOK], BF16, kind="Internal").ap(),
        "f1_w1": nc.dram_tensor("f1_w1", [32, 128, CD, 128], F32, kind="ExternalInput").ap(),
        "f1_w2": nc.dram_tensor("f1_w2", [32, 128, CD, 128], F32, kind="ExternalInput").ap(),
        "f1_h1": nc.dram_tensor("f1_h1", [32, 128, 32, 128], F32, kind="ExternalInput").ap(),
        "f1_h2": nc.dram_tensor("f1_h2", [32, 128, 32, 128], F32, kind="ExternalInput").ap(),
        "f1_w3": nc.dram_tensor("f1_w3", [8, 128, 32, 128], F32, kind="ExternalInput").ap(),
        "f1_s1v": nc.dram_tensor("f1_s1v", [4096], F32, kind="ExternalInput").ap(),
        "f1_s2v": nc.dram_tensor("f1_s2v", [4096], F32, kind="ExternalInput").ap(),
        "f1_sh1v": nc.dram_tensor("f1_sh1v", [4096], F32, kind="ExternalInput").ap(),
        "f1_sh2v": nc.dram_tensor("f1_sh2v", [4096], F32, kind="ExternalInput").ap(),
        "f1_s3v": nc.dram_tensor("f1_s3v", [D], F32, kind="ExternalInput").ap(),
        "EYE_F": nc.dram_tensor("EYE_F", [128, 128], F32, kind="ExternalInput").ap(),
    }
    kb = KB(nc)
    p = Prog()
    cst = emit_consts(p, kb)
    ev = emit_l1_ffn(p, kb, cst, io)
    p.add("sync", lambda s: None, waits=[ev])
    kb.realize(p)
    return nc


def chunk_major(wT, n_out_chunks):
    K, N = wT.shape
    return np.ascontiguousarray(
        wT.reshape(K // 128, 128, n_out_chunks, N // n_out_chunks)
        .transpose(2, 1, 0, 3))


def l1ffn_host_inputs(inputs):
    cc = host_constants()
    return {
        "f1_w1": chunk_major(np.ascontiguousarray(inputs["L1_ffn_w1_w"].T), 32),
        "f1_w2": chunk_major(np.ascontiguousarray(inputs["L1_ffn_w2_w"].T), 32),
        "f1_h1": chunk_major(np.ascontiguousarray(inputs["L1_ffn_h1_w"].T), 32),
        "f1_h2": chunk_major(np.ascontiguousarray(inputs["L1_ffn_h2_w"].T), 32),
        "f1_w3": chunk_major(np.ascontiguousarray(inputs["L1_ffn_w3_w"].T), 8),
        "f1_s1v": np.ascontiguousarray(inputs["L1_ffn_w1_s"]),
        "f1_s2v": np.ascontiguousarray(inputs["L1_ffn_w2_s"]),
        "f1_sh1v": np.ascontiguousarray(inputs["L1_ffn_h1_s"]),
        "f1_sh2v": np.ascontiguousarray(inputs["L1_ffn_h2_s"]),
        "f1_s3v": np.ascontiguousarray(inputs["L1_ffn_w3_s"]),
        "EYE_F": cc["EYE_F"],
    }


def run_l1ffn(inputs, x3):
    shared = l1ffn_host_inputs(inputs)
    nc = build_l1ffn_nc()
    in_maps = []
    for c in range(N_CORES):
        b, h = c // 2, c % 2
        m = dict(shared)
        m["x3T_in"] = np.ascontiguousarray(x3[b, h * TOK:(h + 1) * TOK, :].T)
        in_maps.append(m)
    res = run_bass_kernel_spmd(nc, in_maps, core_ids=list(range(N_CORES)))
    out = np.empty((B, T_FULL, D), dtype=np.float32)
    for c in range(N_CORES):
        b, h = c // 2, c % 2
        out[b, h * TOK:(h + 1) * TOK, :] = res.results[c]["out_tok"]
    return out


PAIRS = [[0, 1], [2, 3], [4, 5], [6, 7]]


def build_fused_nc():
    nc = bass.Bass("TRN2", target_bir_lowering=False, detect_race_conditions=False)
    ext_f32 = lambda n, s: nc.dram_tensor(n, s, F32, kind="ExternalInput").ap()
    io = {
        "xT": ext_f32("xT", [D, TOK]),
        "l0_wq": ext_f32("l0_wq", [D, D]), "l0_bq": ext_f32("l0_bq", [D]),
        "l0_wd": ext_f32("l0_wd", [D, 256]), "l0_bd": ext_f32("l0_bd", [256]),
        "l0_wu": ext_f32("l0_wu", [256, 2048]), "l0_bu": ext_f32("l0_bu", [2048]),
        "l0_wo": ext_f32("l0_wo", [D, D]), "l0_bo": ext_f32("l0_bo", [D]),
        "EYE_BF": nc.dram_tensor("EYE_BF", [128, 128], BF16, kind="ExternalInput").ap(),
        "f0_wg": nc.dram_tensor("f0_wg", [24, 128, CD, 128], I32, kind="ExternalInput").ap(),
        "f0_wu": nc.dram_tensor("f0_wu", [24, 128, CD, 128], I32, kind="ExternalInput").ap(),
        "f0_wd": nc.dram_tensor("f0_wd", [3072, D], I32, kind="ExternalInput").ap(),
        "a1_wq": ext_f32("a1_wq", [D, 512]), "a1_wk": ext_f32("a1_wk", [D, 512]),
        "a1_wv": ext_f32("a1_wv", [D, 512]), "a1_wo": ext_f32("a1_wo", [D, D]),
        "MEGAMASK": ext_f32("MEGAMASK", [128, 896]),
        "msel": ext_f32("msel", [128, 2]),
        "f1_w1": nc.dram_tensor("f1_w1", [32, 128, CD, 128], F32, kind="ExternalInput").ap(), "f1_w2": nc.dram_tensor("f1_w2", [32, 128, CD, 128], F32, kind="ExternalInput").ap(),
        "f1_h1": nc.dram_tensor("f1_h1", [32, 128, 32, 128], F32, kind="ExternalInput").ap(),
        "f1_h2": nc.dram_tensor("f1_h2", [32, 128, 32, 128], F32, kind="ExternalInput").ap(),
        "f1_w3": nc.dram_tensor("f1_w3", [8, 128, 32, 128], F32, kind="ExternalInput").ap(),
        "f1_s1v": ext_f32("f1_s1v", [4096]), "f1_s2v": ext_f32("f1_s2v", [4096]),
        "f1_sh1v": ext_f32("f1_sh1v", [4096]), "f1_sh2v": ext_f32("f1_sh2v", [4096]),
        "f1_s3v": ext_f32("f1_s3v", [D]),
        "EYE_F": ext_f32("EYE_F", [128, 128]),
        "out_tok": nc.dram_tensor("out_tok", [TOK, D], F32, kind="ExternalOutput").ap(),
        "x1T": nc.dram_tensor("x1T", [D, TOK], F32, kind="Internal").ap(),
        "x3T": nc.dram_tensor("x3T", [D, TOK], F32, kind="Internal").ap(),
        "g1_hbm": nc.dram_tensor("g1_hbm", [4096, TOK], BF16, kind="Internal").ap(),
        "g2_hbm": nc.dram_tensor("g2_hbm", [4096, TOK], BF16, kind="Internal").ap(),
    }
    x2T = nc.dram_tensor("x2T", [D, TOK], F32, kind="Internal")
    io["x2T"] = x2T.ap()
    x2bf_in = nc.dram_tensor("x2bf_in", [D, TOK], BF16)
    x2bf_out = nc.dram_tensor("x2bf_out", [2 * D, TOK], BF16)
    o_cc_in = nc.dram_tensor("o_cc_in", [512, T_FULL], BF16)
    o_cc_out = nc.dram_tensor("o_cc_out", [D, T_FULL], BF16)
    io["x2bf_cc_in"] = x2bf_in.ap()
    io["x2bf_cc_out"] = x2bf_out.ap()
    io["o_out"] = o_cc_in.ap()
    io["o_cc_out"] = o_cc_out.ap()

    kb = KB(nc, fused=True)
    p = Prog()
    cst = emit_consts(p, kb)

    io["x1T_in"] = io["x1T"]
    io["x2T_own"] = io["x2T"]
    io["x3T_in"] = io["x3T"]

    p.scope = "B0_l0attn"
    emit_l0_attn(p, kb, cst, io)
    kb.block_boundary(p)
    p.scope = "B1_l0ffn"
    emit_l0_ffn(p, kb, cst, io)
    kb.block_boundary(p)
    p.scope = "B2_cc1"

    def f_cc1(g):
        return g.collective_compute("AllGather", ALU.bypass, replica_groups=PAIRS,
                                    ins=[x2bf_in.ap().opt()],
                                    outs=[x2bf_out.ap().opt()])
    io["ev_x2g"] = p.add("gpsimd", f_cc1, waits=io.get("x2bf_evs", []), inc="poolc")

    p.scope = "B3_l1attn"
    emit_l1_attn(p, kb, cst, io)
    ev_ost = None
    kb.block_boundary(p)
    p.scope = "B4_cc2"

    def f_cc2(g):
        return g.collective_compute("AllGather", ALU.bypass, replica_groups=PAIRS,
                                    ins=[o_cc_in.ap().opt()],
                                    outs=[o_cc_out.ap().opt()])
    io["ev_og"] = p.add("gpsimd", f_cc2, inc="poolc")

    p.scope = "B5_l1out"
    emit_l1_outproj(p, kb, cst, io)
    kb.block_boundary(p)
    p.scope = "B6_l1ffn"
    ev_fin = emit_l1_ffn(p, kb, cst, io)
    p.add("sync", lambda s: None, waits=[ev_fin])
    kb.realize(p)
    return nc


def fused_in_maps(inputs):
    shared = {}
    shared.update(l0attn_host_inputs(inputs))
    shared.update(l0ffn_host_inputs(inputs))
    shared.update(l1ffn_host_inputs(inputs))
    shared["a1_wo"] = np.ascontiguousarray(inputs["L1_attn_w_o"])
    cc = host_constants()
    shared["MEGAMASK"] = cc["MEGAMASK"]
    wslices = l1_head_weight_slices(inputs)
    x = np.asarray(inputs["L0_attn_x"], dtype=np.float32)
    in_maps = []
    for c in range(N_CORES):
        b, hc = c // 2, c % 2
        m = dict(shared)
        wq, wk, wv = wslices[hc]
        m["a1_wq"], m["a1_wk"], m["a1_wv"] = wq, wk, wv
        msel = np.zeros((128, 2), dtype=np.float32)
        msel[:, 0] = 1.0 - hc
        msel[:, 1] = hc
        m["msel"] = msel
        m["xT"] = np.ascontiguousarray(x[b, hc * TOK:(hc + 1) * TOK, :].T)
        in_maps.append(m)
    return in_maps


def kernel_fused(**inputs):
    nc = build_fused_nc()
    in_maps = fused_in_maps(inputs)
    res = run_bass_kernel_spmd(nc, in_maps, core_ids=list(range(N_CORES)))
    out = np.empty((B, T_FULL, D), dtype=np.float32)
    for c in range(N_CORES):
        b, hc = c // 2, c % 2
        out[b, hc * TOK:(hc + 1) * TOK, :] = res.results[c]["out_tok"]
    return out


def kernel_unfused(**inputs):
    x = np.asarray(inputs["L0_attn_x"], dtype=np.float32)
    x1 = run_l0attn(inputs, x)
    x2 = run_l0ffn(inputs, x1)
    o_full = run_l1attn(inputs, x2)
    x3 = run_l1out(inputs, x2, o_full)
    return run_l1ffn(inputs, x3)


def kernel(**inputs):
    return kernel_fused(**inputs)
